# version 17
# baseline (speedup 1.0000x reference)
import numpy as np
import ml_dtypes
from contextlib import ExitStack
import concourse.bass as bass
import concourse.mybir as mybir
from concourse.bass_utils import run_bass_kernel_spmd

F32 = mybir.dt.float32
BF16 = mybir.dt.bfloat16
F16 = mybir.dt.float16
ALU = mybir.AluOpType
ACT = mybir.ActivationFunctionType
AX = mybir.AxisListType.X
MAGIC = 12582912.0
BF = ml_dtypes.bfloat16

T, D = 2048, 1024
NT, NC = 16, 8
NH, HD, NHP = 16, 64, 8
TB, NB = 512, 4
I0, NI0 = 2816, 22
H1, NH1 = 4096, 32
EPS = 1e-5
RG8 = [[0, 1, 2, 3, 4, 5, 6, 7]]


class Ctr:
    def __init__(self, stack, nc, name, step=1):
        self.sem = stack.enter_context(nc.semaphore(name))
        self.n = 0
        self.step = step
        self.maxwait = 0
        self.selfwait = 0
        self.sealed = 0


class MK:
    def __init__(self, tap=None):
        self.tap = tap
        self.q2 = ("sync", "dX")
        self.nc = bass.Bass("TRN2", target_bir_lowering=False)
        self.stack = ExitStack()
        self.q = {e: [] for e in ("sync", "vector", "scalar", "tensor", "gpsimd")}
        st, nc = self.stack, self.nc
        self.c = {k: Ctr(st, nc, k, 16 if (k.startswith("d") or k == "out") else 1)
                  for k in ("v", "s", "t", "g", "dS", "dG", "dV", "dT", "dC",
                            "dP", "dX", "out")}
        self.dram = {}

    def op(self, eng, fn, tick=None, k=1):
        if tick is None:
            self.q[eng].append(fn)
            return None
        ctr = self.c[tick]
        ctr.n += k * ctr.step
        kk = k * ctr.step
        self.q[eng].append(lambda e, f=fn, s=ctr.sem, kk=kk: f(e).then_inc(s, kk))
        return ctr.n

    def wait(self, eng, tick, val):
        if val and val > 0:
            ctr = self.c[tick]
            if ctr.step == 16 and val < ctr.n and val > ctr.sealed:
                raise AssertionError(
                    f"late dma wait: {tick}>={val} registered after {ctr.n} issued "
                    f"(sealed {ctr.sealed}) - wait on the newest mark or "
                    f"reorder so the wait precedes subsequent dmas")
            ctr.maxwait = max(ctr.maxwait, val)
            self.q[eng].append(lambda e, s=ctr.sem, v=val: e.wait_ge(s, v))

    def _dma_gate(self, eng, tick):
        ctr = self.c[tick]
        if ctr.maxwait > ctr.selfwait:
            v = ctr.maxwait
            self.q[eng].append(lambda e, s=ctr.sem, v=v: e.wait_ge(s, v))
            ctr.selfwait = v
        ctr.sealed = ctr.selfwait

    def dma(self, eng, out, in_, tick):
        self._dma_gate(eng, tick)
        return self.op(eng, lambda e, o=out, i=in_: e.dma_start(out=o, in_=i), tick=tick)

    def dmat(self, eng, out, in_, tick):
        self._dma_gate(eng, tick)
        return self.op(eng, lambda e, o=out, i=in_: e.dma_start_transpose(out=o, in_=i), tick=tick)

    def din(self, name, shape, dtype=F32):
        h = self.nc.dram_tensor(name, list(shape), dtype, kind="ExternalInput")
        self.dram[name] = h
        return h.ap()

    def dout(self, name, shape, dtype=F32):
        h = self.nc.dram_tensor(name, list(shape), dtype, kind="ExternalOutput")
        self.dram[name] = h
        return h.ap()

    def dscr(self, name, shape, dtype=F32):
        h = self.nc.dram_tensor(name, list(shape), dtype, kind="Internal")
        self.dram[name] = h
        return h.ap()

    def sb(self, name, shape, dtype, off):
        return self.nc.alloc_sbuf_tensor_at(name, list(shape), dtype, offset=off)

    def _finish(self):
        self.wait("sync", "out", self.c["out"].n)
        for k in ("dS", "dG", "dV", "dT", "dC", "dP", "dX"):
            self.wait("sync", k, self.c[k].n)
        block = self.stack.enter_context(self.nc.Block())
        for name in ("sync", "vector", "scalar", "tensor", "gpsimd"):
            ops = self.q[name]

            def make(ops):
                def _(e):
                    for f in ops:
                        f(e)
                return _
            getattr(block, name)(make(ops))
        self.stack.close()
        return self.nc

    def _tap_sbuf(self, name, ap_list, shape):
        dbg = self.dout("dbg", shape, F32)
        col = 0
        for ap, width in ap_list:
            self.dma("gpsimd", dbg[:, col:col + width], ap, "out")
            col += width
        return self._finish()

    def build(self):
        nc, tap = self.nc, self.tap
        K = 1024

        x_in = self.din("x", [T, D])
        y_out = self.dout("y", [T, D])
        identity = self.din("identity", [128, 128])
        onesf_in = self.din("onesf", [1, 128])
        onesb_in = self.din("onesb", [1, 512], BF16)
        triu_in = self.din("triu", [128, 128], BF16)
        ropeC_in = self.din("ropeC", [T, 512])
        ropeS_in = self.din("ropeS", [T, 512])
        L0_WT = self.din("L0_WT", [D, 3 * D])
        L0_bcol_in = self.din("L0_bcol", [128, 16])
        L0_bvrow_in = self.din("L0_bvrow", [1, D])
        L0_wg_in = self.din("L0_wg", [I0, D])
        L0_wu_in = self.din("L0_wu", [I0, D])
        L0_wd_in = self.din("L0_wd", [D, I0])
        L0_wdT_in = self.din("L0_wdT", [I0, D])
        L0_clipx = self.din("L0_clipx", [1, 1])
        L0_clipd = self.din("L0_clipd", [1, 1])
        L1_WT = {k: self.din(f"L1_W{k}T", [D, D]) for k in "qkvo"}
        L1_brow = {k: self.din(f"L1_b{k}row", [1, D]) for k in "qkvo"}
        L1_w1T_in = self.din("L1_w1T", [D, H1])
        L1_w2T_in = self.din("L1_w2T", [H1, D])
        L1_cw_in = self.din("L1_cw", [128, NH1 * 3])
        L1_cb_in = self.din("L1_cb", [128, NH1])

        qpark = self.dscr("qpark", [D, T], BF16)
        xpark = self.dscr("xpark", [T, D], F32)
        w1q_park = self.dscr("w1q_park", [D, H1], BF16)
        w2q_park = self.dscr("w2q_park", [H1, D], BF16)
        cc_in = self.dscr("cc_in", [1, 1])
        cc_out = self.dscr("cc_out", [1, 1])
        cc_in2 = self.dscr("cc_in2", [1, 1])
        cc_out2 = self.dscr("cc_out2", [1, 1])
        self.ccs = (cc_in, cc_out, cc_in2, cc_out2)

        BASE = 17 * K
        CONST = self.sb("CONST", [128, 128], F32, 205 * K)
        TRIU = self.sb("TRIU", [128, 128], BF16, int(205.5 * K))
        ONESF = self.sb("ONESF", [1, 128], F32, int(205.75 * K))
        ONESB = self.sb("ONESB", [1, 512], BF16, int(206.25 * K))
        BCOL = self.sb("BCOL", [128, 16], F32, int(207.25 * K))
        BQ8 = self.sb("BQ8", [128, 8], F32, int(207.5 * K))
        SM = self.sb("SM", [128, 32], F32, int(207.625 * K))
        ROW = self.sb("ROW", [1, 32], F32, int(207.8125 * K))
        NK = self.sb("NK", [128, 2], F32, 208 * K)
        RC = self.sb("RC", [128, 4], F32, int(208.0625 * K))
        MU = self.sb("MU", [128, 8], F32, int(208.25 * K))
        SC = self.sb("SC", [128, 8], F32, int(208.5 * K))
        BVROW = self.sb("BVROW", [1, D], BF16, 209 * K)
        TMPB = self.sb("TMPB", [128, D], BF16, 211 * K)
        TMP2 = self.sb("TMP2", [128, D], F32, int(219.5 * K))

        X = self.sb("X", [128, NT * D], F32, BASE)
        ROPEC = self.sb("ROPEC", [128, NT * 512], BF16, BASE)
        ROPES = self.sb("ROPES", [128, NT * 512], BF16, BASE + 16 * K)
        BROWS = self.sb("BROWS", [1, 4 * D], BF16, BASE + 32 * K)
        XSTG = self.sb("XSTG", [128, 2 * D], F32, BASE + 40 * K)
        WRES = self.sb("WRES", [128, NC * D], BF16, BASE + 48 * K)
        W2QT = self.sb("W2QT", [128, NH1 * D], BF16, BASE)
        XT = self.sb("XT", [128, NC * T], BF16, 81 * K)
        WS2 = self.sb("WS2", [128, 7 * D], F32, 81 * K)
        WACC = self.sb("WACC", [128, D], F32, 109 * K)
        WQB = self.sb("WQB", [128, D], BF16, 213 * K)
        WQF = self.sb("WQF", [128, D], F32, 215 * K)
        WSM = self.sb("WSM", [128, 4], F32, 219 * K)
        WROW = self.sb("WROW", [1, 8], F32, int(219.125 * K))
        WSC = self.sb("WSC", [128, 4], F32, int(219.25 * K))
        NK8 = self.sb("NK8", [128, NHP], F32, int(219.3125 * K))
        WDSB = self.sb("WDSB", [128, 1024], F32, 213 * K)
        WDSA = self.sb("WDSA", [128, 512], F32, 217 * K)
        MUX = self.sb("MUX", [128, 16], F32, int(219.375 * K))
        YTM = self.sb("YTM", [128, 4 * D], BF16, 81 * K)
        XPOSE = self.sb("XPOSE", [128, D], BF16, 89 * K)
        XPO4 = self.sb("XPO4", [128, 3 * D], BF16, 107 * K)
        YS2 = self.sb("YS2", [128, 2 * D], F32, 81 * K)
        WORES = self.sb("WORES", [128, NC * D], BF16, 91 * K)
        XQ1T = self.sb("XQ1T", [128, NC * T], BF16, 81 * K)
        KF = self.sb("KF", [128, NC * T], BF16, 113 * K)
        H1S = self.sb("H1S", [128, NH1 * 1026], BF16, 113 * K)
        VT = self.sb("VT", [128, NT * NH * 65], BF16, 145 * K)
        QSTG = self.sb("QSTG", [128, T], BF16, 178 * K)
        YF = self.sb("YF", [128, 2 * TB], F32, 182 * K)
        PT = self.sb("PT", [128, 2 * 1024], BF16, 186 * K)
        QB = self.sb("QB", [128, NC * TB], BF16, 190 * K)
        VW = self.sb("VW", [128, NC * D], BF16, 182 * K)
        WTILE = self.sb("WTILE", [128, 2 * NC * 128], BF16, 198 * K)
        HBUF = self.sb("HBUF", [128, NI0 * T], F16, BASE)
        XQT = self.sb("XQT", [128, NC * T], BF16, 105 * K)
        WDQT = self.sb("WDQT", [128, NI0 * D], F16, 139 * K)
        WQT = self.sb("WQT", [128, NC * 128], BF16, 137 * K)
        WSTR = self.sb("WSTR", [128, I0], F32, 183 * K)
        TMPF = self.sb("TMPF", [128, I0], F32, 194 * K)
        ABSA = self.sb("ABSA", [128, D], F32, 178 * K)
        HTOK = self.sb("HTOK", [128, D], F32, 182 * K)
        RTOK = self.sb("RTOK", [128, T], F32, 186 * K)
        W1C = self.sb("W1C", [128, 2 * NC * 128], BF16, 194 * K)
        ROPT = self.sb("ROPT", [128, D], BF16, 202 * K)
        CWCOL = self.sb("CWCOL", [128, NH1 * 3], F32, 198 * K)
        CBCOL = self.sb("CBCOL", [128, NH1], F32, int(198.5 * K))

        self.t = dict(locals())

        psA = self.stack.enter_context(nc.psum_tensor("psA", [128, 1024], F32))
        psB = self.stack.enter_context(nc.psum_tensor("psB", [128, 1024], F32))
        psPA = self.stack.enter_context(nc.psum_tensor("psPA", [128, 512], F32))
        psPB = self.stack.enter_context(nc.psum_tensor("psPB", [128, 512], F32))
        psX = self.stack.enter_context(nc.psum_tensor("psX", [128, 512], F32))
        psY = self.stack.enter_context(nc.psum_tensor("psY", [128, 512], F32))
        self.p = dict(psA=psA, psB=psB, psPA=psPA, psPB=psPB, psX=psX, psY=psY)

        c = self.c

        self.dma("gpsimd", CONST[:], identity, "dG")
        self.dma("gpsimd", TRIU[:], triu_in, "dG")
        self.dma("gpsimd", ONESF[:], onesf_in, "dG")
        self.dma("gpsimd", ONESB[:], onesb_in, "dG")
        self.dma("gpsimd", BCOL[:], L0_bcol_in, "dG")
        self.dma("gpsimd", BVROW[:], L0_bvrow_in, "dG")
        g_consts = c["dG"].n
        x_marks = {}
        for i in range(NT):
            x_marks[i] = self.dma(
                "sync", X[:, i * D:(i + 1) * D], x_in[i * 128:(i + 1) * 128, :], "dS")
            self.wait("sync", "dS", x_marks[i])
        s_x = c["dS"].n
        self.wait("vector", "dG", g_consts)
        v_consts = self.op("vector", lambda e: e.tensor_scalar_mul(
            out=BQ8[:], in0=BCOL[:, 0:8], scalar1=0.125), "v")

        self.wait("tensor", "dG", g_consts)
        self._xt_tile_gate = x_marks
        v_xt = self._transpose_f32_to_bf16(
            src=lambda i, cp: X[:, i * D + cp * 128:i * D + cp * 128 + 128],
            dst=lambda i, cp: XT[:, cp * T + i * 128:cp * T + i * 128 + 128],
            n_i=NT, n_c=NC, psX=psX, psY=psY, CONST=CONST)
        self._xt_tile_gate = None
        if tap == "xt":
            self.wait("gpsimd", "s", v_xt)
            return self._tap_sbuf("xt", [(XT[:, cp * T:(cp + 1) * T], T) for cp in range(NC)],
                                  [128, NC * T])

        self.wait("tensor", "s", v_xt)
        self.wait("scalar", "v", v_consts)
        ps_free = {0: 0, 1: 0}
        wt_free = {0: 0, 1: 0}
        qstg_parked = 0
        for o in range(16):
            buf = o % 2
            self.wait("gpsimd", "t", wt_free[buf])
            for cp in range(NC):
                self.dma("gpsimd", WTILE[:, buf * 1024 + cp * 128:buf * 1024 + cp * 128 + 128],
                         L0_WT[cp * 128:(cp + 1) * 128, o * 128:(o + 1) * 128], "dG")
            gm = c["dG"].n
            self.wait("tensor", "dG", gm)
            for tb4 in range(4):
                pst, pi = (psA, 0) if tb4 % 2 == 0 else (psB, 1)
                self.wait("tensor", "s", ps_free[pi])
                tm = None
                for cp in range(NC):
                    tm = self.op("tensor", lambda e, b=buf, cp=cp, tb4=tb4, pst=pst: e.matmul(
                        pst[:, 0:512], WTILE[:, b * 1024 + cp * 128:b * 1024 + cp * 128 + 128],
                        XT[:, cp * T + tb4 * 512:cp * T + (tb4 + 1) * 512],
                        start=(cp == 0), stop=(cp == NC - 1)),
                        tick=("t" if cp == NC - 1 else None))
                if tb4 == 3:
                    wt_free[buf] = tm
                self.wait("scalar", "t", tm)
                if o < 8:
                    if tb4 == 0 and qstg_parked:
                        self.wait("scalar", "dS", qstg_parked)
                    sm = self.op("scalar", lambda e, o=o, tb4=tb4, pst=pst: e.activation(
                        out=QSTG[:, tb4 * 512:(tb4 + 1) * 512], in_=pst[:, 0:512],
                        func=ACT.Identity, bias=BQ8[:, o:o + 1], scale=0.125), "s")
                else:
                    sm = self.op("scalar", lambda e, o=o, tb4=tb4, pst=pst: e.activation(
                        out=KF[:, (o - 8) * T + tb4 * 512:(o - 8) * T + (tb4 + 1) * 512],
                        in_=pst[:, 0:512], func=ACT.Identity,
                        bias=BCOL[:, o:o + 1], scale=1.0), "s")
                ps_free[pi] = sm
            if o < 8:
                self.wait("sync", "s", sm)
                qstg_parked = self.dma("sync", qpark[o * 128:(o + 1) * 128, :], QSTG[:], "dS")
        s_qk = c["s"].n
        d_qpark = c["dS"].n

        for cp in range(NC):
            self.dma("gpsimd", VW[:, cp * D:(cp + 1) * D],
                     L0_WT[cp * 128:(cp + 1) * 128, 2 * D:3 * D], "dG")
        gm_vw = c["dG"].n
        self.wait("tensor", "dG", gm_vw)
        ps_free_v = {0: 0, 1: 0}
        for tk in range(NT):
            pst, pi = (psA, 0) if tk % 2 == 0 else (psB, 1)
            if ps_free[pi]:
                self.wait("tensor", "s", ps_free[pi])
                ps_free[pi] = 0
            if ps_free_v[pi]:
                self.wait("tensor", "v", ps_free_v[pi])
            for cp in range(NC):
                self.op("tensor", lambda e, cp=cp, tk=tk, pst=pst: e.matmul(
                    pst[:, 0:512], XT[:, cp * T + tk * 128:cp * T + tk * 128 + 128],
                    VW[:, cp * D:cp * D + 512], start=(cp == 0), stop=False), None)
                self.op("tensor", lambda e, cp=cp, tk=tk, pst=pst: e.matmul(
                    pst[:, 512:1024], XT[:, cp * T + tk * 128:cp * T + tk * 128 + 128],
                    VW[:, cp * D + 512:(cp + 1) * D], start=(cp == 0), stop=False), None)
            self.op("tensor", lambda e, pst=pst: e.matmul(
                pst[:, 0:512], ONESB[0:1, 0:128], BVROW[0:1, 0:512],
                start=False, stop=True), None)
            tm = self.op("tensor", lambda e, pst=pst: e.matmul(
                pst[:, 512:1024], ONESB[0:1, 0:128], BVROW[0:1, 512:1024],
                start=False, stop=True), "t")
            self.wait("vector", "t", tm)
            vt_tile = VT[:, tk * NH * 65:(tk + 1) * NH * 65].rearrange("p (h d) -> p h d", h=NH)
            self.op("vector", lambda e, pst=pst, vt=vt_tile: e.tensor_copy(
                out=vt[:, :, 0:64], in_=pst[:, 0:1024].rearrange("p (h d) -> p h d", h=NH)), None)
            ps_free_v[pi] = self.op("vector", lambda e, vt=vt_tile: e.memset(
                vt[:, :, 64:65], 1.0), "v")
        v_vt = c["v"].n
        t_vproj = c["t"].n

        if tap == "qkv":
            self.wait("gpsimd", "s", s_qk)
            self.wait("gpsimd", "v", v_vt)
            aps = [(KF[:, cp * T:(cp + 1) * T], T) for cp in range(NC)]
            aps += [(VT[:, i * NH * 65:(i + 1) * NH * 65], NH * 65) for i in range(NT)]
            return self._tap_sbuf("qkv", aps, [128, NC * T + NT * NH * 65])

        wp_hook, wp_drain = self._wprep_build(xt_free_t=t_vproj)
        self._attention(layer=0, start_after={
            "sync": [("dS", d_qpark), ("t", t_vproj)],
            "tensor": [("v", v_vt), ("s", s_qk)]}, hook=wp_hook)
        wp_drain()
        v_ln1 = c["v"].n

        if tap == "x1":
            self.wait("gpsimd", "v", v_ln1)
            return self._tap_sbuf("x1", [(X[:, i * D:(i + 1) * D], D) for i in range(NT)],
                                  [128, NT * D])

        if tap is not None and tap.startswith("stop_x1"):
            return self._finish()

        self._l0_ffn()
        if tap == "x2":
            self.wait("gpsimd", "dS", self.x2_parked)
            dbg = self.dout("dbg", [T, D], F32)
            for i in range(NT):
                self.dma("gpsimd", dbg[i * 128:(i + 1) * 128, :],
                         xpark[i * 128:(i + 1) * 128, :], "out")
            return self._finish()

        self._l1_attn()
        if tap == "x3":
            self.wait("gpsimd", "dP", self.x3_parked)
            dbg = self.dout("dbg", [T, D], F32)
            for i in range(NT):
                self.dma("gpsimd", dbg[i * 128:(i + 1) * 128, :],
                         xpark[i * 128:(i + 1) * 128, :], "out")
            return self._finish()

        self._l1_ffn()
        return self._finish()

    def _transpose_f32_to_bf16(self, src, dst, n_i, n_c, psX, psY, CONST):
        evacs = []
        gate = getattr(self, "_xt_tile_gate", None)
        for kk in range(n_i * n_c):
            i, cp = divmod(kk, n_c)
            pst = psX if kk % 2 == 0 else psY
            if gate is not None and cp == 0:
                self.wait("tensor", "dS", gate[i])
            if kk >= 2:
                self.wait("tensor", "s", evacs[kk - 2])
            tm = self.op("tensor", lambda e, i=i, cp=cp, pst=pst: e.transpose(
                pst[:, 0:128], src(i, cp), CONST[:]), "t")
            self.wait("scalar", "t", tm)
            evacs.append(self.op("scalar", lambda e, i=i, cp=cp, pst=pst: e.activation(
                out=dst(i, cp), in_=pst[:, 0:128], func=ACT.Identity), "s"))
        return self.c["s"].n

    def vop(self, fn):
        m = self.op("vector", fn, "v")
        self.wait("vector", "v", m)
        return m

    def gop(self, fn):
        m = self.op("gpsimd", fn, "g")
        self.wait("gpsimd", "g", m)
        return m

    def recip(self, dst, src, tmp):
        self.vop(lambda e: e.reciprocal(out=dst, in_=src))
        self.vop(lambda e: e.tensor_mul(out=tmp, in0=dst, in1=src))
        self.vop(lambda e: e.tensor_scalar(out=tmp, in0=tmp, scalar1=-1.0,
                                           scalar2=2.0, op0=ALU.mult, op1=ALU.add))
        return self.vop(lambda e: e.tensor_mul(out=dst, in0=dst, in1=tmp))

    def _layernorm_X(self, xt_of, tiles, SQS, mu_of=None):
        self._ln_done = {}
        for i in tiles:
            MU = mu_of(i) if mu_of is not None else self.t["MU"]
            xt = xt_of(i)
            vm = self.vop(lambda e, xt=xt: e.reduce_sum(
                out=MU[:, 0:1], in_=xt, axis=AX))
            self.wait("scalar", "v", vm)
            sm = self.op("scalar", lambda e, xt=xt: e.activation(
                out=SQS[:], in_=xt, func=ACT.Square, accum_out=MU[:, 1:2]), "s")
            self.wait("vector", "s", sm)
            self.vop(lambda e: e.tensor_scalar_mul(
                out=MU[:, 0:1], in0=MU[:, 0:1], scalar1=1.0 / D))
            self.vop(lambda e: e.tensor_mul(
                out=MU[:, 2:3], in0=MU[:, 0:1], in1=MU[:, 0:1]))
            self.vop(lambda e: e.scalar_tensor_tensor(
                out=MU[:, 1:2], in0=MU[:, 1:2], scalar=1.0 / D, in1=MU[:, 2:3],
                op0=ALU.mult, op1=ALU.subtract))
            vm = self.vop(lambda e: e.tensor_scalar_add(
                out=MU[:, 1:2], in0=MU[:, 1:2], scalar1=EPS))
            self.wait("scalar", "v", vm)
            sm0 = self.op("scalar", lambda e: e.activation(
                out=MU[:, 2:3], in_=MU[:, 1:2], func=ACT.Ln), "s")
            self.wait("scalar", "s", sm0)
            sm = self.op("scalar", lambda e: e.activation(
                out=MU[:, 2:3], in_=MU[:, 2:3], func=ACT.Exp, scale=-0.5), "s")
            self.wait("vector", "s", sm)
            self.vop(lambda e: e.scalar_tensor_tensor(
                out=MU[:, 3:4], in0=MU[:, 0:1], scalar=-1.0, in1=MU[:, 2:3],
                op0=ALU.mult, op1=ALU.mult))
            self._ln_done[i] = self.op("vector", lambda e, xt=xt: e.tensor_scalar(
                out=xt, in0=xt, scalar1=MU[:, 2:3], scalar2=MU[:, 3:4],
                op0=ALU.mult, op1=ALU.add), "v")

    def _attention(self, layer, start_after=None, hook=None):
        t = self.t
        c = self.c
        KF, VT, QB, PT, YF = t["KF"], t["VT"], t["QB"], t["PT"], t["YF"]
        NK8, RC, ROW, CONST, ONESF, TRIU = t["NK8"], t["RC"], t["ROW"], t["CONST"], t["ONESF"], t["TRIU"]
        X, YTM = t["X"], t["YTM"]
        psA, psB, psPA, psPB, psX, psY = (self.p[k] for k in
                                          ("psA", "psB", "psPA", "psPB", "psX", "psY"))
        qpark = self.dram["qpark"].ap()
        causal = (layer == 0)

        exp_done = {0: 0, 1: 0}
        km_done = {0: 0, 1: 0}
        pt_free = {0: 0, 1: 0}
        pv_free = 0
        ps = {"x": 0, "y": 0}
        qb_used = 0

        self._post_ps_free = {0: 0, 1: 0}
        self._post_wo = {}
        self._post_dm = {}
        self._post_park = {0: 0, 1: 0, 2: 0, 3: 0}
        for b in range(NB):
            self.wait("sync", "t", qb_used)
            if start_after and b == 0:
                for tick, val in start_after.get("sync", []):
                    self.wait("sync", tick, val)
            qb_marks = {}
            for cp in range(NC):
                qb_marks[cp] = self.dma(
                    "sync", QB[:, cp * TB:(cp + 1) * TB],
                    qpark[cp * 128:(cp + 1) * 128, b * TB:(b + 1) * TB], "dS")
                self.wait("sync", "dS", qb_marks[cp])
            if start_after and b == 0:
                for tick, val in start_after.get("tensor", []):
                    self.wait("tensor", tick, val)
            if self._post_ps_free[0] or self._post_ps_free[1]:
                self.wait("tensor", "v", max(self._post_ps_free[0], self._post_ps_free[1]))
                self._post_ps_free = {0: 0, 1: 0}

            valid = list(range(4 * (b + 1))) if causal else list(range(NT))
            last_i = len(valid) - 1
            for hp in range(NHP):
                if hook is not None:
                    hook(2 * (b * NHP + hp), ps)
                self.wait("tensor", "dS", qb_marks[hp])
                hA, hB = 2 * hp, 2 * hp + 1

                def chunk_lo(tc):
                    jj = tc - 4 * b
                    return 128 * jj if (causal and jj > 0) else 0

                def emit_sc(idx):
                    tc = valid[idx]
                    buf = idx % 2
                    pst = psA if buf == 0 else psB
                    lo = chunk_lo(tc)
                    self.wait("tensor", "s", exp_done[buf])
                    self.wait("tensor", "v", km_done[buf])
                    self.op("tensor", lambda e, hp=hp, tc=tc, pst=pst, lo=lo: e.matmul(
                        pst[:, lo:512], KF[0:64, hp * T + tc * 128:hp * T + tc * 128 + 128],
                        QB[0:64, hp * TB + lo:hp * TB + 512],
                        start=True, stop=True, tile_position=(0, 0)), None)
                    return self.op("tensor", lambda e, hp=hp, tc=tc, pst=pst, lo=lo: e.matmul(
                        pst[:, 512 + lo:1024], KF[64:128, hp * T + tc * 128:hp * T + tc * 128 + 128],
                        QB[64:128, hp * TB + lo:hp * TB + 512],
                        start=True, stop=True, tile_position=(64, 0)), "t")

                sc_marks = {0: emit_sc(0)}
                nk_ready = 0
                if b == 0 or getattr(self, "fresh_k", False):
                    self.wait("vector", "t", sc_marks[0])
                    km = self.op("vector", lambda e: e.reduce_max(
                        out=RC[:, 2:3], in_=psA[:, 0:1024], axis=AX, negate=True), "v")
                    km_done[0] = km
                    sc_marks[1] = emit_sc(1)
                    self.wait("tensor", "v", km)
                    self.wait("tensor", "v", ps["x"])
                    tmk = self.op("tensor", lambda e: e.transpose(
                        psX[0:1, 0:128], RC[:, 2:3], CONST[:]), "t")
                    self.wait("vector", "t", tmk)
                    vk = self.op("vector", lambda e: e.tensor_reduce(
                        out=ROW[0:1, 0:1], in_=psX[0:1, 0:128], axis=AX, op=ALU.min), "v")
                    if layer == 1:
                        self.wait("vector", "v", vk)
                        vk = self.vop(lambda e: e.tensor_scalar_mul(
                            out=ROW[0:1, 0:1], in0=ROW[0:1, 0:1], scalar1=0.125))
                    ps["x"] = vk
                    self.wait("tensor", "v", vk)
                    self.wait("tensor", "v", ps["y"])
                    tmk2 = self.op("tensor", lambda e: e.matmul(
                        psY[:, 0:1], ONESF[0:1, 0:128], ROW[0:1, 0:1],
                        start=True, stop=True), "t")
                    self.wait("vector", "t", tmk2)
                    vk2 = self.op("vector", lambda e, hp=hp: e.tensor_copy(
                        out=NK8[:, hp:hp + 1], in_=psY[:, 0:1]), "v")
                    ps["y"] = vk2
                    nk_ready = vk2
                else:
                    sc_marks[1] = emit_sc(1)

                for idx, tc in enumerate(valid):
                    buf = idx % 2
                    pst = psA if buf == 0 else psB
                    lo = chunk_lo(tc)
                    self.wait("scalar", "t", sc_marks.pop(idx))
                    if idx == 0 and nk_ready:
                        self.wait("scalar", "v", nk_ready)
                    self.wait("scalar", "t", pt_free[buf])
                    pview_in = pst[:, 0:1024].rearrange("p (g f) -> p g f", g=2)
                    pview_out = PT[:, buf * 1024:(buf + 1) * 1024].rearrange("p (g f) -> p g f", g=2)
                    esc = 0.125 if layer == 1 else 1.0
                    sm_exp = self.op("scalar", lambda e, pi=pview_in, po=pview_out, lo=lo, hp=hp, esc=esc: e.activation(
                        out=po[:, :, lo:512], in_=pi[:, :, lo:512], func=ACT.Exp,
                        bias=NK8[:, hp:hp + 1], scale=esc), "s")
                    exp_done[buf] = sm_exp
                    pv_wait_v = 0
                    if causal and tc >= 4 * b:
                        self.wait("vector", "s", sm_exp)
                        self.op("vector", lambda e, buf=buf, lo=lo: e.tensor_mul(
                            out=PT[:, buf * 1024 + lo:buf * 1024 + lo + 128],
                            in0=PT[:, buf * 1024 + lo:buf * 1024 + lo + 128], in1=TRIU[:]), None)
                        pv_wait_v = self.op("vector", lambda e, buf=buf, lo=lo: e.tensor_mul(
                            out=PT[:, buf * 1024 + 512 + lo:buf * 1024 + 512 + lo + 128],
                            in0=PT[:, buf * 1024 + 512 + lo:buf * 1024 + 512 + lo + 128],
                            in1=TRIU[:]), "v")
                    if idx + 2 <= last_i:
                        sc_marks[idx + 2] = emit_sc(idx + 2)
                    self.wait("tensor", "s", sm_exp)
                    if pv_wait_v:
                        self.wait("tensor", "v", pv_wait_v)
                    if idx == 0:
                        self.wait("tensor", "s", pv_free)
                    self.op("tensor", lambda e, tk=tc, hA=hA, buf=buf, lo=lo, st=(idx == 0), sp=(idx == last_i): e.matmul(
                        psPA[0:65, lo:512],
                        VT[:, tk * NH * 65 + hA * 65:tk * NH * 65 + hA * 65 + 65],
                        PT[:, buf * 1024 + lo:buf * 1024 + 512],
                        start=st, stop=sp), None)
                    pt_free[buf] = self.op("tensor", lambda e, tk=tc, hB=hB, buf=buf, lo=lo, st=(idx == 0), sp=(idx == last_i): e.matmul(
                        psPB[0:65, lo:512],
                        VT[:, tk * NH * 65 + hB * 65:tk * NH * 65 + hB * 65 + 65],
                        PT[:, buf * 1024 + 512 + lo:buf * 1024 + 1024],
                        start=st, stop=sp), "t")
                tm_pv = pt_free[last_i % 2]
                if hook is not None:
                    hook(2 * (b * NHP + hp) + 1, ps)
                self.wait("scalar", "t", tm_pv)
                self.op("scalar", lambda e: e.activation(
                    out=YF[0:65, 0:512], in_=psPA[0:65, 0:512], func=ACT.Identity), None)
                pv_free = self.op("scalar", lambda e: e.activation(
                    out=YF[0:65, 512:1024], in_=psPB[0:65, 0:512], func=ACT.Identity), "s")
                for j in range(4):
                    ti = 4 * b + j
                    self.wait("tensor", "v", ps["x"])
                    self.wait("tensor", "s", pv_free)
                    tmtA = self.op("tensor", lambda e, j=j: e.transpose(
                        psX[:, 0:65], YF[0:65, j * 128:j * 128 + 128],
                        CONST[0:65, 0:65]), "t")
                    self.wait("tensor", "v", ps["y"])
                    tmtB = self.op("tensor", lambda e, j=j: e.transpose(
                        psY[:, 0:65], YF[0:65, 512 + j * 128:512 + j * 128 + 128],
                        CONST[0:65, 0:65]), "t")
                    self.wait("vector", "t", tmtA)
                    self.op("vector", lambda e: e.reciprocal(
                        out=RC[:, 0:1], in_=psX[:, 64:65]), None)
                    self.wait("vector", "t", tmtB)
                    vrc = self.op("vector", lambda e: e.reciprocal(
                        out=RC[:, 1:2], in_=psY[:, 64:65]), "v")
                    self.wait("vector", "v", vrc)
                    for half, h, pst2 in ((0, hA, psX), (1, hB, psY)):
                        if layer == 0:
                            mark = self.op("vector", lambda e, pst2=pst2, half=half, ti=ti, h=h: e.scalar_tensor_tensor(
                                out=X[:, ti * D + h * 64:ti * D + h * 64 + 64],
                                in0=pst2[:, 0:64], scalar=RC[:, half:half + 1],
                                in1=X[:, ti * D + h * 64:ti * D + h * 64 + 64],
                                op0=ALU.mult, op1=ALU.add), "v")
                        else:
                            mark = self.op("vector", lambda e, pst2=pst2, half=half, j=j, h=h: e.tensor_scalar_mul(
                                out=YTM[:, j * D + h * 64:j * D + h * 64 + 64],
                                in0=pst2[:, 0:64], scalar1=RC[:, half:half + 1]), "v")
                        if half == 0:
                            ps["x"] = mark
                        else:
                            ps["y"] = mark
                    if hp == NHP - 1:
                        if layer == 1:
                            self._l1_post_stage1(b, j)
                        else:
                            self._l0_post_tile(b, j)
            qb_used = c["t"].n
            if layer == 1:
                self._l1_block_post(b)
        self.attn_v_done = c["v"].n

    def _scalar_bcast(self, src_row_ap, dst_col_ap, psY, ONESF, width=1):
        self.wait("tensor", "v", self.c["v"].n)
        tm = self.op("tensor", lambda e: e.matmul(
            psY[:, 0:width], ONESF[0:1, 0:128], src_row_ap,
            start=True, stop=True), "t")
        self.wait("vector", "t", tm)
        return self.vop(lambda e: e.tensor_copy(out=dst_col_ap, in_=psY[:, 0:width]))

    def _part_reduce_max(self, col_ap, row_cell, psX, CONST):
        self.wait("tensor", "v", self.c["v"].n)
        tm = self.op("tensor", lambda e: e.transpose(
            psX[0:1, 0:128], col_ap, CONST[:]), "t")
        self.wait("vector", "t", tm)
        return self.vop(lambda e: e.tensor_reduce(
            out=row_cell, in_=psX[0:1, 0:128], axis=AX, op=ALU.max))

    def _wprep_build(self, xt_free_t):
        t, c = self.t, self.c
        WS2, WQB, WQF, WSM, WROW, WSC, CONST, ONESF = (t[k] for k in
            ("WS2", "WQB", "WQF", "WSM", "WROW", "WSC", "CONST", "ONESF"))
        psX, psY = self.p["psX"], self.p["psY"]
        w1q_park = self.dram["w1q_park"].ap()
        w2q_park = self.dram["w2q_park"].ap()
        pieces = [("w1", t["L1_w1T_in"], w1q_park, cp, qq)
                  for cp in range(NC) for qq in range(4)] + \
                 [("w2", t["L1_w2T_in"], w2q_park, ch, 0) for ch in range(NH1)]
        NPC = len(pieces)
        st = {"xs": {}, "first": {"w1": True, "w2": True}, "gated": False,
              "need_wsc": True, "wsc_v": 0}
        self._wprep_parked = 0

        def load(n_):
            which, src, dst, r0, q0 = pieces[n_ % NPC]
            sl = n_ % 7
            eng, tick = ("scalar", "dC") if n_ % 2 == 0 else ("gpsimd", "dG")
            if not st["gated"]:
                self.wait("scalar", "t", xt_free_t)
                self.wait("gpsimd", "t", xt_free_t)
                st["gated"] = True
            if st["xs"].get(sl):
                kind, m = st["xs"][sl]
                self.wait(eng, kind, m)
            self.dma(eng, WS2[:, sl * D:(sl + 1) * D],
                     src[r0 * 128:(r0 + 1) * 128, q0 * D:(q0 + 1) * D], tick)

        def gate_loads(scal=False):
            self.wait("vector", "dC", c["dC"].n)
            self.wait("vector", "dG", c["dG"].n)
            if scal:
                self.wait("scalar", "dC", c["dC"].n)
                self.wait("scalar", "dG", c["dG"].n)

        def comp_a(n_):
            which = pieces[n_][0]
            sl = n_ % 7
            self.vop(lambda e, sl=sl: e.reduce_sum(
                out=WSM[:, 0:1], in_=WS2[:, sl * D:(sl + 1) * D], axis=AX,
                apply_absolute_value=True))
            col = 1 if which == "w1" else 2
            if st["first"][which]:
                st["first"][which] = False
                m = self.vop(lambda e, col=col: e.tensor_copy(
                    out=WSM[:, col:col + 1], in_=WSM[:, 0:1]))
            else:
                m = self.vop(lambda e, col=col: e.tensor_add(
                    out=WSM[:, col:col + 1], in0=WSM[:, col:col + 1],
                    in1=WSM[:, 0:1]))
            st["xs"][sl] = ("v", m)

        def finale(ps):
            for which, col, rc in (("w1", 1, 0), ("w2", 2, 2)):
                self.wait("tensor", "v", c["v"].n)
                tm = self.op("tensor", lambda e, col=col: e.transpose(
                    psX[0:1, 0:128], WSM[:, col:col + 1], CONST[:]), "t")
                self.wait("vector", "t", tm)
                ps["x"] = self.vop(lambda e, rc=rc: e.tensor_reduce(
                    out=WROW[0:1, rc:rc + 1], in_=psX[0:1, 0:128], axis=AX,
                    op=ALU.add))
                self.vop(lambda e, rc=rc: e.tensor_scalar_mul(
                    out=WROW[0:1, rc:rc + 1], in0=WROW[0:1, rc:rc + 1],
                    scalar1=1.0 / (D * H1)))
                self.vop(lambda e, rc=rc: e.tensor_scalar_max(
                    out=WROW[0:1, rc:rc + 1], in0=WROW[0:1, rc:rc + 1],
                    scalar1=1e-5))
                self.recip(WROW[0:1, rc + 1:rc + 2], WROW[0:1, rc:rc + 1],
                           WROW[0:1, 6:7])
            self._scalar_bcast(WROW[0:1, 1:2], WSC[:, 0:1], psY, ONESF, width=1)
            self._scalar_bcast(WROW[0:1, 3:4], WSC[:, 1:2], psY, ONESF, width=1)
            self._scalar_bcast(WROW[0:1, 0:1], WSC[:, 2:3], psY, ONESF, width=1)
            ps["y"] = self._scalar_bcast(WROW[0:1, 2:3], WSC[:, 3:4], psY, ONESF,
                                         width=1)
            st["wsc_v"] = self.vop(lambda e: e.memset(WSM[:, 3:4], MAGIC))

        def comp_b(n_):
            which, src, dst, r0, q0 = pieces[n_ - NPC]
            sl = n_ % 7
            if st["need_wsc"]:
                st["need_wsc"] = False
                self.wait("scalar", "v", st["wsc_v"])
            scol = WSC[:, 0:1] if which == "w1" else WSC[:, 1:2]
            if st.get("wqf_free"):
                self.wait("scalar", "v", st["wqf_free"])
            m = self.op("scalar", lambda e, sl=sl, scol=scol: e.activation(
                out=WQF[:], in_=WS2[:, sl * D:(sl + 1) * D], func=ACT.Identity,
                bias=WSM[:, 3:4], scale=scol), "s")
            st["xs"][sl] = ("s", m)
            self.wait("vector", "s", m)
            self.vop(lambda e: e.tensor_scalar(
                out=WQF[:], in0=WQF[:], scalar1=MAGIC, scalar2=-1.0,
                op0=ALU.subtract, op1=ALU.max))
            if self._wprep_parked:
                self.wait("vector", "dP", self._wprep_parked)
            v3 = self.vop(lambda e: e.tensor_scalar_min(
                out=WQB[:], in0=WQF[:], scalar1=1.0))
            st["wqf_free"] = v3
            self.wait("sync", "v", v3)
            self._wprep_parked = self.dma(
                "sync", dst[r0 * 128:(r0 + 1) * 128, q0 * D:(q0 + 1) * D],
                WQB[:], "dP")

        def hook(i, ps):
            if i == 0:
                pass
            elif i <= 32:
                gate_loads()
                for n_ in range(2 * i - 2, 2 * i):
                    comp_a(n_)
                if i == 32:
                    finale(ps)
            else:
                gate_loads(scal=True)
                for n_ in range(2 * i - 2, 2 * i):
                    comp_b(n_)
            for n_ in range(2 * i, min(2 * i + 2, 2 * NPC)):
                load(n_)

        def drain():
            self.wait("vector", "dC", c["dC"].n)
            self.wait("vector", "dG", c["dG"].n)
            self.wait("scalar", "dC", c["dC"].n)
            self.wait("scalar", "dG", c["dG"].n)
            for n_ in range(2 * NPC - 2, 2 * NPC):
                comp_b(n_)

        return hook, drain

    def _l0_ffn(self):
        t, c = self.t, self.c
        X, XQT, HBUF, WDQT, WSTR, TMPF, WQT = (t[k] for k in
            ("X", "XQT", "HBUF", "WDQT", "WSTR", "TMPF", "WQT"))
        TMPB, TMP2, SM, ROW, SC, MU, CONST, ONESF = (t[k] for k in
            ("TMPB", "TMP2", "SM", "ROW", "SC", "MU", "CONST", "ONESF"))
        psA, psB, psX, psY = (self.p[k] for k in ("psA", "psB", "psX", "psY"))
        L0_wg_in, L0_wu_in, L0_wd_in = t["L0_wg_in"], t["L0_wu_in"], t["L0_wd_in"]
        L0_wdT_in, WDSA, WDSB = t["L0_wdT_in"], t["WDSA"], t["WDSB"]
        psPA, psPB = self.p["psPA"], self.p["psPB"]
        L0_clipx, L0_clipd = t["L0_clipx"], t["L0_clipd"]
        xpark = self.dram["xpark"].ap()
        cc_in, cc_out, cc_in2, cc_out2 = self.ccs
        rg = getattr(self, "rg", RG8)
        SWDB = self.sb("SWDB", [128, D], F32, 105 * 1024)
        self.wait("tensor", "s", c["s"].n)
        self.wait("tensor", "v", c["v"].n)

        vloc = self._part_reduce_max(SM[:, 1:2], ROW[0:1, 0:1], psX, CONST)
        self.wait("sync", "v", vloc)
        dm = self.dma("sync", cc_in, ROW[0:1, 0:1], "dS")
        self.wait("gpsimd", "dS", dm)
        gm = self.op("gpsimd", lambda e: e.collective_compute(
            "AllReduce", ALU.max, replica_groups=getattr(self, "rg1", rg),
            ins=[cc_in], outs=[cc_out]), "g")
        self.wait("sync", "g", gm)
        dm = self.dma("sync", ROW[0:1, 1:2], cc_out, "dS")
        dmc = self.dma("sync", ROW[0:1, 2:3], L0_clipx, "dS")
        self.wait("scalar", "dS", dmc)
        sm = self.op("scalar", lambda e: e.activation(
            out=ROW[0:1, 3:4], in_=ROW[0:1, 2:3], func=ACT.Exp, scale=-1.0), "s")
        self.wait("vector", "s", sm)
        self.wait("vector", "dS", dmc)
        self.vop(lambda e: e.tensor_scalar_add(out=ROW[0:1, 3:4], in0=ROW[0:1, 3:4], scalar1=1.0))
        self.vop(lambda e: e.reciprocal(out=ROW[0:1, 4:5], in_=ROW[0:1, 3:4]))
        self.vop(lambda e: e.tensor_mul(out=ROW[0:1, 5:6], in0=ROW[0:1, 4:5], in1=ROW[0:1, 1:2]))
        self.vop(lambda e: e.tensor_scalar_mul(out=ROW[0:1, 5:6], in0=ROW[0:1, 5:6], scalar1=1.0 / 127.0))
        self.recip(ROW[0:1, 6:7], ROW[0:1, 5:6], ROW[0:1, 30:31])
        self.vop(lambda e: e.tensor_scalar_mul(out=ROW[0:1, 7:8], in0=ROW[0:1, 5:6], scalar1=1.0 / 127.0))
        vbc = self._scalar_bcast(ROW[0:1, 5:8], SC[:, 0:3], psY, ONESF, width=3)

        tq_free = {0: 0, 1: 0}
        tq_t_last = 0
        for i in range(NT):
            if tq_t_last:
                self.wait("vector", "t", tq_t_last)
            self.vop(lambda e, i=i: e.tensor_scalar(
                out=TMPF[:, 0:D], in0=X[:, i * D:(i + 1) * D],
                scalar1=SC[:, 1:2], scalar2=MAGIC, op0=ALU.mult, op1=ALU.add))
            self.vop(lambda e: e.tensor_scalar(
                out=TMPF[:, D:2 * D], in0=TMPF[:, 0:D],
                scalar1=MAGIC, scalar2=-128.0, op0=ALU.subtract, op1=ALU.max))
            vq = self.vop(lambda e: e.tensor_scalar_min(
                out=TMPF[:, 0:D], in0=TMPF[:, D:2 * D], scalar1=127.0))
            self.wait("tensor", "v", vq)
            for cp in range(NC):
                pst = psX if cp % 2 == 0 else psY
                self.wait("tensor", "s", tq_free[cp % 2])
                tm = self.op("tensor", lambda e, cp=cp, pst=pst: e.transpose(
                    pst[:, 0:128], TMPF[:, cp * 128:(cp + 1) * 128], CONST[:]), "t")
                tq_t_last = tm
                self.wait("scalar", "t", tm)
                tq_free[cp % 2] = self.op("scalar", lambda e, i=i, cp=cp, pst=pst: e.activation(
                    out=XQT[:, cp * T + i * 128:cp * T + i * 128 + 128],
                    in_=pst[:, 0:128], func=ACT.Identity), "s")
        v_xq = c["v"].n
        s_xq = c["s"].n
        self.x1_parked = self._l0_park
        self.wait("scalar", "v", v_xq)
        self.wait("vector", "v", v_xq)
        self.wait("scalar", "dS", c["dS"].n)
        self.wait("vector", "dS", c["dS"].n)

        wd_slots = [WDSB[:, 0:512], WDSB[:, 512:1024], WDSA[:, 0:512]]
        wd = {"xs": {0: 0, 1: 0, 2: 0}, "done": 0, "sinv": 0}
        self.wait("sync", "dP", self._wprep_parked)
        self.wait("sync", "v", c["v"].n)

        def wd_piece_a(n_):
            dc, q = divmod(n_, 6)
            width = 512 if q < 5 else 256
            return (L0_wd_in[dc * 128:(dc + 1) * 128,
                             q * 512:q * 512 + width], dc, q, width)

        def wd_load(n_):
            sl = n_ % 3
            if wd["xs"].get(sl):
                self.wait("sync", "v", wd["xs"][sl])
            if n_ < 48:
                srcap, dc, q, width = wd_piece_a(n_)
            else:
                m = n_ - 48
                ch, half = divmod(m, 2)
                srcap = L0_wdT_in[ch * 128:(ch + 1) * 128,
                                  half * 512:(half + 1) * 512]
                width = 512
            self.dma("sync", wd_slots[sl][:, 0:width], srcap, "dS")

        def wd_comp(n_):
            sl = n_ % 3
            if n_ < 48:
                dc, q, width = wd_piece_a(n_)[1:]
                self.vop(lambda e, sl=sl, width=width: e.reduce_max(
                    out=SM[:, 4:5], in_=wd_slots[sl][:, 0:width], axis=AX,
                    apply_absolute_value=True))
                if q == 0:
                    wd["xs"][sl] = self.vop(lambda e, dc=dc: e.tensor_copy(
                        out=SM[:, 8 + dc:9 + dc], in_=SM[:, 4:5]))
                else:
                    wd["xs"][sl] = self.vop(lambda e, dc=dc: e.tensor_max(
                        out=SM[:, 8 + dc:9 + dc], in0=SM[:, 8 + dc:9 + dc],
                        in1=SM[:, 4:5]))
            else:
                m = n_ - 48
                ch, half = divmod(m, 2)
                if wd["sinv"]:
                    self.wait("vector", "v", wd["sinv"])
                    wd["sinv"] = 0
                self.vop(lambda e, sl=sl, half=half: e.tensor_mul(
                    out=wd_slots[sl], in0=wd_slots[sl],
                    in1=TMP2[:, half * 512:(half + 1) * 512]))
                wd["done"] = self.vop(lambda e, ch=ch, half=half, sl=sl: e.tensor_scalar(
                    out=WDQT[:, ch * D + half * 512:ch * D + (half + 1) * 512],
                    in0=wd_slots[sl], scalar1=MAGIC, scalar2=MAGIC,
                    op0=ALU.add, op1=ALU.subtract))
                wd["xs"][sl] = wd["done"]

        def wd_chain():
            for dc in range(NC):
                self.wait("tensor", "v", c["v"].n)
                tm = self.op("tensor", lambda e, dc=dc: e.transpose(
                    psPA[0:1, 0:128], SM[:, 8 + dc:9 + dc], CONST[:]), "t")
                self.wait("vector", "t", tm)
                self.vop(lambda e, dc=dc: e.tensor_copy(
                    out=TMP2[0:1, dc * 128:(dc + 1) * 128], in_=psPA[0:1, 0:128]))
            self.vop(lambda e: e.reciprocal(out=TMP2[0:1, 0:D], in_=TMP2[0:1, 0:D]))
            vr = self.vop(lambda e: e.tensor_scalar_mul(
                out=TMP2[0:1, 0:D], in0=TMP2[0:1, 0:D], scalar1=127.0))
            for half in range(2):
                self.wait("tensor", "v", vr if half == 0 else c["v"].n)
                tm = self.op("tensor", lambda e, half=half: e.matmul(
                    psPB[:, 0:512], ONESF[0:1, 0:128],
                    TMP2[0:1, half * 512:(half + 1) * 512],
                    start=True, stop=True), "t")
                self.wait("vector", "t", tm)
                wd["sinv"] = self.vop(lambda e, half=half: e.tensor_copy(
                    out=TMP2[:, half * 512:(half + 1) * 512], in_=psPB[:, 0:512]))

        def wd_hook(hk):
            if hk == 17:
                wd_chain()
            if 1 <= hk <= 31:
                self.wait("vector", "dS", c["dS"].n)
                for n_ in range(3 * hk - 3, min(3 * hk, 92)):
                    wd_comp(n_)
            for n_ in range(3 * hk, min(3 * hk + 3, 92)):
                wd_load(n_)

        self.wait("tensor", "s", s_xq)
        wq_slots = [WQT, TMPB]
        wq_free = [0, 0]
        wstr_free = [0, 0]
        ps_free = {0: 0, 1: 0}
        habs_started = False
        tp_last = tq_t_last
        sc5_safe = 0
        for ic in range(NI0):
            self.wait("sync", "v", wstr_free[0])
            mg = self.dma("sync", WSTR[:, 0:D], L0_wg_in[ic * 128:(ic + 1) * 128, :], "dS")
            self.wait("sync", "v", wstr_free[1])
            mu_ = self.dma("sync", WSTR[:, D:2 * D], L0_wu_in[ic * 128:(ic + 1) * 128, :], "dS")
            self.wait("vector", "dS", mu_)
            for which, wofs in ((0, 0), (1, D)):
                slot = wq_slots[which]
                self.wait("vector", "v", vbc if ic == 0 and which == 0 else 0)
                self.vop(lambda e, wofs=wofs: e.reduce_max(
                    out=SC[:, 3:4], in_=WSTR[:, wofs:wofs + D], axis=AX,
                    apply_absolute_value=True))
                self.recip(SC[:, 4:5], SC[:, 3:4], SC[:, 7:8])
                self.vop(lambda e: e.tensor_scalar_mul(out=SC[:, 4:5], in0=SC[:, 4:5], scalar1=127.0))
                if which == 0 and sc5_safe:
                    self.wait("vector", "s", sc5_safe)
                dqm = self.vop(lambda e, which=which: e.tensor_mul(
                    out=SC[:, 5 + which:6 + which], in0=SC[:, 3:4], in1=SC[:, 2:3]))
                if tp_last:
                    self.wait("vector", "t", tp_last)
                self.vop(lambda e, wofs=wofs: e.tensor_scalar(
                    out=TMPF[:, 0:D], in0=WSTR[:, wofs:wofs + D],
                    scalar1=SC[:, 4:5], scalar2=MAGIC, op0=ALU.mult, op1=ALU.add))
                wsf = self.vop(lambda e: e.tensor_scalar(
                    out=TMPF[:, D:2 * D], in0=TMPF[:, 0:D],
                    scalar1=MAGIC, scalar2=None, op0=ALU.subtract))
                wstr_free[which] = wsf
                self.wait("tensor", "v", wsf)
                self.wait("vector", "t", wq_free[which])
                ev_marks = {}
                for cp in range(NC):
                    pst = psX if cp % 2 == 0 else psY
                    if cp >= 2:
                        self.wait("tensor", "s", ev_marks[cp - 2])
                    tm = self.op("tensor", lambda e, cp=cp, pst=pst: e.transpose(
                        pst[:, 0:128], TMPF[:, D + cp * 128:D + (cp + 1) * 128], CONST[:]), "t")
                    tp_last = tm
                    self.wait("scalar", "t", tm)
                    wvm = self.op("scalar", lambda e, cp=cp, pst=pst, slot=slot: e.activation(
                        out=slot[:, cp * 128:(cp + 1) * 128], in_=pst[:, 0:128],
                        func=ACT.Identity), "s")
                    ev_marks[cp] = wvm
                self.wait("tensor", "s", wvm)
                for tb4 in range(4):
                    pst, pi = (psA, 0) if tb4 % 2 == 0 else (psB, 1)
                    po = (tb4 // 2) * 512
                    if ps_free[pi]:
                        eng, val = ps_free[pi]
                        self.wait("tensor", eng, val)
                    tmm = None
                    for cp in range(NC):
                        tmm = self.op("tensor", lambda e, cp=cp, tb4=tb4, pst=pst, po=po, slot=slot: e.matmul(
                            pst[:, po:po + 512], slot[:, cp * 128:(cp + 1) * 128],
                            XQT[:, cp * T + tb4 * 512:cp * T + (tb4 + 1) * 512],
                            start=(cp == 0), stop=(cp == NC - 1)),
                            tick=("t" if cp == NC - 1 else None))
                    wq_free[which] = tmm
                    col = ic * T + tb4 * 512
                    if which == 0:
                        self.wait("scalar", "t", tmm)
                        self.wait("scalar", "v", dqm)
                        smk = self.op("scalar", lambda e, col=col, pst=pst, po=po: e.activation(
                            out=HBUF[:, col:col + 512], in_=pst[:, po:po + 512],
                            func=ACT.Silu, scale=SC[:, 5:6]), "s")
                        ps_free[pi] = ("s", smk)
                        sc5_safe = smk
                    else:
                        self.wait("vector", "t", tmm)
                        vmk = self.vop(lambda e, col=col, pst=pst, po=po: e.scalar_tensor_tensor(
                            out=HBUF[:, col:col + 512], in0=pst[:, po:po + 512],
                            scalar=SC[:, 6:7], in1=HBUF[:, col:col + 512],
                            op0=ALU.mult, op1=ALU.mult))
                        ps_free[pi] = ("v", vmk)
                if which == 1:
                    self.vop(lambda e, ic=ic: e.reduce_max(
                        out=SM[:, 2:3], in_=HBUF[:, ic * T:(ic + 1) * T], axis=AX,
                        apply_absolute_value=True))
                    if not habs_started:
                        self.vop(lambda e: e.tensor_copy(out=SM[:, 3:4], in_=SM[:, 2:3]))
                        habs_started = True
                    else:
                        self.vop(lambda e: e.tensor_max(out=SM[:, 3:4], in0=SM[:, 3:4], in1=SM[:, 2:3]))
                wd_hook(2 * ic + which)

        vloc = self._part_reduce_max(SM[:, 3:4], ROW[0:1, 8:9], psX, CONST)
        self.wait("sync", "v", vloc)
        dm = self.dma("sync", cc_in2, ROW[0:1, 8:9], "dS")
        self.wait("gpsimd", "dS", dm)
        gm = self.op("gpsimd", lambda e: e.collective_compute(
            "AllReduce", ALU.max, replica_groups=getattr(self, "rg2", rg),
            ins=[cc_in2], outs=[cc_out2]), "g")
        self.wait("sync", "g", gm)
        dm = self.dma("sync", ROW[0:1, 9:10], cc_out2, "dS")
        dmc = self.dma("sync", ROW[0:1, 10:11], L0_clipd, "dS")
        self.wait("scalar", "dS", dmc)
        sm = self.op("scalar", lambda e: e.activation(
            out=ROW[0:1, 11:12], in_=ROW[0:1, 10:11], func=ACT.Exp, scale=-1.0), "s")
        self.wait("vector", "s", sm)
        self.wait("vector", "dS", dmc)
        self.vop(lambda e: e.tensor_scalar_add(out=ROW[0:1, 11:12], in0=ROW[0:1, 11:12], scalar1=1.0))
        self.vop(lambda e: e.reciprocal(out=ROW[0:1, 12:13], in_=ROW[0:1, 11:12]))
        self.vop(lambda e: e.tensor_mul(out=ROW[0:1, 13:14], in0=ROW[0:1, 12:13], in1=ROW[0:1, 9:10]))
        self.vop(lambda e: e.tensor_scalar_mul(out=ROW[0:1, 13:14], in0=ROW[0:1, 13:14], scalar1=1.0 / 127.0))
        self.recip(ROW[0:1, 14:15], ROW[0:1, 13:14], ROW[0:1, 30:31])
        self.vop(lambda e: e.tensor_scalar_mul(out=ROW[0:1, 15:16], in0=ROW[0:1, 13:14], scalar1=1.0 / 127.0))
        vbc2 = self._scalar_bcast(ROW[0:1, 13:16], SC[:, 3:6], psY, ONESF, width=3)

        vmg = self.vop(lambda e: e.memset(SM[:, 7:8], MAGIC))
        self.wait("scalar", "v", vmg)
        slot_free = {0: 0, 1: 0}
        for n_ in range(2 * NI0):
            ic, half = divmod(n_, 2)
            col = ic * T + half * D
            sl = n_ % 2
            if slot_free[sl]:
                self.wait("scalar", "v", slot_free[sl])
            sm_ = self.op("scalar", lambda e, col=col, sl=sl: e.activation(
                out=TMPF[:, sl * D:(sl + 1) * D], in_=HBUF[:, col:col + D],
                func=ACT.Identity, bias=SM[:, 7:8], scale=SC[:, 4:5]), "s")
            self.wait("vector", "s", sm_)
            self.vop(lambda e, sl=sl: e.tensor_scalar(
                out=TMP2[:], in0=TMPF[:, sl * D:(sl + 1) * D],
                scalar1=MAGIC, scalar2=-128.0, op0=ALU.subtract, op1=ALU.max))
            slot_free[sl] = self.vop(lambda e, col=col: e.tensor_scalar_min(
                out=HBUF[:, col:col + D], in0=TMP2[:], scalar1=127.0))
        v_hq = c["v"].n

        wstr_m = wd["done"]
        self.wait("tensor", "v", wstr_m)
        vv = 0
        for dc in range(NC):
            tm = self.op("tensor", lambda e, dc=dc: e.transpose(
                psX[0:1, 0:128], SM[:, 8 + dc:9 + dc], CONST[:]), "t")
            self.wait("vector", "t", tm)
            vv = self.vop(lambda e, dc=dc: e.tensor_copy(
                out=TMP2[0:1, dc * 128:(dc + 1) * 128], in_=psX[0:1, 0:128]))
            self.wait("tensor", "v", vv)
        for half in range(2):
            tm = self.op("tensor", lambda e, half=half: e.matmul(
                psY[:, 0:512], ONESF[0:1, 0:128], TMP2[0:1, half * 512:(half + 1) * 512],
                start=True, stop=True), "t")
            self.wait("vector", "t", tm)
            vv = self.vop(lambda e, half=half: e.tensor_copy(
                out=SWDB[:, half * 512:(half + 1) * 512], in_=psY[:, 0:512]))
            self.wait("tensor", "v", vv)
        vsw = self.vop(lambda e: e.tensor_scalar_mul(
            out=SWDB[:], in0=SWDB[:], scalar1=SC[:, 5:6]))

        xr_m = 0
        ps_free2 = {0: 0, 1: 0}
        self.x2_tiles = []
        for tk in range(NT):
            pst, pi = (psA, 0) if tk % 2 == 0 else (psB, 1)
            if ps_free2[pi]:
                self.wait("tensor", "v", ps_free2[pi])
            if tk == 0:
                self.wait("tensor", "v", max(v_hq, vsw, wd["done"]))
            tmm = None
            for ic in range(NI0):
                for half in range(2):
                    tmm = self.op("tensor", lambda e, ic=ic, tk=tk, half=half, pst=pst: e.matmul(
                        pst[:, half * 512:(half + 1) * 512],
                        HBUF[:, ic * T + tk * 128:ic * T + tk * 128 + 128],
                        WDQT[:, ic * D + half * 512:ic * D + (half + 1) * 512],
                        start=(ic == 0), stop=(ic == NI0 - 1)),
                        tick=("t" if (ic == NI0 - 1 and half == 1) else None))
            if tk > 0:
                self.wait("vector", "dS", self._x2_park_last)
            if tk == 0:
                self.wait("sync", "v", vsw)
            if xr_m:
                self.wait("sync", "v", xr_m)
            dm = self.dma("sync", TMPF[:, 0:D], xpark[tk * 128:(tk + 1) * 128, :], "dS")
            self.wait("vector", "t", tmm)
            self.vop(lambda e, pst=pst: e.tensor_mul(
                out=TMP2[:], in0=pst[:, 0:D], in1=SWDB[:]))
            self.wait("vector", "dS", dm)
            xr_m = self.vop(lambda e: e.tensor_add(
                out=TMPF[:, D:2 * D], in0=TMP2[:], in1=TMPF[:, 0:D]))
            ps_free2[pi] = xr_m
            self._layernorm_X(lambda i: TMPF[:, D:2 * D], [0], WSTR[:, 0:D])
            vln = c["v"].n
            self.wait("sync", "v", vln)
            self._x2_park_last = self.dma("sync", xpark[tk * 128:(tk + 1) * 128, :], TMPF[:, D:2 * D], "dS")
        self.x2_parked = c["dS"].n

    def _rope(self, view_of, cview, sview, t1, t2, t3, t4, inverse=False):
        e_, o_ = view_of(0), view_of(1)
        self.vop(lambda e: e.tensor_mul(out=t1, in0=o_, in1=sview))
        self.vop(lambda e: e.tensor_mul(out=t2, in0=o_, in1=cview))
        self.vop(lambda e: e.tensor_mul(out=t3, in0=e_, in1=sview))
        self.vop(lambda e: e.tensor_mul(out=t4, in0=e_, in1=cview))
        if not inverse:
            self.vop(lambda e: e.tensor_add(out=o_, in0=t3, in1=t2))
            return self.vop(lambda e: e.tensor_sub(out=e_, in0=t4, in1=t1))
        else:
            self.vop(lambda e: e.tensor_sub(out=e_, in0=t1, in1=t4))
            return self.vop(lambda e: e.scalar_tensor_tensor(
                out=o_, in0=t3, scalar=-1.0, in1=t2, op0=ALU.mult, op1=ALU.subtract))

    def _rope_io(self, in_of, out_of, cview, sview, t1, t2, t3, t4):
        ei, oi = in_of(0), in_of(1)
        eo, oo = out_of(0), out_of(1)
        self.vop(lambda e: e.tensor_mul(out=t1, in0=oi, in1=sview))
        self.vop(lambda e: e.tensor_mul(out=t2, in0=oi, in1=cview))
        self.vop(lambda e: e.tensor_mul(out=t3, in0=ei, in1=sview))
        m4 = self.vop(lambda e: e.tensor_mul(out=t4, in0=ei, in1=cview))
        self.vop(lambda e: e.tensor_add(out=oo, in0=t3, in1=t2))
        done = self.vop(lambda e: e.tensor_sub(out=eo, in0=t4, in1=t1))
        return m4, done

    def _rope_out(self, ybase, cbase, sbase, t1, t2, t3, t4):
        yp = ybase.rearrange("p (x two) -> p x two", two=2)
        av, bv = yp[:, :, 0:1], yp[:, :, 1:2]
        r3 = lambda t: t.rearrange("p (x one) -> p x one", one=1)
        c3, s3 = r3(cbase), r3(sbase)
        self.vop(lambda e: e.tensor_mul(out=r3(t1), in0=bv, in1=s3))
        self.vop(lambda e: e.tensor_mul(out=r3(t2), in0=bv, in1=c3))
        self.vop(lambda e: e.tensor_mul(out=r3(t3), in0=av, in1=s3))
        self.vop(lambda e: e.tensor_mul(out=r3(t4), in0=av, in1=c3))
        yh = ybase.rearrange("p (h d) -> p h d", d=HD)
        th = lambda t: t.rearrange("p (h d) -> p h d", d=32)
        self.vop(lambda e: e.tensor_sub(out=yh[:, :, 0:32], in0=th(t1), in1=th(t4)))
        return self.vop(lambda e: e.scalar_tensor_tensor(
            out=yh[:, :, 32:64], in0=th(t3), scalar=-1.0, in1=th(t2),
            op0=ALU.mult, op1=ALU.subtract))

    def _l1_attn(self):
        t, c = self.t, self.c
        XT, KF, VT, QSTG, XSTG, WRES, WORES, TMPB, ROPT = (t[k] for k in
            ("XT", "KF", "VT", "QSTG", "XSTG", "WRES", "WORES", "TMPB", "ROPT"))
        ROPEC, ROPES, BROWS, CONST, ONESB, TMP2 = (t[k] for k in
            ("ROPEC", "ROPES", "BROWS", "CONST", "ONESB", "TMP2"))
        psA, psB, psX, psY = (self.p[k] for k in ("psA", "psB", "psX", "psY"))
        L1_WT, L1_brow = t["L1_WT"], t["L1_brow"]
        xpark = self.dram["xpark"].ap()
        qpark = self.dram["qpark"].ap()

        self.wait("tensor", "s", c["s"].n)
        self.wait("tensor", "v", c["v"].n)
        self.wait("gpsimd", "dS", self.x2_parked)
        for i in range(NT):
            self.dma("gpsimd", ROPEC[:, i * 512:(i + 1) * 512],
                     t["ropeC_in"][i * 128:(i + 1) * 128, :], "dG")
            self.dma("gpsimd", ROPES[:, i * 512:(i + 1) * 512],
                     t["ropeS_in"][i * 128:(i + 1) * 128, :], "dG")
        for w_i, knm in enumerate("qkvo"):
            self.dma("gpsimd", BROWS[0:1, w_i * D:(w_i + 1) * D], L1_brow[knm], "dG")
        g_tabs = c["dG"].n

        xs_free = {0: 0, 1: 0}
        ev_marks = {}
        for i in range(NT):
            sl = i % 2
            if xs_free[sl]:
                self.wait("sync", "t", xs_free[sl])
            dm = self.dma("sync", XSTG[:, sl * D:(sl + 1) * D],
                          xpark[i * 128:(i + 1) * 128, :], "dS")
            self.wait("tensor", "dS", dm)
            for cp in range(NC):
                kk = i * NC + cp
                pst = psX if kk % 2 == 0 else psY
                if ev_marks.get(kk - 2):
                    self.wait("tensor", "s", ev_marks[kk - 2])
                tm = self.op("tensor", lambda e, sl=sl, cp=cp, pst=pst: e.transpose(
                    pst[:, 0:128], XSTG[:, sl * D + cp * 128:sl * D + cp * 128 + 128],
                    CONST[:]), "t")
                self.wait("scalar", "t", tm)
                ev_marks[kk] = self.op("scalar", lambda e, i=i, cp=cp, pst=pst: e.activation(
                    out=XT[:, cp * T + i * 128:cp * T + i * 128 + 128],
                    in_=pst[:, 0:128], func=ACT.Identity), "s")
            xs_free[sl] = tm
        v_xt = c["s"].n

        self.wait("tensor", "s", v_xt)
        ps_ev = {0: 0, 1: 0}
        self._l1_last_dv = 0
        self._l1_last_park = 0
        for w_i, knm in enumerate("qkv"):
            self.wait("gpsimd", "dG", g_tabs)
            if w_i > 0:
                self.wait("gpsimd", "t", self._wres_used)
            for cp in range(NC):
                self.dma("gpsimd", WRES[:, cp * D:(cp + 1) * D],
                         L1_WT[knm][cp * 128:(cp + 1) * 128, :], "dG")
            gw = c["dG"].n
            self.wait("tensor", "dG", gw)
            for tk in range(NT):
                pst, pi = (psA, 0) if tk % 2 == 0 else (psB, 1)
                if ps_ev[pi]:
                    self.wait("tensor", "v", ps_ev[pi])
                for cp in range(NC):
                    self.op("tensor", lambda e, cp=cp, tk=tk, pst=pst: e.matmul(
                        pst[:, 0:512], XT[:, cp * T + tk * 128:cp * T + tk * 128 + 128],
                        WRES[:, cp * D:cp * D + 512], start=(cp == 0), stop=False), None)
                    self.op("tensor", lambda e, cp=cp, tk=tk, pst=pst: e.matmul(
                        pst[:, 512:1024], XT[:, cp * T + tk * 128:cp * T + tk * 128 + 128],
                        WRES[:, cp * D + 512:(cp + 1) * D], start=(cp == 0), stop=False), None)
                self.op("tensor", lambda e, pst=pst, w_i=w_i: e.matmul(
                    pst[:, 0:512], ONESB[0:1, 0:128], BROWS[0:1, w_i * D:w_i * D + 512],
                    start=False, stop=True), None)
                tm = self.op("tensor", lambda e, pst=pst, w_i=w_i: e.matmul(
                    pst[:, 512:1024], ONESB[0:1, 0:128],
                    BROWS[0:1, w_i * D + 512:(w_i + 1) * D],
                    start=False, stop=True), "t")
                self._wres_used = tm
                self.wait("vector", "t", tm)
                pv_h = pst[:, 0:1024].rearrange("p (h d) -> p h d", h=NH)
                cv = ROPEC[:, tk * 512:(tk + 1) * 512].rearrange("p (h d) -> p h d", d=32)
                sv = ROPES[:, tk * 512:(tk + 1) * 512].rearrange("p (h d) -> p h d", d=32)
                rt = (ROPT[:, 0:512].rearrange("p (h d) -> p h d", d=32),
                      ROPT[:, 512:1024].rearrange("p (h d) -> p h d", d=32),
                      TMPB[:, 0:512].rearrange("p (h d) -> p h d", d=32),
                      TMPB[:, 512:1024].rearrange("p (h d) -> p h d", d=32))
                if knm == "v":
                    vt_tile = VT[:, tk * NH * 65:(tk + 1) * NH * 65].rearrange(
                        "p (h d) -> p h d", h=NH)
                    self.vop(lambda e, vt=vt_tile: e.memset(vt[:, :, 64:65], 1.0))
                    if getattr(self, "old_rope", False):
                        self.vop(lambda e, pst=pst, vt=vt_tile: e.tensor_copy(
                            out=vt[:, :, 0:64], in_=pst[:, 0:1024].rearrange(
                                "p (h d) -> p h d", h=NH)))
                        ps_ev[pi] = self._rope(
                            lambda half, vt=vt_tile: vt[:, :, half * 32:(half + 1) * 32],
                            cv, sv, *rt)
                    else:
                        ps_ev[pi], _ = self._rope_io(
                            lambda half, pv=pv_h: pv[:, :, half * 32:(half + 1) * 32],
                            lambda half, vt=vt_tile: vt[:, :, half * 32:(half + 1) * 32],
                            cv, sv, *rt)
                else:
                    if self._l1_last_dv:
                        self.wait("vector", "dC", self._l1_last_dv)
                        if self.q2[0] != "vector":
                            self.wait("vector", self.q2[1], c[self.q2[1]].n)
                    qv = QSTG[:, 0:D].rearrange("p (h d) -> p h d", h=NH)
                    if getattr(self, "old_rope", False):
                        ev = self.vop(lambda e, pst=pst: e.tensor_copy(
                            out=QSTG[:, 0:D], in_=pst[:, 0:1024]))
                        rm = self._rope(
                            lambda half, qv=qv: qv[:, :, half * 32:(half + 1) * 32],
                            cv, sv, *rt)
                        ps_ev[pi] = ev
                    else:
                        ps_ev[pi], rm = self._rope_io(
                            lambda half, pv=pv_h: pv[:, :, half * 32:(half + 1) * 32],
                            lambda half, qv=qv: qv[:, :, half * 32:(half + 1) * 32],
                            cv, sv, *rt)
                    if knm == "k":
                        self.wait("scalar", "v", rm)
                        kfv = KF.rearrange("p (c t) -> p c t", c=NC)[
                            :, :, tk * 128:(tk + 1) * 128]
                        self.dmat("scalar", kfv, QSTG[:, 0:D], "dC")
                        self._l1_last_dv = c["dC"].n
                    else:
                        self.wait("scalar", "v", rm)
                        if self._l1_last_park:
                            self.wait("scalar", "dS", self._l1_last_park)
                        self.dmat("scalar",
                                  QSTG[:, D:2 * D].rearrange("p (c f) -> p c f", c=NC),
                                  QSTG[:, 0:D], "dC")
                        self._l1_last_dv = c["dC"].n
                        self.wait("sync", "dC", c["dC"].n)
                        for cp in range(NC):
                            self._l1_last_park = self.dma(
                                "sync",
                                qpark[cp * 128:(cp + 1) * 128, tk * 128:(tk + 1) * 128],
                                QSTG[:, D + cp * 128:D + (cp + 1) * 128], "dS")
        d_qpark = c["dS"].n
        t_proj = c["t"].n
        v_proj = c["v"].n

        self.wait("gpsimd", "t", t_proj)
        for cp in range(NC):
            self.dma("gpsimd", WORES[:, cp * D:(cp + 1) * D],
                     L1_WT["o"][cp * 128:(cp + 1) * 128, :], "dG")
        self._g_wo = c["dG"].n

        self._attention(layer=1, start_after={
            "sync": [("dS", d_qpark), ("dC", c["dC"].n), ("t", t_proj)],
            "tensor": [("v", v_proj), ("s", c["s"].n), ("dC", c["dC"].n),
                       ("dS", d_qpark), (self.q2[1], c[self.q2[1]].n)]})
        self.x3_parked = self._x3_last_park

    def _l0_post_tile(self, b, j):
        t, c = self.t, self.c
        X, TMP2, SM, MUX = t["X"], t["TMP2"], t["SM"], t["MUX"]
        ti = 4 * b + j
        xpark = self.dram["xpark"].ap()
        self.wait("vector", "v", c["v"].n)
        self._layernorm_X(lambda i: X[:, ti * D:(ti + 1) * D], [0], TMP2,
                          mu_of=lambda i, j=j: MUX[:, 4 * j:4 * j + 4])
        ln_m = self._ln_done[0]
        self.wait("vector", "v", ln_m)
        self.vop(lambda e, ti=ti: e.reduce_max(
            out=SM[:, 0:1], in_=X[:, ti * D:(ti + 1) * D], axis=AX,
            apply_absolute_value=True))
        if ti == 0:
            self.vop(lambda e: e.tensor_copy(out=SM[:, 1:2], in_=SM[:, 0:1]))
        else:
            self.vop(lambda e: e.tensor_max(
                out=SM[:, 1:2], in0=SM[:, 1:2], in1=SM[:, 0:1]))
        self.wait("sync", "v", ln_m)
        self._l0_park = self.dma(
            "sync", xpark[ti * 128:(ti + 1) * 128, :], X[:, ti * D:(ti + 1) * D],
            "dS")

    def _l1_post_stage1(self, b, j):
        t, c = self.t, self.c
        YTM, XPOSE, XPO4, ROPT, TMPB = (t[k] for k in
            ("YTM", "XPOSE", "XPO4", "ROPT", "TMPB"))
        ROPEC, ROPES = t["ROPEC"], t["ROPES"]
        tk = 4 * b + j
        self.wait("vector", "v", self.c["v"].n)
        rope_done = self._rope_out(
            YTM[:, j * D:(j + 1) * D],
            ROPEC[:, tk * 512:(tk + 1) * 512],
            ROPES[:, tk * 512:(tk + 1) * 512],
            ROPT[:, 0:512], ROPT[:, 512:1024],
            TMPB[:, 0:512], TMPB[:, 512:1024])
        xpo = XPO4[:, j * D:(j + 1) * D] if j < 3 else XPOSE[:, 0:D]
        self.wait("scalar", "v", rope_done)
        if b > 0:
            self.wait("scalar", "t", self._post_wo[j])
        self.dmat("scalar", xpo.rearrange("p (c f) -> p c f", c=NC),
                  YTM[:, j * D:(j + 1) * D], "dC")
        self._post_dm[j] = (c["dC"].n, c[self.q2[1]].n)

    def _l1_block_post(self, b):
        t, c = self.t, self.c
        XPOSE, XPO4, WORES, XSTG, YS2, TMP2 = (t[k] for k in
            ("XPOSE", "XPO4", "WORES", "XSTG", "YS2", "TMP2"))
        ONESB, BROWS = t["ONESB"], t["BROWS"]
        psA, psB = self.p["psA"], self.p["psB"]
        xpark = self.dram["xpark"].ap()
        bufs = [XSTG[:, 0:D], XSTG[:, D:2 * D], YS2[:, 0:D], YS2[:, D:2 * D]]

        if b == 0:
            self.wait("tensor", "dG", self._g_wo)
        dmc, dms = self._post_dm[3]
        self.wait("tensor", "dC", dmc)
        self.wait("tensor", self.q2[1], dms)
        self.wait("sync", "dC", dmc)
        self.wait("sync", self.q2[1], dms)
        if self._post_park[3]:
            self.wait("sync", "dP", c["dP"].n)
        rld = []
        for j in range(4):
            tk = 4 * b + j
            rld.append(self.dma("sync", bufs[j],
                                xpark[tk * 128:(tk + 1) * 128, :], "dS"))
        for j in range(4):
            xpo = XPO4[:, j * D:(j + 1) * D] if j < 3 else XPOSE[:, 0:D]
            pst, pi = (psA, 0) if j % 2 == 0 else (psB, 1)
            if self._post_ps_free[pi]:
                self.wait("tensor", "v", self._post_ps_free[pi])
            for cp in range(NC):
                self.op("tensor", lambda e, cp=cp, pst=pst, xpo=xpo: e.matmul(
                    pst[:, 0:512], xpo[:, cp * 128:(cp + 1) * 128],
                    WORES[:, cp * D:cp * D + 512], start=(cp == 0), stop=False), None)
                self.op("tensor", lambda e, cp=cp, pst=pst, xpo=xpo: e.matmul(
                    pst[:, 512:1024], xpo[:, cp * 128:(cp + 1) * 128],
                    WORES[:, cp * D + 512:(cp + 1) * D], start=(cp == 0), stop=False), None)
            self.op("tensor", lambda e, pst=pst: e.matmul(
                pst[:, 0:512], ONESB[0:1, 0:128], BROWS[0:1, 3 * D:3 * D + 512],
                start=False, stop=True), None)
            tm = self.op("tensor", lambda e, pst=pst: e.matmul(
                pst[:, 512:1024], ONESB[0:1, 0:128], BROWS[0:1, 3 * D + 512:4 * D],
                start=False, stop=True), "t")
            self._post_wo[j] = tm
            self.wait("vector", "t", tm)
            if j == 0:
                self.wait("vector", "dS", rld[3])
            self._post_ps_free[pi] = self.vop(lambda e, pst=pst, j=j: e.tensor_add(
                out=bufs[j], in0=pst[:, 0:D], in1=bufs[j]))
        MUX = t["MUX"]
        self._layernorm_X(lambda j: bufs[j], range(4), TMP2,
                          mu_of=lambda j: MUX[:, 4 * j:4 * j + 4])
        for j in range(4):
            tk = 4 * b + j
            self.wait("sync", "v", self._ln_done[j])
            self._post_park[j] = self.dma(
                "sync", xpark[tk * 128:(tk + 1) * 128, :], bufs[j], "dP")
        self._x3_last_park = self._post_park[3]
        self.wait("vector", "dC", self.c["dC"].n)
        if self.q2[0] != "vector":
            self.wait("vector", self.q2[1], self.c[self.q2[1]].n)
        self.wait("vector", "dP", self._post_park[3])

    def _l1_ffn(self):
        t, c = self.t, self.c
        W2QT, XQ1T, H1S, ABSA, HTOK, RTOK, W1C = (t[k] for k in
            ("W2QT", "XQ1T", "H1S", "ABSA", "HTOK", "RTOK", "W1C"))
        CWCOL, CBCOL, TMP2, TMPB, SM, ROW, SC, CONST, ONESF, MU = (t[k] for k in
            ("CWCOL", "CBCOL", "TMP2", "TMPB", "SM", "ROW", "SC", "CONST", "ONESF", "MU"))
        psA, psB, psX, psY = (self.p[k] for k in ("psA", "psB", "psX", "psY"))
        w1T_in, w2T_in = t["L1_w1T_in"], t["L1_w2T_in"]
        xpark = self.dram["xpark"].ap()
        w1q_park = self.dram["w1q_park"].ap()
        w2q_park = self.dram["w2q_park"].ap()
        y_out = t["y_out"]
        XS2 = self.sb("XS2", [128, 2 * D], F32, 186 * 1024)

        self.wait("tensor", "s", c["s"].n)
        self.wait("tensor", "v", c["v"].n)
        self.wait("sync", "v", c["v"].n)
        self.wait("sync", "t", c["t"].n)
        self.wait("sync", "s", c["s"].n)
        dcw = self.dma("sync", CWCOL[:], t["L1_cw_in"], "dS")
        dcb = self.dma("sync", CBCOL[:], t["L1_cb_in"], "dS")
        WSC = t["WSC"]
        self.wait("sync", "dP", c["dP"].n)

        self.wait("gpsimd", "v", c["v"].n)
        self.wait("gpsimd", "s", c["s"].n)
        self.wait("gpsimd", "t", c["t"].n)
        self.wait("gpsimd", "dS", c["dS"].n)
        self.wait("gpsimd", "dP", c["dP"].n)
        for ch in range(NH1):
            self.dma("gpsimd", W2QT[:, ch * D:(ch + 1) * D],
                     w2q_park[ch * 128:(ch + 1) * 128, :], "dG")
        g_w2 = c["dG"].n

        self.wait("sync", "v", c["v"].n)
        tq_free = {0: 0, 1: 0}
        tq_t_last = 0
        xs_rd = {0: 0, 1: 0}
        for i in range(NT):
            sl = i % 2
            if xs_rd[sl]:
                self.wait("sync", "v", xs_rd[sl])
            dm = self.dma("sync", XS2[:, sl * D:(sl + 1) * D],
                          xpark[i * 128:(i + 1) * 128, :], "dS")
            self.wait("vector", "dS", dm)
            self.vop(lambda e, sl=sl: e.reduce_max(
                out=SM[:, 4:5], in_=XS2[:, sl * D:(sl + 1) * D], axis=AX,
                apply_absolute_value=True))
            self.vop(lambda e: e.tensor_scalar_max(
                out=SM[:, 4:5], in0=SM[:, 4:5], scalar1=1e-5))
            self.recip(SC[:, 2:3], SM[:, 4:5], SC[:, 7:8])
            self.vop(lambda e: e.tensor_scalar_mul(
                out=SC[:, 2:3], in0=SC[:, 2:3], scalar1=127.0))
            self.vop(lambda e, i=i: e.tensor_scalar_mul(
                out=SM[:, 16 + i:17 + i], in0=SM[:, 4:5], scalar1=1.0 / 127.0))
            self.vop(lambda e, i=i: e.tensor_mul(
                out=SM[:, 16 + i:17 + i], in0=SM[:, 16 + i:17 + i], in1=WSC[:, 2:3]))
            if tq_t_last:
                self.wait("vector", "t", tq_t_last)
            self.vop(lambda e, sl=sl: e.tensor_scalar(
                out=TMP2[:], in0=XS2[:, sl * D:(sl + 1) * D],
                scalar1=SC[:, 2:3], scalar2=MAGIC, op0=ALU.mult, op1=ALU.add))
            vq = self.vop(lambda e: e.tensor_scalar(
                out=TMP2[:], in0=TMP2[:], scalar1=MAGIC, scalar2=None,
                op0=ALU.subtract))
            xs_rd[sl] = vq
            self.wait("tensor", "v", vq)
            for cp in range(NC):
                pst = psX if cp % 2 == 0 else psY
                self.wait("tensor", "s", tq_free[cp % 2])
                tm = self.op("tensor", lambda e, cp=cp, pst=pst: e.transpose(
                    pst[:, 0:128], TMP2[:, cp * 128:(cp + 1) * 128], CONST[:]), "t")
                tq_t_last = tm
                self.wait("scalar", "t", tm)
                tq_free[cp % 2] = self.op("scalar", lambda e, i=i, cp=cp, pst=pst: e.activation(
                    out=XQ1T[:, cp * T + i * 128:cp * T + i * 128 + 128],
                    in_=pst[:, 0:128], func=ACT.Identity), "s")
        rt_marks = 0
        self.wait("tensor", "s", c["s"].n)
        for i in range(NT):
            self.wait("tensor", "v", c["v"].n)
            tm = self.op("tensor", lambda e, i=i: e.transpose(
                psX[0:1, 0:128], SM[:, 16 + i:17 + i], CONST[:]), "t")
            self.wait("vector", "t", tm)
            vv = self.vop(lambda e: e.tensor_copy(out=TMP2[0:1, 0:128], in_=psX[0:1, 0:128]))
            self.wait("tensor", "v", vv)
            tm2 = self.op("tensor", lambda e: e.matmul(
                psY[:, 0:128], ONESF[0:1, 0:128], TMP2[0:1, 0:128],
                start=True, stop=True), "t")
            self.wait("vector", "t", tm2)
            rt_marks = self.vop(lambda e, i=i: e.tensor_copy(
                out=RTOK[:, i * 128:(i + 1) * 128], in_=psY[:, 0:128]))

        self.wait("vector", "dS", c["dS"].n)
        self.wait("scalar", "dS", c["dS"].n)

        self.y_stored = 0
        w1c_free = {0: 0, 1: 0}
        ps_done = 0
        for tau in range(2):
            if tau == 1:
                self.wait("vector", "out", self._y_park_last)
                self.wait("vector", "dS", c["dS"].n)
            habs_first = True
            for j in range(NH1):
                sl = j % 2
                if w1c_free[sl]:
                    self.wait("sync", "t", w1c_free[sl])
                for cp in range(NC):
                    self.dma("sync", W1C[:, sl * D + cp * 128:sl * D + (cp + 1) * 128],
                             w1q_park[cp * 128:(cp + 1) * 128, j * 128:(j + 1) * 128], "dS")
                dmw = c["dS"].n
                self.wait("tensor", "dS", dmw)
                if j == 0:
                    self.wait("tensor", "v", c["v"].n)
                    self.wait("tensor", "s", c["s"].n)
                pst, pi = (psA, 0) if j % 2 == 0 else (psB, 1)
                if ps_done:
                    self.wait("tensor", "v", ps_done)
                tmm = None
                for cp in range(NC):
                    for half in range(2):
                        tmm = self.op("tensor", lambda e, cp=cp, half=half, tau=tau, sl=sl, pst=pst: e.matmul(
                            pst[:, half * 512:(half + 1) * 512],
                            W1C[:, sl * D + cp * 128:sl * D + (cp + 1) * 128],
                            XQ1T[:, cp * T + tau * 1024 + half * 512:cp * T + tau * 1024 + (half + 1) * 512],
                            start=(cp == 0), stop=(cp == NC - 1)),
                            tick=("t" if (cp == NC - 1 and half == 1) else None))
                halo_tok = tau * 1024 - 1 if tau == 1 else 1024
                tmh = None
                for cp in range(NC):
                    tmh = self.op("tensor", lambda e, cp=cp, sl=sl, ht=halo_tok: e.matmul(
                        psX[:, 0:1],
                        W1C[:, sl * D + cp * 128:sl * D + (cp + 1) * 128],
                        XQ1T[:, cp * T + ht:cp * T + ht + 1],
                        start=(cp == 0), stop=(cp == NC - 1)),
                        tick=("t" if cp == NC - 1 else None))
                w1c_free[sl] = tmh
                self.wait("vector", "t", tmh)
                hrow = j * 1026
                self.vop(lambda e, hrow=hrow, tau=tau, pst=pst: e.tensor_mul(
                    out=H1S[:, hrow + 1:hrow + 1025], in0=pst[:, 0:1024],
                    in1=RTOK[:, tau * 1024:(tau + 1) * 1024]))
                hcol = hrow + (0 if tau == 1 else 1025)
                self.vop(lambda e, hcol=hcol, ht=halo_tok: e.tensor_mul(
                    out=H1S[:, hcol:hcol + 1], in0=psX[:, 0:1],
                    in1=RTOK[:, ht:ht + 1]))
                zcol = hrow + (1025 if tau == 1 else 0)
                ps_done = self.vop(lambda e, zcol=zcol: e.memset(
                    H1S[:, zcol:zcol + 1], 0.0))
                self.vop(lambda e, hrow=hrow, j=j: e.tensor_scalar_mul(
                    out=TMPB[:], in0=H1S[:, hrow:hrow + 1024],
                    scalar1=CWCOL[:, j * 3:j * 3 + 1]))
                self.vop(lambda e, hrow=hrow, j=j: e.scalar_tensor_tensor(
                    out=TMPB[:], in0=H1S[:, hrow + 1:hrow + 1025],
                    scalar=CWCOL[:, j * 3 + 1:j * 3 + 2], in1=TMPB[:],
                    op0=ALU.mult, op1=ALU.add))
                vcv = self.vop(lambda e, hrow=hrow, j=j: e.scalar_tensor_tensor(
                    out=TMPB[:], in0=H1S[:, hrow + 2:hrow + 1026],
                    scalar=CWCOL[:, j * 3 + 2:j * 3 + 3], in1=TMPB[:],
                    op0=ALU.mult, op1=ALU.add))
                self.wait("scalar", "v", vcv)
                smk = self.op("scalar", lambda e, hrow=hrow, j=j: e.activation(
                    out=H1S[:, hrow + 1:hrow + 1025], in_=TMPB[:], func=ACT.Silu,
                    bias=CBCOL[:, j:j + 1], scale=1.0), "s")
                self.wait("vector", "s", smk)
                if habs_first:
                    self.vop(lambda e, hrow=hrow: e.tensor_copy(
                        out=ABSA[:], in_=H1S[:, hrow + 1:hrow + 1025]))
                    self.vop(lambda e, hrow=hrow: e.tensor_copy(
                        out=HTOK[:], in_=H1S[:, hrow + 1:hrow + 1025]))
                    habs_first = False
                else:
                    self.vop(lambda e, hrow=hrow: e.tensor_max(
                        out=ABSA[:], in0=ABSA[:], in1=H1S[:, hrow + 1:hrow + 1025]))
                    self.vop(lambda e, hrow=hrow: e.tensor_tensor(
                        out=HTOK[:], in0=HTOK[:], in1=H1S[:, hrow + 1:hrow + 1025],
                        op=ALU.min))
            self.vop(lambda e: e.tensor_scalar_mul(out=HTOK[:], in0=HTOK[:], scalar1=-1.0))
            self.vop(lambda e: e.tensor_max(out=ABSA[:], in0=ABSA[:], in1=HTOK[:]))
            for cc8 in range(NC):
                self.wait("tensor", "v", c["v"].n)
                tm = self.op("tensor", lambda e, cc8=cc8: e.transpose(
                    psX[:, 0:128], ABSA[:, cc8 * 128:(cc8 + 1) * 128], CONST[:]), "t")
                self.wait("vector", "t", tm)
                self.vop(lambda e: e.reduce_max(
                    out=SM[:, 5:6], in_=psX[:, 0:128], axis=AX))
                self.vop(lambda e: e.tensor_scalar_max(
                    out=SM[:, 5:6], in0=SM[:, 5:6], scalar1=1e-5))
                self.recip(SC[:, 3:4], SM[:, 5:6], SC[:, 7:8])
                self.vop(lambda e: e.tensor_scalar_mul(
                    out=SC[:, 3:4], in0=SC[:, 3:4], scalar1=127.0))
                self.vop(lambda e, cc8=cc8: e.tensor_scalar_mul(
                    out=SM[:, 24 + cc8:25 + cc8], in0=SM[:, 5:6], scalar1=1.0 / 127.0))
                self.vop(lambda e, cc8=cc8: e.tensor_mul(
                    out=SM[:, 24 + cc8:25 + cc8], in0=SM[:, 24 + cc8:25 + cc8],
                    in1=WSC[:, 3:4]))
                vv = self.vop(lambda e: e.tensor_copy(out=SM[:, 6:7], in_=SC[:, 3:4]))
                self.wait("tensor", "v", vv)
                tm2 = self.op("tensor", lambda e: e.transpose(
                    psX[0:1, 0:128], SM[:, 6:7], CONST[:]), "t")
                self.wait("vector", "t", tm2)
                vv = self.vop(lambda e: e.tensor_copy(out=TMP2[0:1, 0:128], in_=psX[0:1, 0:128]))
                self.wait("tensor", "v", vv)
                tm3 = self.op("tensor", lambda e: e.matmul(
                    psY[:, 0:128], ONESF[0:1, 0:128], TMP2[0:1, 0:128],
                    start=True, stop=True), "t")
                self.wait("vector", "t", tm3)
                self.vop(lambda e, cc8=cc8: e.tensor_copy(
                    out=HTOK[:, cc8 * 128:(cc8 + 1) * 128], in_=psY[:, 0:128]))
            hq_marks = {}
            for j in range(NH1):
                hrow = j * 1026
                self.vop(lambda e, hrow=hrow: e.tensor_mul(
                    out=TMP2[:], in0=H1S[:, hrow + 1:hrow + 1025], in1=HTOK[:]))
                hq_marks[j] = self.vop(lambda e, hrow=hrow: e.tensor_scalar(
                    out=H1S[:, hrow + 1:hrow + 1025], in0=TMP2[:],
                    scalar1=MAGIC, scalar2=MAGIC, op0=ALU.add, op1=ALU.subtract))
            v_hq = c["v"].n
            if tau == 0:
                self.wait("tensor", "dG", g_w2)
            xr_m = 0
            for lc in range(8):
                tk = tau * 8 + lc
                pst, pi = (psA, 0) if lc % 2 == 0 else (psB, 1)
                if ps_done:
                    self.wait("tensor", "v", ps_done)
                    ps_done = 0
                if getattr(self, "_w2_ps_free", {}).get(pi):
                    self.wait("tensor", "v", self._w2_ps_free[pi])
                tmm = None
                for j in range(NH1):
                    if lc == 0:
                        self.wait("tensor", "v", hq_marks[j])
                    for half in range(2):
                        tmm = self.op("tensor", lambda e, j=j, lc=lc, half=half, pst=pst: e.matmul(
                            pst[:, half * 512:(half + 1) * 512],
                            H1S[:, j * 1026 + 1 + lc * 128:j * 1026 + 1 + (lc + 1) * 128],
                            W2QT[:, j * D + half * 512:j * D + (half + 1) * 512],
                            start=(j == 0), stop=(j == NH1 - 1)),
                            tick=("t" if (j == NH1 - 1 and half == 1) else None))
                if xr_m:
                    self.wait("vector", "out", self._y_park_last)
                if xr_m:
                    self.wait("sync", "v", xr_m)
                if lc == 0:
                    self.wait("sync", "v", c["v"].n)
                dm = self.dma("sync", ABSA[:], xpark[tk * 128:(tk + 1) * 128, :], "dS")
                self.wait("vector", "t", tmm)
                self.wait("vector", "dS", dm)
                if not hasattr(self, "_w2_ps_free"):
                    self._w2_ps_free = {0: 0, 1: 0}
                xr_m = self.vop(lambda e, pst=pst, cc8=lc: e.scalar_tensor_tensor(
                    out=HTOK[:], in0=pst[:, 0:D],
                    scalar=SM[:, 24 + cc8:25 + cc8], in1=ABSA[:],
                    op0=ALU.mult, op1=ALU.add))
                self._w2_ps_free[pi] = xr_m
                self._layernorm_X(lambda i: HTOK[:], [0], TMP2)
                vln = c["v"].n
                self.wait("sync", "v", vln)
                self._y_park_last = self.dma(
                    "sync", y_out[tk * 128:(tk + 1) * 128, :], HTOK[:], "out")



def _host_consts():
    cst = {}
    cst["identity"] = np.eye(128, dtype=np.float32)
    cst["onesf"] = np.ones((1, 128), dtype=np.float32)
    cst["onesb"] = np.ones((1, 512), dtype=BF)
    cst["triu"] = np.triu(np.ones((128, 128))).astype(BF)
    pos = np.arange(T, dtype=np.float32)[:, None]
    dim_pair = np.arange(0, HD, 2, dtype=np.float32)[None, :]
    theta = np.float32(10000.0) ** (-dim_pair / np.float32(HD))
    freqs = (pos * theta).astype(np.float32)
    rope = np.concatenate([np.cos(freqs), np.sin(freqs)], axis=-1).astype(np.float32)
    cst["ropeC"] = np.ascontiguousarray(np.tile(rope[:, 0::2], (1, NH))).astype(np.float32)
    cst["ropeS"] = np.ascontiguousarray(np.tile(rope[:, 1::2], (1, NH))).astype(np.float32)
    return cst


def _prep_weights(inp):
    w = {}
    w["L0_WT"] = np.ascontiguousarray(np.asarray(inp["L0_attn_W"], np.float32).T)
    b = np.asarray(inp["L0_attn_b"], np.float32)
    w["L0_bcol"] = np.ascontiguousarray(b[:2048].reshape(16, 128).T)
    w["L0_bvrow"] = np.ascontiguousarray(b[2048:].reshape(1, D))
    w["L0_wg"] = np.asarray(inp["L0_ffn_w_gate"], np.float32)
    w["L0_wu"] = np.asarray(inp["L0_ffn_w_up"], np.float32)
    w["L0_wd"] = np.asarray(inp["L0_ffn_w_down"], np.float32)
    w["L0_wdT"] = np.ascontiguousarray(w["L0_wd"].T)
    w["L0_clipx"] = np.asarray(inp["L0_ffn_clip_x"], np.float32).reshape(1, 1)
    w["L0_clipd"] = np.asarray(inp["L0_ffn_clip_down"], np.float32).reshape(1, 1)
    pi = np.concatenate([np.arange(0, HD, 2), np.arange(1, HD, 2)])
    rowperm = np.concatenate([h * HD + pi for h in range(NH)])
    for nm, wn, bn in (("q", "L1_attn_Wq", "L1_attn_bq"), ("k", "L1_attn_Wk", "L1_attn_bk"),
                       ("v", "L1_attn_Wv", "L1_attn_bv")):
        W = np.asarray(inp[wn], np.float32)[rowperm, :]
        w[f"L1_W{nm}T"] = np.ascontiguousarray(W.T)
        w[f"L1_b{nm}row"] = np.asarray(inp[bn], np.float32)[rowperm].reshape(1, D)
    Wo = np.asarray(inp["L1_attn_Wo"], np.float32)
    w["L1_WoT"] = np.ascontiguousarray(Wo.T)
    w["L1_borow"] = np.asarray(inp["L1_attn_bo"], np.float32).reshape(1, D)
    w["L1_w1T"] = np.ascontiguousarray(np.asarray(inp["L1_ffn_w1"], np.float32).T)
    w["L1_w2T"] = np.ascontiguousarray(np.asarray(inp["L1_ffn_w2"], np.float32).T)
    cw = np.asarray(inp["L1_ffn_conv_w"], np.float32).reshape(H1, 3)
    w["L1_cw"] = np.ascontiguousarray(cw.reshape(NH1, 128, 3).transpose(1, 0, 2).reshape(128, NH1 * 3))
    w["L1_cb"] = np.ascontiguousarray(np.asarray(inp["L1_ffn_conv_b"], np.float32).reshape(NH1, 128).T)
    return w


def _build_in_maps(inputs):
    x = np.asarray(inputs["L0_attn_x"], np.float32)
    shared = {**_host_consts(), **_prep_weights(inputs)}
    return [dict(shared, x=np.ascontiguousarray(x[core])) for core in range(8)]


def kernel(**inputs):
    in_maps = _build_in_maps(inputs)
    nc = MK(tap=None).build()
    res = run_bass_kernel_spmd(nc, in_maps, core_ids=list(range(8)))
    return np.stack([np.asarray(r["y"], dtype=np.float32) for r in res.results], axis=0)


def run_debug(tap, inputs):
    in_maps = _build_in_maps(inputs)
    nc = MK(tap=tap).build()
    res = run_bass_kernel_spmd(nc, in_maps, core_ids=list(range(8)))
    return res.results


# version 18
# speedup vs baseline: 1.1561x; 1.1561x over previous
import numpy as np
import ml_dtypes
from contextlib import ExitStack
import concourse.bass as bass
import concourse.mybir as mybir
from concourse.bass_utils import run_bass_kernel_spmd

F32 = mybir.dt.float32
BF16 = mybir.dt.bfloat16
F16 = mybir.dt.float16
ALU = mybir.AluOpType
ACT = mybir.ActivationFunctionType
AX = mybir.AxisListType.X
MAGIC = 12582912.0
BF = ml_dtypes.bfloat16

T, D = 2048, 1024
NT, NC = 16, 8
NH, HD, NHP = 16, 64, 8
TB, NB = 512, 4
I0, NI0 = 2816, 22
H1, NH1 = 4096, 32
EPS = 1e-5
RG8 = [[0, 1, 2, 3, 4, 5, 6, 7]]


class Ctr:
    def __init__(self, stack, nc, name, step=1):
        self.sem = stack.enter_context(nc.semaphore(name))
        self.n = 0
        self.step = step
        self.maxwait = 0
        self.selfwait = 0
        self.sealed = 0


class MK:
    def __init__(self, tap=None):
        self.tap = tap
        self.q2 = ("sync", "dX")
        self.nc = bass.Bass("TRN2", target_bir_lowering=False)
        self.stack = ExitStack()
        self.q = {e: [] for e in ("sync", "vector", "scalar", "tensor", "gpsimd")}
        st, nc = self.stack, self.nc
        self.c = {k: Ctr(st, nc, k, 16 if (k.startswith("d") or k == "out") else 1)
                  for k in ("v", "s", "t", "g", "dS", "dG", "dV", "dT", "dC",
                            "dP", "dX", "out")}
        self.dram = {}

    def op(self, eng, fn, tick=None, k=1):
        if tick is None:
            self.q[eng].append(fn)
            return None
        ctr = self.c[tick]
        ctr.n += k * ctr.step
        kk = k * ctr.step
        self.q[eng].append(lambda e, f=fn, s=ctr.sem, kk=kk: f(e).then_inc(s, kk))
        return ctr.n

    def wait(self, eng, tick, val):
        if val and val > 0:
            ctr = self.c[tick]
            if ctr.step == 16 and val < ctr.n and val > ctr.sealed:
                raise AssertionError(
                    f"late dma wait: {tick}>={val} registered after {ctr.n} issued "
                    f"(sealed {ctr.sealed}) - wait on the newest mark or "
                    f"reorder so the wait precedes subsequent dmas")
            ctr.maxwait = max(ctr.maxwait, val)
            self.q[eng].append(lambda e, s=ctr.sem, v=val: e.wait_ge(s, v))

    def _dma_gate(self, eng, tick):
        ctr = self.c[tick]
        if ctr.maxwait > ctr.selfwait:
            v = ctr.maxwait
            self.q[eng].append(lambda e, s=ctr.sem, v=v: e.wait_ge(s, v))
            ctr.selfwait = v
        ctr.sealed = ctr.selfwait

    def dma(self, eng, out, in_, tick):
        self._dma_gate(eng, tick)
        return self.op(eng, lambda e, o=out, i=in_: e.dma_start(out=o, in_=i), tick=tick)

    def dmat(self, eng, out, in_, tick):
        self._dma_gate(eng, tick)
        return self.op(eng, lambda e, o=out, i=in_: e.dma_start_transpose(out=o, in_=i), tick=tick)

    def din(self, name, shape, dtype=F32):
        h = self.nc.dram_tensor(name, list(shape), dtype, kind="ExternalInput")
        self.dram[name] = h
        return h.ap()

    def dout(self, name, shape, dtype=F32):
        h = self.nc.dram_tensor(name, list(shape), dtype, kind="ExternalOutput")
        self.dram[name] = h
        return h.ap()

    def dscr(self, name, shape, dtype=F32):
        h = self.nc.dram_tensor(name, list(shape), dtype, kind="Internal")
        self.dram[name] = h
        return h.ap()

    def sb(self, name, shape, dtype, off):
        return self.nc.alloc_sbuf_tensor_at(name, list(shape), dtype, offset=off)

    def _finish(self):
        self.wait("sync", "out", self.c["out"].n)
        for k in ("dS", "dG", "dV", "dT", "dC", "dP", "dX"):
            self.wait("sync", k, self.c[k].n)
        block = self.stack.enter_context(self.nc.Block())
        for name in ("sync", "vector", "scalar", "tensor", "gpsimd"):
            ops = self.q[name]

            def make(ops):
                def _(e):
                    for f in ops:
                        f(e)
                return _
            getattr(block, name)(make(ops))
        self.stack.close()
        return self.nc

    def _tap_sbuf(self, name, ap_list, shape):
        dbg = self.dout("dbg", shape, F32)
        col = 0
        for ap, width in ap_list:
            self.dma("gpsimd", dbg[:, col:col + width], ap, "out")
            col += width
        return self._finish()

    def build(self):
        nc, tap = self.nc, self.tap
        K = 1024

        x_in = self.din("x", [T, D])
        y_out = self.dout("y", [T, D])
        identity = self.din("identity", [128, 128])
        onesf_in = self.din("onesf", [1, 128])
        onesb_in = self.din("onesb", [1, 512], BF16)
        triu_in = self.din("triu", [128, 128], BF16)
        ropeC_in = self.din("ropeC", [T, 512])
        ropeS_in = self.din("ropeS", [T, 512])
        L0_WT = self.din("L0_WT", [D, 3 * D])
        L0_bcol_in = self.din("L0_bcol", [128, 16])
        L0_bvrow_in = self.din("L0_bvrow", [1, D])
        L0_wg_in = self.din("L0_wg", [I0, D])
        L0_wu_in = self.din("L0_wu", [I0, D])
        L0_wd_in = self.din("L0_wd", [D, I0])
        L0_wdT_in = self.din("L0_wdT", [I0, D])
        L0_clipx = self.din("L0_clipx", [1, 1])
        L0_clipd = self.din("L0_clipd", [1, 1])
        L1_WT = {k: self.din(f"L1_W{k}T", [D, D]) for k in "qkvo"}
        L1_brow = {k: self.din(f"L1_b{k}row", [1, D]) for k in "qkvo"}
        L1_w1T_in = self.din("L1_w1T", [D, H1])
        L1_w2T_in = self.din("L1_w2T", [H1, D])
        L1_cw_in = self.din("L1_cw", [128, NH1 * 3])
        L1_cb_in = self.din("L1_cb", [128, NH1])

        qpark = self.dscr("qpark", [D, T], BF16)
        xpark = self.dscr("xpark", [T, D], F32)
        w1q_park = self.dscr("w1q_park", [D, H1], BF16)
        w2q_park = self.dscr("w2q_park", [H1, D], BF16)
        cc_in = self.dscr("cc_in", [1, 1])
        cc_out = self.dscr("cc_out", [1, 1])
        cc_in2 = self.dscr("cc_in2", [1, 1])
        cc_out2 = self.dscr("cc_out2", [1, 1])
        self.ccs = (cc_in, cc_out, cc_in2, cc_out2)

        BASE = 17 * K
        CONST = self.sb("CONST", [128, 128], F32, 205 * K)
        TRIU = self.sb("TRIU", [128, 128], BF16, int(205.5 * K))
        ONESF = self.sb("ONESF", [1, 128], F32, int(205.75 * K))
        ONESB = self.sb("ONESB", [1, 512], BF16, int(206.25 * K))
        BCOL = self.sb("BCOL", [128, 16], F32, int(207.25 * K))
        BQ8 = self.sb("BQ8", [128, 8], F32, int(207.5 * K))
        SM = self.sb("SM", [128, 32], F32, int(207.625 * K))
        ROW = self.sb("ROW", [1, 32], F32, int(207.8125 * K))
        NK = self.sb("NK", [128, 2], F32, 208 * K)
        RC = self.sb("RC", [128, 4], F32, int(208.0625 * K))
        MU = self.sb("MU", [128, 8], F32, int(208.25 * K))
        SC = self.sb("SC", [128, 8], F32, int(208.5 * K))
        BVROW = self.sb("BVROW", [1, D], BF16, 209 * K)
        TMPB = self.sb("TMPB", [128, D], BF16, 211 * K)
        TMP2 = self.sb("TMP2", [128, D], F32, int(219.5 * K))

        X = self.sb("X", [128, NT * D], F32, BASE)
        ROPEC = self.sb("ROPEC", [128, NT * 512], BF16, BASE)
        ROPES = self.sb("ROPES", [128, NT * 512], BF16, BASE + 16 * K)
        BROWS = self.sb("BROWS", [1, 4 * D], BF16, BASE + 32 * K)
        XSTG = self.sb("XSTG", [128, 2 * D], F32, BASE + 40 * K)
        WRES = self.sb("WRES", [128, NC * D], BF16, BASE + 48 * K)
        W2QT = self.sb("W2QT", [128, NH1 * D], BF16, BASE)
        XT = self.sb("XT", [128, NC * T], BF16, 81 * K)
        WS2 = self.sb("WS2", [128, 7 * D], F32, 81 * K)
        WACC = self.sb("WACC", [128, D], F32, 109 * K)
        WQB = self.sb("WQB", [128, D], BF16, 213 * K)
        WQF = self.sb("WQF", [128, D], F32, 215 * K)
        WSM = self.sb("WSM", [128, 4], F32, 219 * K)
        WROW = self.sb("WROW", [1, 8], F32, int(219.125 * K))
        WSC = self.sb("WSC", [128, 4], F32, int(219.25 * K))
        NK8 = self.sb("NK8", [128, NHP], F32, int(219.3125 * K))
        WDSB = self.sb("WDSB", [128, 1024], F32, 213 * K)
        WDSA = self.sb("WDSA", [128, 512], F32, 217 * K)
        MUX = self.sb("MUX", [128, 16], F32, int(219.375 * K))
        YTM = self.sb("YTM", [128, 4 * D], BF16, 81 * K)
        XPOSE = self.sb("XPOSE", [128, D], BF16, 89 * K)
        XPO4 = self.sb("XPO4", [128, 3 * D], BF16, 107 * K)
        YS2 = self.sb("YS2", [128, 2 * D], F32, 81 * K)
        WORES = self.sb("WORES", [128, NC * D], BF16, 91 * K)
        XQ1T = self.sb("XQ1T", [128, NC * T], BF16, 81 * K)
        KF = self.sb("KF", [128, NC * T], BF16, 113 * K)
        H1S = self.sb("H1S", [128, NH1 * 1026], BF16, 113 * K)
        VT = self.sb("VT", [128, NT * NH * 65], BF16, 145 * K)
        QSTG = self.sb("QSTG", [128, T], BF16, 178 * K)
        YF = self.sb("YF", [128, 2 * TB], F32, 182 * K)
        PT = self.sb("PT", [128, 2 * 1024], BF16, 186 * K)
        QB = self.sb("QB", [128, NC * TB], BF16, 190 * K)
        VW = self.sb("VW", [128, NC * D], BF16, 182 * K)
        WTILE = self.sb("WTILE", [128, 2 * NC * 128], BF16, 198 * K)
        HBUF = self.sb("HBUF", [128, NI0 * T], F16, BASE)
        XQT = self.sb("XQT", [128, NC * T], BF16, 105 * K)
        WDQT = self.sb("WDQT", [128, NI0 * D], F16, 139 * K)
        WQT = self.sb("WQT", [128, NC * 128], BF16, 137 * K)
        WSTR = self.sb("WSTR", [128, I0], F32, 183 * K)
        TMPF = self.sb("TMPF", [128, I0], F32, 194 * K)
        ABSA = self.sb("ABSA", [128, D], F32, 178 * K)
        HTOK = self.sb("HTOK", [128, D], F32, 182 * K)
        RTOK = self.sb("RTOK", [128, T], F32, 186 * K)
        W1C = self.sb("W1C", [128, 2 * NC * 128], BF16, 194 * K)
        ROPT = self.sb("ROPT", [128, D], BF16, 202 * K)
        CWCOL = self.sb("CWCOL", [128, NH1 * 3], F32, 198 * K)
        CBCOL = self.sb("CBCOL", [128, NH1], F32, int(198.5 * K))

        self.t = dict(locals())

        psA = self.stack.enter_context(nc.psum_tensor("psA", [128, 1024], F32))
        psB = self.stack.enter_context(nc.psum_tensor("psB", [128, 1024], F32))
        psPA = self.stack.enter_context(nc.psum_tensor("psPA", [128, 512], F32))
        psPB = self.stack.enter_context(nc.psum_tensor("psPB", [128, 512], F32))
        psX = self.stack.enter_context(nc.psum_tensor("psX", [128, 512], F32))
        psY = self.stack.enter_context(nc.psum_tensor("psY", [128, 512], F32))
        self.p = dict(psA=psA, psB=psB, psPA=psPA, psPB=psPB, psX=psX, psY=psY)

        c = self.c

        self.dma("gpsimd", CONST[:], identity, "dG")
        self.dma("gpsimd", TRIU[:], triu_in, "dG")
        self.dma("gpsimd", ONESF[:], onesf_in, "dG")
        self.dma("gpsimd", ONESB[:], onesb_in, "dG")
        self.dma("gpsimd", BCOL[:], L0_bcol_in, "dG")
        self.dma("gpsimd", BVROW[:], L0_bvrow_in, "dG")
        g_consts = c["dG"].n
        for i in range(NT):
            self.dma("sync", X[:, i * D:(i + 1) * D], x_in[i * 128:(i + 1) * 128, :], "dS")
        s_x = c["dS"].n
        self.wait("vector", "dG", g_consts)
        v_consts = self.op("vector", lambda e: e.tensor_scalar_mul(
            out=BQ8[:], in0=BCOL[:, 0:8], scalar1=0.125), "v")

        self.wait("tensor", "dS", s_x)
        self.wait("tensor", "dG", g_consts)
        v_xt = self._transpose_f32_to_bf16(
            src=lambda i, cp: X[:, i * D + cp * 128:i * D + cp * 128 + 128],
            dst=lambda i, cp: XT[:, cp * T + i * 128:cp * T + i * 128 + 128],
            n_i=NT, n_c=NC, psX=psX, psY=psY, CONST=CONST)
        if tap == "xt":
            self.wait("gpsimd", "s", v_xt)
            return self._tap_sbuf("xt", [(XT[:, cp * T:(cp + 1) * T], T) for cp in range(NC)],
                                  [128, NC * T])

        self.wait("tensor", "s", v_xt)
        self.wait("scalar", "v", v_consts)
        ps_free = {0: 0, 1: 0}
        wt_free = {0: 0, 1: 0}
        qstg_parked = 0
        for o in range(16):
            buf = o % 2
            self.wait("gpsimd", "t", wt_free[buf])
            for cp in range(NC):
                self.dma("gpsimd", WTILE[:, buf * 1024 + cp * 128:buf * 1024 + cp * 128 + 128],
                         L0_WT[cp * 128:(cp + 1) * 128, o * 128:(o + 1) * 128], "dG")
            gm = c["dG"].n
            self.wait("tensor", "dG", gm)
            for tb4 in range(4):
                pst, pi = (psA, 0) if tb4 % 2 == 0 else (psB, 1)
                self.wait("tensor", "s", ps_free[pi])
                tm = None
                for cp in range(NC):
                    tm = self.op("tensor", lambda e, b=buf, cp=cp, tb4=tb4, pst=pst: e.matmul(
                        pst[:, 0:512], WTILE[:, b * 1024 + cp * 128:b * 1024 + cp * 128 + 128],
                        XT[:, cp * T + tb4 * 512:cp * T + (tb4 + 1) * 512],
                        start=(cp == 0), stop=(cp == NC - 1)),
                        tick=("t" if cp == NC - 1 else None))
                if tb4 == 3:
                    wt_free[buf] = tm
                self.wait("scalar", "t", tm)
                if o < 8:
                    if tb4 == 0 and qstg_parked:
                        self.wait("scalar", "dS", qstg_parked)
                    sm = self.op("scalar", lambda e, o=o, tb4=tb4, pst=pst: e.activation(
                        out=QSTG[:, tb4 * 512:(tb4 + 1) * 512], in_=pst[:, 0:512],
                        func=ACT.Identity, bias=BQ8[:, o:o + 1], scale=0.125), "s")
                else:
                    sm = self.op("scalar", lambda e, o=o, tb4=tb4, pst=pst: e.activation(
                        out=KF[:, (o - 8) * T + tb4 * 512:(o - 8) * T + (tb4 + 1) * 512],
                        in_=pst[:, 0:512], func=ACT.Identity,
                        bias=BCOL[:, o:o + 1], scale=1.0), "s")
                ps_free[pi] = sm
            if o < 8:
                self.wait("sync", "s", sm)
                qstg_parked = self.dma("sync", qpark[o * 128:(o + 1) * 128, :], QSTG[:], "dS")
        s_qk = c["s"].n
        d_qpark = c["dS"].n

        for cp in range(NC):
            self.dma("gpsimd", VW[:, cp * D:(cp + 1) * D],
                     L0_WT[cp * 128:(cp + 1) * 128, 2 * D:3 * D], "dG")
        gm_vw = c["dG"].n
        self.wait("tensor", "dG", gm_vw)
        ps_free_v = {0: 0, 1: 0}
        for tk in range(NT):
            pst, pi = (psA, 0) if tk % 2 == 0 else (psB, 1)
            if ps_free[pi]:
                self.wait("tensor", "s", ps_free[pi])
                ps_free[pi] = 0
            if ps_free_v[pi]:
                self.wait("tensor", "v", ps_free_v[pi])
            for cp in range(NC):
                self.op("tensor", lambda e, cp=cp, tk=tk, pst=pst: e.matmul(
                    pst[:, 0:512], XT[:, cp * T + tk * 128:cp * T + tk * 128 + 128],
                    VW[:, cp * D:cp * D + 512], start=(cp == 0), stop=False), None)
                self.op("tensor", lambda e, cp=cp, tk=tk, pst=pst: e.matmul(
                    pst[:, 512:1024], XT[:, cp * T + tk * 128:cp * T + tk * 128 + 128],
                    VW[:, cp * D + 512:(cp + 1) * D], start=(cp == 0), stop=False), None)
            self.op("tensor", lambda e, pst=pst: e.matmul(
                pst[:, 0:512], ONESB[0:1, 0:128], BVROW[0:1, 0:512],
                start=False, stop=True), None)
            tm = self.op("tensor", lambda e, pst=pst: e.matmul(
                pst[:, 512:1024], ONESB[0:1, 0:128], BVROW[0:1, 512:1024],
                start=False, stop=True), "t")
            self.wait("vector", "t", tm)
            vt_tile = VT[:, tk * NH * 65:(tk + 1) * NH * 65].rearrange("p (h d) -> p h d", h=NH)
            self.op("vector", lambda e, pst=pst, vt=vt_tile: e.tensor_copy(
                out=vt[:, :, 0:64], in_=pst[:, 0:1024].rearrange("p (h d) -> p h d", h=NH)), None)
            ps_free_v[pi] = self.op("vector", lambda e, vt=vt_tile: e.memset(
                vt[:, :, 64:65], 1.0), "v")
        v_vt = c["v"].n
        t_vproj = c["t"].n

        if tap == "qkv":
            self.wait("gpsimd", "s", s_qk)
            self.wait("gpsimd", "v", v_vt)
            aps = [(KF[:, cp * T:(cp + 1) * T], T) for cp in range(NC)]
            aps += [(VT[:, i * NH * 65:(i + 1) * NH * 65], NH * 65) for i in range(NT)]
            return self._tap_sbuf("qkv", aps, [128, NC * T + NT * NH * 65])

        wp_hook, wp_drain = self._wprep_build(xt_free_t=t_vproj)
        self._attention(layer=0, start_after={
            "sync": [("dS", d_qpark), ("t", t_vproj)],
            "tensor": [("v", v_vt), ("s", s_qk)]}, hook=wp_hook)
        wp_drain()
        v_ln1 = c["v"].n

        if tap == "x1":
            self.wait("gpsimd", "v", v_ln1)
            return self._tap_sbuf("x1", [(X[:, i * D:(i + 1) * D], D) for i in range(NT)],
                                  [128, NT * D])

        if tap is not None and tap.startswith("stop_x1"):
            return self._finish()

        self._l0_ffn()
        if tap == "x2":
            self.wait("gpsimd", "dS", self.x2_parked)
            dbg = self.dout("dbg", [T, D], F32)
            for i in range(NT):
                self.dma("gpsimd", dbg[i * 128:(i + 1) * 128, :],
                         xpark[i * 128:(i + 1) * 128, :], "out")
            return self._finish()

        self._l1_attn()
        if tap == "x3":
            self.wait("gpsimd", "dP", self.x3_parked)
            dbg = self.dout("dbg", [T, D], F32)
            for i in range(NT):
                self.dma("gpsimd", dbg[i * 128:(i + 1) * 128, :],
                         xpark[i * 128:(i + 1) * 128, :], "out")
            return self._finish()

        self._l1_ffn()
        return self._finish()

    def _transpose_f32_to_bf16(self, src, dst, n_i, n_c, psX, psY, CONST):
        evacs = []
        for kk in range(n_i * n_c):
            i, cp = divmod(kk, n_c)
            pst = psX if kk % 2 == 0 else psY
            if kk >= 2:
                self.wait("tensor", "s", evacs[kk - 2])
            tm = self.op("tensor", lambda e, i=i, cp=cp, pst=pst: e.transpose(
                pst[:, 0:128], src(i, cp), CONST[:]), "t")
            self.wait("scalar", "t", tm)
            evacs.append(self.op("scalar", lambda e, i=i, cp=cp, pst=pst: e.activation(
                out=dst(i, cp), in_=pst[:, 0:128], func=ACT.Identity), "s"))
        return self.c["s"].n

    def vop(self, fn):
        m = self.op("vector", fn, "v")
        self.wait("vector", "v", m)
        return m

    def gop(self, fn):
        m = self.op("gpsimd", fn, "g")
        self.wait("gpsimd", "g", m)
        return m

    def recip(self, dst, src, tmp):
        self.vop(lambda e: e.reciprocal(out=dst, in_=src))
        self.vop(lambda e: e.tensor_mul(out=tmp, in0=dst, in1=src))
        self.vop(lambda e: e.tensor_scalar(out=tmp, in0=tmp, scalar1=-1.0,
                                           scalar2=2.0, op0=ALU.mult, op1=ALU.add))
        return self.vop(lambda e: e.tensor_mul(out=dst, in0=dst, in1=tmp))

    def _layernorm_X(self, xt_of, tiles, SQS, mu_of=None):
        self._ln_done = {}
        for i in tiles:
            MU = mu_of(i) if mu_of is not None else self.t["MU"]
            xt = xt_of(i)
            vm = self.vop(lambda e, xt=xt: e.reduce_sum(
                out=MU[:, 0:1], in_=xt, axis=AX))
            self.wait("scalar", "v", vm)
            sm = self.op("scalar", lambda e, xt=xt: e.activation(
                out=SQS[:], in_=xt, func=ACT.Square, accum_out=MU[:, 1:2]), "s")
            self.wait("vector", "s", sm)
            self.vop(lambda e: e.tensor_scalar_mul(
                out=MU[:, 0:1], in0=MU[:, 0:1], scalar1=1.0 / D))
            self.vop(lambda e: e.tensor_mul(
                out=MU[:, 2:3], in0=MU[:, 0:1], in1=MU[:, 0:1]))
            self.vop(lambda e: e.scalar_tensor_tensor(
                out=MU[:, 1:2], in0=MU[:, 1:2], scalar=1.0 / D, in1=MU[:, 2:3],
                op0=ALU.mult, op1=ALU.subtract))
            vm = self.vop(lambda e: e.tensor_scalar_add(
                out=MU[:, 1:2], in0=MU[:, 1:2], scalar1=EPS))
            self.wait("scalar", "v", vm)
            sm0 = self.op("scalar", lambda e: e.activation(
                out=MU[:, 2:3], in_=MU[:, 1:2], func=ACT.Ln), "s")
            self.wait("scalar", "s", sm0)
            sm = self.op("scalar", lambda e: e.activation(
                out=MU[:, 2:3], in_=MU[:, 2:3], func=ACT.Exp, scale=-0.5), "s")
            self.wait("vector", "s", sm)
            self.vop(lambda e: e.scalar_tensor_tensor(
                out=MU[:, 3:4], in0=MU[:, 0:1], scalar=-1.0, in1=MU[:, 2:3],
                op0=ALU.mult, op1=ALU.mult))
            self._ln_done[i] = self.op("vector", lambda e, xt=xt: e.tensor_scalar(
                out=xt, in0=xt, scalar1=MU[:, 2:3], scalar2=MU[:, 3:4],
                op0=ALU.mult, op1=ALU.add), "v")

    def _attention(self, layer, start_after=None, hook=None):
        t = self.t
        c = self.c
        KF, VT, QB, PT, YF = t["KF"], t["VT"], t["QB"], t["PT"], t["YF"]
        NK8, RC, ROW, CONST, ONESF, TRIU = t["NK8"], t["RC"], t["ROW"], t["CONST"], t["ONESF"], t["TRIU"]
        X, YTM = t["X"], t["YTM"]
        psA, psB, psPA, psPB, psX, psY = (self.p[k] for k in
                                          ("psA", "psB", "psPA", "psPB", "psX", "psY"))
        qpark = self.dram["qpark"].ap()
        causal = (layer == 0)

        exp_done = {0: 0, 1: 0}
        km_done = {0: 0, 1: 0}
        pt_free = {0: 0, 1: 0}
        pv_free = 0
        ps = {"x": 0, "y": 0}
        qb_used = 0

        self._post_ps_free = {0: 0, 1: 0}
        self._post_wo = {}
        self._post_dm = {}
        self._post_park = {0: 0, 1: 0, 2: 0, 3: 0}
        for b in range(NB):
            self.wait("sync", "t", qb_used)
            if start_after and b == 0:
                for tick, val in start_after.get("sync", []):
                    self.wait("sync", tick, val)
            qb_marks = {}
            for cp in range(NC):
                qb_marks[cp] = self.dma(
                    "sync", QB[:, cp * TB:(cp + 1) * TB],
                    qpark[cp * 128:(cp + 1) * 128, b * TB:(b + 1) * TB], "dS")
                self.wait("sync", "dS", qb_marks[cp])
            if start_after and b == 0:
                for tick, val in start_after.get("tensor", []):
                    self.wait("tensor", tick, val)
            if self._post_ps_free[0] or self._post_ps_free[1]:
                self.wait("tensor", "v", max(self._post_ps_free[0], self._post_ps_free[1]))
                self._post_ps_free = {0: 0, 1: 0}

            valid = list(range(4 * (b + 1))) if causal else list(range(NT))
            last_i = len(valid) - 1
            for hp in range(NHP):
                if hook is not None:
                    hook(2 * (b * NHP + hp), ps)
                self.wait("tensor", "dS", qb_marks[hp])
                hA, hB = 2 * hp, 2 * hp + 1

                def chunk_lo(tc):
                    jj = tc - 4 * b
                    return 128 * jj if (causal and jj > 0) else 0

                def emit_sc(idx):
                    tc = valid[idx]
                    buf = idx % 2
                    pst = psA if buf == 0 else psB
                    lo = chunk_lo(tc)
                    self.wait("tensor", "s", exp_done[buf])
                    self.wait("tensor", "v", km_done[buf])
                    self.op("tensor", lambda e, hp=hp, tc=tc, pst=pst, lo=lo: e.matmul(
                        pst[:, lo:512], KF[0:64, hp * T + tc * 128:hp * T + tc * 128 + 128],
                        QB[0:64, hp * TB + lo:hp * TB + 512],
                        start=True, stop=True, tile_position=(0, 0)), None)
                    return self.op("tensor", lambda e, hp=hp, tc=tc, pst=pst, lo=lo: e.matmul(
                        pst[:, 512 + lo:1024], KF[64:128, hp * T + tc * 128:hp * T + tc * 128 + 128],
                        QB[64:128, hp * TB + lo:hp * TB + 512],
                        start=True, stop=True, tile_position=(64, 0)), "t")

                sc_marks = {0: emit_sc(0)}
                nk_ready = 0
                if b == 0 or getattr(self, "fresh_k", False):
                    self.wait("vector", "t", sc_marks[0])
                    km = self.op("vector", lambda e: e.reduce_max(
                        out=RC[:, 2:3], in_=psA[:, 0:1024], axis=AX, negate=True), "v")
                    km_done[0] = km
                    sc_marks[1] = emit_sc(1)
                    self.wait("tensor", "v", km)
                    self.wait("tensor", "v", ps["x"])
                    tmk = self.op("tensor", lambda e: e.transpose(
                        psX[0:1, 0:128], RC[:, 2:3], CONST[:]), "t")
                    self.wait("vector", "t", tmk)
                    vk = self.op("vector", lambda e: e.tensor_reduce(
                        out=ROW[0:1, 0:1], in_=psX[0:1, 0:128], axis=AX, op=ALU.min), "v")
                    if layer == 1:
                        self.wait("vector", "v", vk)
                        vk = self.vop(lambda e: e.tensor_scalar_mul(
                            out=ROW[0:1, 0:1], in0=ROW[0:1, 0:1], scalar1=0.125))
                    ps["x"] = vk
                    self.wait("tensor", "v", vk)
                    self.wait("tensor", "v", ps["y"])
                    tmk2 = self.op("tensor", lambda e: e.matmul(
                        psY[:, 0:1], ONESF[0:1, 0:128], ROW[0:1, 0:1],
                        start=True, stop=True), "t")
                    self.wait("vector", "t", tmk2)
                    vk2 = self.op("vector", lambda e, hp=hp: e.tensor_copy(
                        out=NK8[:, hp:hp + 1], in_=psY[:, 0:1]), "v")
                    ps["y"] = vk2
                    nk_ready = vk2
                else:
                    sc_marks[1] = emit_sc(1)

                for idx, tc in enumerate(valid):
                    buf = idx % 2
                    pst = psA if buf == 0 else psB
                    lo = chunk_lo(tc)
                    self.wait("scalar", "t", sc_marks.pop(idx))
                    if idx == 0 and nk_ready:
                        self.wait("scalar", "v", nk_ready)
                    self.wait("scalar", "t", pt_free[buf])
                    esc = 0.125 if layer == 1 else 1.0
                    sm_expA = self.op("scalar", lambda e, pst=pst, buf=buf, lo=lo, hp=hp, esc=esc: e.activation(
                        out=PT[:, buf * 1024 + lo:buf * 1024 + 512],
                        in_=pst[:, lo:512], func=ACT.Exp,
                        bias=NK8[:, hp:hp + 1], scale=esc), "s")
                    sm_exp = self.op("scalar", lambda e, pst=pst, buf=buf, lo=lo, hp=hp, esc=esc: e.activation(
                        out=PT[:, buf * 1024 + 512 + lo:buf * 1024 + 1024],
                        in_=pst[:, 512 + lo:1024], func=ACT.Exp,
                        bias=NK8[:, hp:hp + 1], scale=esc), "s")
                    exp_done[buf] = sm_exp
                    pv_wait_vA = pv_wait_v = 0
                    if causal and tc >= 4 * b:
                        self.wait("vector", "s", sm_expA)
                        pv_wait_vA = self.op("vector", lambda e, buf=buf, lo=lo: e.tensor_mul(
                            out=PT[:, buf * 1024 + lo:buf * 1024 + lo + 128],
                            in0=PT[:, buf * 1024 + lo:buf * 1024 + lo + 128], in1=TRIU[:]), "v")
                        self.wait("vector", "s", sm_exp)
                        pv_wait_v = self.op("vector", lambda e, buf=buf, lo=lo: e.tensor_mul(
                            out=PT[:, buf * 1024 + 512 + lo:buf * 1024 + 512 + lo + 128],
                            in0=PT[:, buf * 1024 + 512 + lo:buf * 1024 + 512 + lo + 128],
                            in1=TRIU[:]), "v")
                    if idx + 2 <= last_i:
                        sc_marks[idx + 2] = emit_sc(idx + 2)
                    self.wait("tensor", "s", sm_expA)
                    if pv_wait_vA:
                        self.wait("tensor", "v", pv_wait_vA)
                    if idx == 0:
                        self.wait("tensor", "s", pv_free)
                    self.op("tensor", lambda e, tk=tc, hA=hA, buf=buf, lo=lo, st=(idx == 0), sp=(idx == last_i): e.matmul(
                        psPA[0:65, lo:512],
                        VT[:, tk * NH * 65 + hA * 65:tk * NH * 65 + hA * 65 + 65],
                        PT[:, buf * 1024 + lo:buf * 1024 + 512],
                        start=st, stop=sp), None)
                    self.wait("tensor", "s", sm_exp)
                    if pv_wait_v:
                        self.wait("tensor", "v", pv_wait_v)
                    pt_free[buf] = self.op("tensor", lambda e, tk=tc, hB=hB, buf=buf, lo=lo, st=(idx == 0), sp=(idx == last_i): e.matmul(
                        psPB[0:65, lo:512],
                        VT[:, tk * NH * 65 + hB * 65:tk * NH * 65 + hB * 65 + 65],
                        PT[:, buf * 1024 + 512 + lo:buf * 1024 + 1024],
                        start=st, stop=sp), "t")
                tm_pv = pt_free[last_i % 2]
                if hook is not None:
                    hook(2 * (b * NHP + hp) + 1, ps)
                self.wait("scalar", "t", tm_pv)
                self.op("scalar", lambda e: e.activation(
                    out=YF[0:65, 0:512], in_=psPA[0:65, 0:512], func=ACT.Identity), None)
                pv_free = self.op("scalar", lambda e: e.activation(
                    out=YF[0:65, 512:1024], in_=psPB[0:65, 0:512], func=ACT.Identity), "s")
                for j in range(4):
                    ti = 4 * b + j
                    self.wait("tensor", "v", ps["x"])
                    self.wait("tensor", "s", pv_free)
                    tmtA = self.op("tensor", lambda e, j=j: e.transpose(
                        psX[:, 0:65], YF[0:65, j * 128:j * 128 + 128],
                        CONST[0:65, 0:65]), "t")
                    self.wait("tensor", "v", ps["y"])
                    tmtB = self.op("tensor", lambda e, j=j: e.transpose(
                        psY[:, 0:65], YF[0:65, 512 + j * 128:512 + j * 128 + 128],
                        CONST[0:65, 0:65]), "t")
                    self.wait("vector", "t", tmtA)
                    self.op("vector", lambda e: e.reciprocal(
                        out=RC[:, 0:1], in_=psX[:, 64:65]), None)
                    self.wait("vector", "t", tmtB)
                    vrc = self.op("vector", lambda e: e.reciprocal(
                        out=RC[:, 1:2], in_=psY[:, 64:65]), "v")
                    self.wait("vector", "v", vrc)
                    for half, h, pst2 in ((0, hA, psX), (1, hB, psY)):
                        if layer == 0:
                            mark = self.op("vector", lambda e, pst2=pst2, half=half, ti=ti, h=h: e.scalar_tensor_tensor(
                                out=X[:, ti * D + h * 64:ti * D + h * 64 + 64],
                                in0=pst2[:, 0:64], scalar=RC[:, half:half + 1],
                                in1=X[:, ti * D + h * 64:ti * D + h * 64 + 64],
                                op0=ALU.mult, op1=ALU.add), "v")
                        else:
                            mark = self.op("vector", lambda e, pst2=pst2, half=half, j=j, h=h: e.tensor_scalar_mul(
                                out=YTM[:, j * D + h * 64:j * D + h * 64 + 64],
                                in0=pst2[:, 0:64], scalar1=RC[:, half:half + 1]), "v")
                        if half == 0:
                            ps["x"] = mark
                        else:
                            ps["y"] = mark
                    if hp == NHP - 1:
                        if layer == 1:
                            self._l1_post_stage1(b, j)
                        else:
                            self._l0_post_tile(b, j)
            qb_used = c["t"].n
            if layer == 1:
                self._l1_block_post(b)
        self.attn_v_done = c["v"].n

    def _scalar_bcast(self, src_row_ap, dst_col_ap, psY, ONESF, width=1):
        self.wait("tensor", "v", self.c["v"].n)
        tm = self.op("tensor", lambda e: e.matmul(
            psY[:, 0:width], ONESF[0:1, 0:128], src_row_ap,
            start=True, stop=True), "t")
        self.wait("vector", "t", tm)
        return self.vop(lambda e: e.tensor_copy(out=dst_col_ap, in_=psY[:, 0:width]))

    def _part_reduce_max(self, col_ap, row_cell, psX, CONST):
        self.wait("tensor", "v", self.c["v"].n)
        tm = self.op("tensor", lambda e: e.transpose(
            psX[0:1, 0:128], col_ap, CONST[:]), "t")
        self.wait("vector", "t", tm)
        return self.vop(lambda e: e.tensor_reduce(
            out=row_cell, in_=psX[0:1, 0:128], axis=AX, op=ALU.max))

    def _wprep_build(self, xt_free_t):
        t, c = self.t, self.c
        WS2, WQB, WQF, WSM, WROW, WSC, CONST, ONESF = (t[k] for k in
            ("WS2", "WQB", "WQF", "WSM", "WROW", "WSC", "CONST", "ONESF"))
        psX, psY = self.p["psX"], self.p["psY"]
        w1q_park = self.dram["w1q_park"].ap()
        w2q_park = self.dram["w2q_park"].ap()
        pieces = [("w1", t["L1_w1T_in"], w1q_park, cp, qq)
                  for cp in range(NC) for qq in range(4)] + \
                 [("w2", t["L1_w2T_in"], w2q_park, ch, 0) for ch in range(NH1)]
        NPC = len(pieces)
        st = {"xs": {}, "first": {"w1": True, "w2": True}, "gated": False,
              "need_wsc": True, "wsc_v": 0}
        self._wprep_parked = 0

        def load(n_):
            which, src, dst, r0, q0 = pieces[n_ % NPC]
            sl = n_ % 7
            eng, tick = ("scalar", "dC") if n_ % 2 == 0 else ("gpsimd", "dG")
            if not st["gated"]:
                self.wait("scalar", "t", xt_free_t)
                self.wait("gpsimd", "t", xt_free_t)
                st["gated"] = True
            if st["xs"].get(sl):
                kind, m = st["xs"][sl]
                self.wait(eng, kind, m)
            self.dma(eng, WS2[:, sl * D:(sl + 1) * D],
                     src[r0 * 128:(r0 + 1) * 128, q0 * D:(q0 + 1) * D], tick)

        def gate_loads(scal=False):
            self.wait("vector", "dC", c["dC"].n)
            self.wait("vector", "dG", c["dG"].n)
            if scal:
                self.wait("scalar", "dC", c["dC"].n)
                self.wait("scalar", "dG", c["dG"].n)

        def comp_a(n_):
            which = pieces[n_][0]
            sl = n_ % 7
            self.vop(lambda e, sl=sl: e.reduce_sum(
                out=WSM[:, 0:1], in_=WS2[:, sl * D:(sl + 1) * D], axis=AX,
                apply_absolute_value=True))
            col = 1 if which == "w1" else 2
            if st["first"][which]:
                st["first"][which] = False
                m = self.vop(lambda e, col=col: e.tensor_copy(
                    out=WSM[:, col:col + 1], in_=WSM[:, 0:1]))
            else:
                m = self.vop(lambda e, col=col: e.tensor_add(
                    out=WSM[:, col:col + 1], in0=WSM[:, col:col + 1],
                    in1=WSM[:, 0:1]))
            st["xs"][sl] = ("v", m)

        def finale(ps):
            for which, col, rc in (("w1", 1, 0), ("w2", 2, 2)):
                self.wait("tensor", "v", c["v"].n)
                tm = self.op("tensor", lambda e, col=col: e.transpose(
                    psX[0:1, 0:128], WSM[:, col:col + 1], CONST[:]), "t")
                self.wait("vector", "t", tm)
                ps["x"] = self.vop(lambda e, rc=rc: e.tensor_reduce(
                    out=WROW[0:1, rc:rc + 1], in_=psX[0:1, 0:128], axis=AX,
                    op=ALU.add))
                self.vop(lambda e, rc=rc: e.tensor_scalar_mul(
                    out=WROW[0:1, rc:rc + 1], in0=WROW[0:1, rc:rc + 1],
                    scalar1=1.0 / (D * H1)))
                self.vop(lambda e, rc=rc: e.tensor_scalar_max(
                    out=WROW[0:1, rc:rc + 1], in0=WROW[0:1, rc:rc + 1],
                    scalar1=1e-5))
                self.recip(WROW[0:1, rc + 1:rc + 2], WROW[0:1, rc:rc + 1],
                           WROW[0:1, 6:7])
            self._scalar_bcast(WROW[0:1, 1:2], WSC[:, 0:1], psY, ONESF, width=1)
            self._scalar_bcast(WROW[0:1, 3:4], WSC[:, 1:2], psY, ONESF, width=1)
            self._scalar_bcast(WROW[0:1, 0:1], WSC[:, 2:3], psY, ONESF, width=1)
            ps["y"] = self._scalar_bcast(WROW[0:1, 2:3], WSC[:, 3:4], psY, ONESF,
                                         width=1)
            st["wsc_v"] = self.vop(lambda e: e.memset(WSM[:, 3:4], MAGIC))

        def comp_b(n_):
            which, src, dst, r0, q0 = pieces[n_ - NPC]
            sl = n_ % 7
            if st["need_wsc"]:
                st["need_wsc"] = False
                self.wait("scalar", "v", st["wsc_v"])
            scol = WSC[:, 0:1] if which == "w1" else WSC[:, 1:2]
            if st.get("wqf_free"):
                self.wait("scalar", "v", st["wqf_free"])
            m = self.op("scalar", lambda e, sl=sl, scol=scol: e.activation(
                out=WQF[:], in_=WS2[:, sl * D:(sl + 1) * D], func=ACT.Identity,
                bias=WSM[:, 3:4], scale=scol), "s")
            st["xs"][sl] = ("s", m)
            self.wait("vector", "s", m)
            self.vop(lambda e: e.tensor_scalar(
                out=WQF[:], in0=WQF[:], scalar1=MAGIC, scalar2=-1.0,
                op0=ALU.subtract, op1=ALU.max))
            if self._wprep_parked:
                self.wait("vector", "dP", self._wprep_parked)
            v3 = self.vop(lambda e: e.tensor_scalar_min(
                out=WQB[:], in0=WQF[:], scalar1=1.0))
            st["wqf_free"] = v3
            self.wait("sync", "v", v3)
            self._wprep_parked = self.dma(
                "sync", dst[r0 * 128:(r0 + 1) * 128, q0 * D:(q0 + 1) * D],
                WQB[:], "dP")

        def hook(i, ps):
            if i == 0:
                pass
            elif i <= 32:
                gate_loads()
                for n_ in range(2 * i - 2, 2 * i):
                    comp_a(n_)
                if i == 32:
                    finale(ps)
            else:
                gate_loads(scal=True)
                for n_ in range(2 * i - 2, 2 * i):
                    comp_b(n_)
            for n_ in range(2 * i, min(2 * i + 2, 2 * NPC)):
                load(n_)

        def drain():
            self.wait("vector", "dC", c["dC"].n)
            self.wait("vector", "dG", c["dG"].n)
            self.wait("scalar", "dC", c["dC"].n)
            self.wait("scalar", "dG", c["dG"].n)
            for n_ in range(2 * NPC - 2, 2 * NPC):
                comp_b(n_)

        return hook, drain

    def _l0_ffn(self):
        t, c = self.t, self.c
        X, XQT, HBUF, WDQT, WSTR, TMPF, WQT = (t[k] for k in
            ("X", "XQT", "HBUF", "WDQT", "WSTR", "TMPF", "WQT"))
        TMPB, TMP2, SM, ROW, SC, MU, CONST, ONESF = (t[k] for k in
            ("TMPB", "TMP2", "SM", "ROW", "SC", "MU", "CONST", "ONESF"))
        psA, psB, psX, psY = (self.p[k] for k in ("psA", "psB", "psX", "psY"))
        L0_wg_in, L0_wu_in, L0_wd_in = t["L0_wg_in"], t["L0_wu_in"], t["L0_wd_in"]
        L0_wdT_in, WDSA, WDSB = t["L0_wdT_in"], t["WDSA"], t["WDSB"]
        psPA, psPB = self.p["psPA"], self.p["psPB"]
        L0_clipx, L0_clipd = t["L0_clipx"], t["L0_clipd"]
        xpark = self.dram["xpark"].ap()
        cc_in, cc_out, cc_in2, cc_out2 = self.ccs
        rg = getattr(self, "rg", RG8)
        SWDB = self.sb("SWDB", [128, D], F32, 105 * 1024)
        self.wait("tensor", "s", c["s"].n)
        self.wait("tensor", "v", c["v"].n)

        vloc = self._part_reduce_max(SM[:, 1:2], ROW[0:1, 0:1], psX, CONST)
        self.wait("sync", "v", vloc)
        dm = self.dma("sync", cc_in, ROW[0:1, 0:1], "dS")
        self.wait("gpsimd", "dS", dm)
        gm = self.op("gpsimd", lambda e: e.collective_compute(
            "AllReduce", ALU.max, replica_groups=getattr(self, "rg1", rg),
            ins=[cc_in], outs=[cc_out]), "g")
        self.wait("sync", "g", gm)
        dm = self.dma("sync", ROW[0:1, 1:2], cc_out, "dS")
        dmc = self.dma("sync", ROW[0:1, 2:3], L0_clipx, "dS")
        self.wait("scalar", "dS", dmc)
        sm = self.op("scalar", lambda e: e.activation(
            out=ROW[0:1, 3:4], in_=ROW[0:1, 2:3], func=ACT.Exp, scale=-1.0), "s")
        self.wait("vector", "s", sm)
        self.wait("vector", "dS", dmc)
        self.vop(lambda e: e.tensor_scalar_add(out=ROW[0:1, 3:4], in0=ROW[0:1, 3:4], scalar1=1.0))
        self.vop(lambda e: e.reciprocal(out=ROW[0:1, 4:5], in_=ROW[0:1, 3:4]))
        self.vop(lambda e: e.tensor_mul(out=ROW[0:1, 5:6], in0=ROW[0:1, 4:5], in1=ROW[0:1, 1:2]))
        self.vop(lambda e: e.tensor_scalar_mul(out=ROW[0:1, 5:6], in0=ROW[0:1, 5:6], scalar1=1.0 / 127.0))
        self.recip(ROW[0:1, 6:7], ROW[0:1, 5:6], ROW[0:1, 30:31])
        self.vop(lambda e: e.tensor_scalar_mul(out=ROW[0:1, 7:8], in0=ROW[0:1, 5:6], scalar1=1.0 / 127.0))
        vbc = self._scalar_bcast(ROW[0:1, 5:8], SC[:, 0:3], psY, ONESF, width=3)

        tq_free = {0: 0, 1: 0}
        tq_t_last = 0
        for i in range(NT):
            if tq_t_last:
                self.wait("vector", "t", tq_t_last)
            self.vop(lambda e, i=i: e.tensor_scalar(
                out=TMPF[:, 0:D], in0=X[:, i * D:(i + 1) * D],
                scalar1=SC[:, 1:2], scalar2=MAGIC, op0=ALU.mult, op1=ALU.add))
            self.vop(lambda e: e.tensor_scalar(
                out=TMPF[:, D:2 * D], in0=TMPF[:, 0:D],
                scalar1=MAGIC, scalar2=-128.0, op0=ALU.subtract, op1=ALU.max))
            vq = self.vop(lambda e: e.tensor_scalar_min(
                out=TMPF[:, 0:D], in0=TMPF[:, D:2 * D], scalar1=127.0))
            self.wait("tensor", "v", vq)
            for cp in range(NC):
                pst = psX if cp % 2 == 0 else psY
                self.wait("tensor", "s", tq_free[cp % 2])
                tm = self.op("tensor", lambda e, cp=cp, pst=pst: e.transpose(
                    pst[:, 0:128], TMPF[:, cp * 128:(cp + 1) * 128], CONST[:]), "t")
                tq_t_last = tm
                self.wait("scalar", "t", tm)
                tq_free[cp % 2] = self.op("scalar", lambda e, i=i, cp=cp, pst=pst: e.activation(
                    out=XQT[:, cp * T + i * 128:cp * T + i * 128 + 128],
                    in_=pst[:, 0:128], func=ACT.Identity), "s")
        v_xq = c["v"].n
        s_xq = c["s"].n
        self.x1_parked = self._l0_park
        self.wait("scalar", "v", v_xq)
        self.wait("vector", "v", v_xq)
        self.wait("scalar", "dS", c["dS"].n)
        self.wait("vector", "dS", c["dS"].n)

        wd_slots = [WDSB[:, 0:512], WDSB[:, 512:1024], WDSA[:, 0:512]]
        wd = {"xs": {0: 0, 1: 0, 2: 0}, "done": 0, "sinv": 0}
        self.wait("sync", "dP", self._wprep_parked)
        self.wait("sync", "v", c["v"].n)

        def wd_piece_a(n_):
            dc, q = divmod(n_, 6)
            width = 512 if q < 5 else 256
            return (L0_wd_in[dc * 128:(dc + 1) * 128,
                             q * 512:q * 512 + width], dc, q, width)

        def wd_load(n_):
            sl = n_ % 3
            if wd["xs"].get(sl):
                self.wait("sync", "v", wd["xs"][sl])
            if n_ < 48:
                srcap, dc, q, width = wd_piece_a(n_)
            else:
                m = n_ - 48
                ch, half = divmod(m, 2)
                srcap = L0_wdT_in[ch * 128:(ch + 1) * 128,
                                  half * 512:(half + 1) * 512]
                width = 512
            self.dma("sync", wd_slots[sl][:, 0:width], srcap, "dS")

        def wd_comp(n_):
            sl = n_ % 3
            if n_ < 48:
                dc, q, width = wd_piece_a(n_)[1:]
                self.vop(lambda e, sl=sl, width=width: e.reduce_max(
                    out=SM[:, 4:5], in_=wd_slots[sl][:, 0:width], axis=AX,
                    apply_absolute_value=True))
                if q == 0:
                    wd["xs"][sl] = self.vop(lambda e, dc=dc: e.tensor_copy(
                        out=SM[:, 8 + dc:9 + dc], in_=SM[:, 4:5]))
                else:
                    wd["xs"][sl] = self.vop(lambda e, dc=dc: e.tensor_max(
                        out=SM[:, 8 + dc:9 + dc], in0=SM[:, 8 + dc:9 + dc],
                        in1=SM[:, 4:5]))
            else:
                m = n_ - 48
                ch, half = divmod(m, 2)
                if wd["sinv"]:
                    self.wait("vector", "v", wd["sinv"])
                    wd["sinv"] = 0
                self.vop(lambda e, sl=sl, half=half: e.tensor_mul(
                    out=wd_slots[sl], in0=wd_slots[sl],
                    in1=TMP2[:, half * 512:(half + 1) * 512]))
                wd["done"] = self.vop(lambda e, ch=ch, half=half, sl=sl: e.tensor_scalar(
                    out=WDQT[:, ch * D + half * 512:ch * D + (half + 1) * 512],
                    in0=wd_slots[sl], scalar1=MAGIC, scalar2=MAGIC,
                    op0=ALU.add, op1=ALU.subtract))
                wd["xs"][sl] = wd["done"]

        def wd_chain():
            for dc in range(NC):
                self.wait("tensor", "v", c["v"].n)
                tm = self.op("tensor", lambda e, dc=dc: e.transpose(
                    psPA[0:1, 0:128], SM[:, 8 + dc:9 + dc], CONST[:]), "t")
                self.wait("vector", "t", tm)
                self.vop(lambda e, dc=dc: e.tensor_copy(
                    out=TMP2[0:1, dc * 128:(dc + 1) * 128], in_=psPA[0:1, 0:128]))
            self.vop(lambda e: e.reciprocal(out=TMP2[0:1, 0:D], in_=TMP2[0:1, 0:D]))
            vr = self.vop(lambda e: e.tensor_scalar_mul(
                out=TMP2[0:1, 0:D], in0=TMP2[0:1, 0:D], scalar1=127.0))
            for half in range(2):
                self.wait("tensor", "v", vr if half == 0 else c["v"].n)
                tm = self.op("tensor", lambda e, half=half: e.matmul(
                    psPB[:, 0:512], ONESF[0:1, 0:128],
                    TMP2[0:1, half * 512:(half + 1) * 512],
                    start=True, stop=True), "t")
                self.wait("vector", "t", tm)
                wd["sinv"] = self.vop(lambda e, half=half: e.tensor_copy(
                    out=TMP2[:, half * 512:(half + 1) * 512], in_=psPB[:, 0:512]))

        def wd_hook(hk):
            if hk == 17:
                wd_chain()
            if 1 <= hk <= 31:
                self.wait("vector", "dS", c["dS"].n)
                for n_ in range(3 * hk - 3, min(3 * hk, 92)):
                    wd_comp(n_)
            for n_ in range(3 * hk, min(3 * hk + 3, 92)):
                wd_load(n_)

        self.wait("tensor", "s", s_xq)
        wq_slots = [WQT, TMPB]
        wq_free = [0, 0]
        wstr_free = [0, 0]
        ps_free = {0: 0, 1: 0}
        habs_started = False
        tp_last = tq_t_last
        sc5_safe = 0
        for ic in range(NI0):
            self.wait("sync", "v", wstr_free[0])
            mg = self.dma("sync", WSTR[:, 0:D], L0_wg_in[ic * 128:(ic + 1) * 128, :], "dS")
            self.wait("sync", "v", wstr_free[1])
            mu_ = self.dma("sync", WSTR[:, D:2 * D], L0_wu_in[ic * 128:(ic + 1) * 128, :], "dS")
            self.wait("vector", "dS", mu_)
            for which, wofs in ((0, 0), (1, D)):
                slot = wq_slots[which]
                self.wait("vector", "v", vbc if ic == 0 and which == 0 else 0)
                self.vop(lambda e, wofs=wofs: e.reduce_max(
                    out=SC[:, 3:4], in_=WSTR[:, wofs:wofs + D], axis=AX,
                    apply_absolute_value=True))
                self.recip(SC[:, 4:5], SC[:, 3:4], SC[:, 7:8])
                self.vop(lambda e: e.tensor_scalar_mul(out=SC[:, 4:5], in0=SC[:, 4:5], scalar1=127.0))
                if which == 0 and sc5_safe:
                    self.wait("vector", "s", sc5_safe)
                dqm = self.vop(lambda e, which=which: e.tensor_mul(
                    out=SC[:, 5 + which:6 + which], in0=SC[:, 3:4], in1=SC[:, 2:3]))
                if tp_last:
                    self.wait("vector", "t", tp_last)
                self.vop(lambda e, wofs=wofs: e.tensor_scalar(
                    out=TMPF[:, 0:D], in0=WSTR[:, wofs:wofs + D],
                    scalar1=SC[:, 4:5], scalar2=MAGIC, op0=ALU.mult, op1=ALU.add))
                wsf = self.vop(lambda e: e.tensor_scalar(
                    out=TMPF[:, D:2 * D], in0=TMPF[:, 0:D],
                    scalar1=MAGIC, scalar2=None, op0=ALU.subtract))
                wstr_free[which] = wsf
                self.wait("tensor", "v", wsf)
                self.wait("vector", "t", wq_free[which])
                ev_marks = {}
                for cp in range(NC):
                    pst = psX if cp % 2 == 0 else psY
                    if cp >= 2:
                        self.wait("tensor", "s", ev_marks[cp - 2])
                    tm = self.op("tensor", lambda e, cp=cp, pst=pst: e.transpose(
                        pst[:, 0:128], TMPF[:, D + cp * 128:D + (cp + 1) * 128], CONST[:]), "t")
                    tp_last = tm
                    self.wait("scalar", "t", tm)
                    wvm = self.op("scalar", lambda e, cp=cp, pst=pst, slot=slot: e.activation(
                        out=slot[:, cp * 128:(cp + 1) * 128], in_=pst[:, 0:128],
                        func=ACT.Identity), "s")
                    ev_marks[cp] = wvm
                self.wait("tensor", "s", wvm)
                for tb4 in range(4):
                    pst, pi = (psA, 0) if tb4 % 2 == 0 else (psB, 1)
                    po = (tb4 // 2) * 512
                    if ps_free[pi]:
                        eng, val = ps_free[pi]
                        self.wait("tensor", eng, val)
                    tmm = None
                    for cp in range(NC):
                        tmm = self.op("tensor", lambda e, cp=cp, tb4=tb4, pst=pst, po=po, slot=slot: e.matmul(
                            pst[:, po:po + 512], slot[:, cp * 128:(cp + 1) * 128],
                            XQT[:, cp * T + tb4 * 512:cp * T + (tb4 + 1) * 512],
                            start=(cp == 0), stop=(cp == NC - 1)),
                            tick=("t" if cp == NC - 1 else None))
                    wq_free[which] = tmm
                    col = ic * T + tb4 * 512
                    if which == 0:
                        self.wait("scalar", "t", tmm)
                        self.wait("scalar", "v", dqm)
                        smk = self.op("scalar", lambda e, col=col, pst=pst, po=po: e.activation(
                            out=HBUF[:, col:col + 512], in_=pst[:, po:po + 512],
                            func=ACT.Silu, scale=SC[:, 5:6]), "s")
                        ps_free[pi] = ("s", smk)
                        sc5_safe = smk
                    else:
                        self.wait("vector", "t", tmm)
                        vmk = self.vop(lambda e, col=col, pst=pst, po=po: e.scalar_tensor_tensor(
                            out=HBUF[:, col:col + 512], in0=pst[:, po:po + 512],
                            scalar=SC[:, 6:7], in1=HBUF[:, col:col + 512],
                            op0=ALU.mult, op1=ALU.mult))
                        ps_free[pi] = ("v", vmk)
                if which == 1:
                    self.vop(lambda e, ic=ic: e.reduce_max(
                        out=SM[:, 2:3], in_=HBUF[:, ic * T:(ic + 1) * T], axis=AX,
                        apply_absolute_value=True))
                    if not habs_started:
                        self.vop(lambda e: e.tensor_copy(out=SM[:, 3:4], in_=SM[:, 2:3]))
                        habs_started = True
                    else:
                        self.vop(lambda e: e.tensor_max(out=SM[:, 3:4], in0=SM[:, 3:4], in1=SM[:, 2:3]))
                wd_hook(2 * ic + which)

        vloc = self._part_reduce_max(SM[:, 3:4], ROW[0:1, 8:9], psX, CONST)
        self.wait("sync", "v", vloc)
        dm = self.dma("sync", cc_in2, ROW[0:1, 8:9], "dS")
        self.wait("gpsimd", "dS", dm)
        gm = self.op("gpsimd", lambda e: e.collective_compute(
            "AllReduce", ALU.max, replica_groups=getattr(self, "rg2", rg),
            ins=[cc_in2], outs=[cc_out2]), "g")
        self.wait("sync", "g", gm)
        dm = self.dma("sync", ROW[0:1, 9:10], cc_out2, "dS")
        dmc = self.dma("sync", ROW[0:1, 10:11], L0_clipd, "dS")
        self.wait("scalar", "dS", dmc)
        sm = self.op("scalar", lambda e: e.activation(
            out=ROW[0:1, 11:12], in_=ROW[0:1, 10:11], func=ACT.Exp, scale=-1.0), "s")
        self.wait("vector", "s", sm)
        self.wait("vector", "dS", dmc)
        self.vop(lambda e: e.tensor_scalar_add(out=ROW[0:1, 11:12], in0=ROW[0:1, 11:12], scalar1=1.0))
        self.vop(lambda e: e.reciprocal(out=ROW[0:1, 12:13], in_=ROW[0:1, 11:12]))
        self.vop(lambda e: e.tensor_mul(out=ROW[0:1, 13:14], in0=ROW[0:1, 12:13], in1=ROW[0:1, 9:10]))
        self.vop(lambda e: e.tensor_scalar_mul(out=ROW[0:1, 13:14], in0=ROW[0:1, 13:14], scalar1=1.0 / 127.0))
        self.recip(ROW[0:1, 14:15], ROW[0:1, 13:14], ROW[0:1, 30:31])
        self.vop(lambda e: e.tensor_scalar_mul(out=ROW[0:1, 15:16], in0=ROW[0:1, 13:14], scalar1=1.0 / 127.0))
        vbc2 = self._scalar_bcast(ROW[0:1, 13:16], SC[:, 3:6], psY, ONESF, width=3)

        vmg = self.vop(lambda e: e.memset(SM[:, 7:8], MAGIC))
        self.wait("scalar", "v", vmg)
        slot_free = {0: 0, 1: 0}
        for n_ in range(2 * NI0):
            ic, half = divmod(n_, 2)
            col = ic * T + half * D
            sl = n_ % 2
            if slot_free[sl]:
                self.wait("scalar", "v", slot_free[sl])
            sm_ = self.op("scalar", lambda e, col=col, sl=sl: e.activation(
                out=TMPF[:, sl * D:(sl + 1) * D], in_=HBUF[:, col:col + D],
                func=ACT.Identity, bias=SM[:, 7:8], scale=SC[:, 4:5]), "s")
            self.wait("vector", "s", sm_)
            self.vop(lambda e, sl=sl: e.tensor_scalar(
                out=TMP2[:], in0=TMPF[:, sl * D:(sl + 1) * D],
                scalar1=MAGIC, scalar2=-128.0, op0=ALU.subtract, op1=ALU.max))
            slot_free[sl] = self.vop(lambda e, col=col: e.tensor_scalar_min(
                out=HBUF[:, col:col + D], in0=TMP2[:], scalar1=127.0))
        v_hq = c["v"].n

        wstr_m = wd["done"]
        self.wait("tensor", "v", wstr_m)
        vv = 0
        for dc in range(NC):
            tm = self.op("tensor", lambda e, dc=dc: e.transpose(
                psX[0:1, 0:128], SM[:, 8 + dc:9 + dc], CONST[:]), "t")
            self.wait("vector", "t", tm)
            vv = self.vop(lambda e, dc=dc: e.tensor_copy(
                out=TMP2[0:1, dc * 128:(dc + 1) * 128], in_=psX[0:1, 0:128]))
            self.wait("tensor", "v", vv)
        for half in range(2):
            tm = self.op("tensor", lambda e, half=half: e.matmul(
                psY[:, 0:512], ONESF[0:1, 0:128], TMP2[0:1, half * 512:(half + 1) * 512],
                start=True, stop=True), "t")
            self.wait("vector", "t", tm)
            vv = self.vop(lambda e, half=half: e.tensor_copy(
                out=SWDB[:, half * 512:(half + 1) * 512], in_=psY[:, 0:512]))
            self.wait("tensor", "v", vv)
        vsw = self.vop(lambda e: e.tensor_scalar_mul(
            out=SWDB[:], in0=SWDB[:], scalar1=SC[:, 5:6]))

        xr_m = 0
        ps_free2 = {0: 0, 1: 0}
        self.x2_tiles = []
        for tk in range(NT):
            pst, pi = (psA, 0) if tk % 2 == 0 else (psB, 1)
            if ps_free2[pi]:
                self.wait("tensor", "v", ps_free2[pi])
            if tk == 0:
                self.wait("tensor", "v", max(v_hq, vsw, wd["done"]))
            tmm = None
            for ic in range(NI0):
                for half in range(2):
                    tmm = self.op("tensor", lambda e, ic=ic, tk=tk, half=half, pst=pst: e.matmul(
                        pst[:, half * 512:(half + 1) * 512],
                        HBUF[:, ic * T + tk * 128:ic * T + tk * 128 + 128],
                        WDQT[:, ic * D + half * 512:ic * D + (half + 1) * 512],
                        start=(ic == 0), stop=(ic == NI0 - 1)),
                        tick=("t" if (ic == NI0 - 1 and half == 1) else None))
            if tk > 0:
                self.wait("vector", "dS", self._x2_park_last)
            if tk == 0:
                self.wait("sync", "v", vsw)
            if xr_m:
                self.wait("sync", "v", xr_m)
            dm = self.dma("sync", TMPF[:, 0:D], xpark[tk * 128:(tk + 1) * 128, :], "dS")
            self.wait("vector", "t", tmm)
            self.vop(lambda e, pst=pst: e.tensor_mul(
                out=TMP2[:], in0=pst[:, 0:D], in1=SWDB[:]))
            self.wait("vector", "dS", dm)
            xr_m = self.vop(lambda e: e.tensor_add(
                out=TMPF[:, D:2 * D], in0=TMP2[:], in1=TMPF[:, 0:D]))
            ps_free2[pi] = xr_m
            self._layernorm_X(lambda i: TMPF[:, D:2 * D], [0], WSTR[:, 0:D])
            vln = c["v"].n
            self.wait("sync", "v", vln)
            self._x2_park_last = self.dma("sync", xpark[tk * 128:(tk + 1) * 128, :], TMPF[:, D:2 * D], "dS")
        self.x2_parked = c["dS"].n

    def _rope(self, view_of, cview, sview, t1, t2, t3, t4, inverse=False):
        e_, o_ = view_of(0), view_of(1)
        self.vop(lambda e: e.tensor_mul(out=t1, in0=o_, in1=sview))
        self.vop(lambda e: e.tensor_mul(out=t2, in0=o_, in1=cview))
        self.vop(lambda e: e.tensor_mul(out=t3, in0=e_, in1=sview))
        self.vop(lambda e: e.tensor_mul(out=t4, in0=e_, in1=cview))
        if not inverse:
            self.vop(lambda e: e.tensor_add(out=o_, in0=t3, in1=t2))
            return self.vop(lambda e: e.tensor_sub(out=e_, in0=t4, in1=t1))
        else:
            self.vop(lambda e: e.tensor_sub(out=e_, in0=t1, in1=t4))
            return self.vop(lambda e: e.scalar_tensor_tensor(
                out=o_, in0=t3, scalar=-1.0, in1=t2, op0=ALU.mult, op1=ALU.subtract))

    def _rope_io(self, in_of, out_of, cview, sview, t1, t2, t3, t4):
        ei, oi = in_of(0), in_of(1)
        eo, oo = out_of(0), out_of(1)
        self.vop(lambda e: e.tensor_mul(out=t1, in0=oi, in1=sview))
        self.vop(lambda e: e.tensor_mul(out=t2, in0=oi, in1=cview))
        self.vop(lambda e: e.tensor_mul(out=t3, in0=ei, in1=sview))
        m4 = self.vop(lambda e: e.tensor_mul(out=t4, in0=ei, in1=cview))
        self.vop(lambda e: e.tensor_add(out=oo, in0=t3, in1=t2))
        done = self.vop(lambda e: e.tensor_sub(out=eo, in0=t4, in1=t1))
        return m4, done

    def _rope_out(self, ybase, cbase, sbase, t1, t2, t3, t4):
        yp = ybase.rearrange("p (x two) -> p x two", two=2)
        av, bv = yp[:, :, 0:1], yp[:, :, 1:2]
        r3 = lambda t: t.rearrange("p (x one) -> p x one", one=1)
        c3, s3 = r3(cbase), r3(sbase)
        self.vop(lambda e: e.tensor_mul(out=r3(t1), in0=bv, in1=s3))
        self.vop(lambda e: e.tensor_mul(out=r3(t2), in0=bv, in1=c3))
        self.vop(lambda e: e.tensor_mul(out=r3(t3), in0=av, in1=s3))
        self.vop(lambda e: e.tensor_mul(out=r3(t4), in0=av, in1=c3))
        yh = ybase.rearrange("p (h d) -> p h d", d=HD)
        th = lambda t: t.rearrange("p (h d) -> p h d", d=32)
        self.vop(lambda e: e.tensor_sub(out=yh[:, :, 0:32], in0=th(t1), in1=th(t4)))
        return self.vop(lambda e: e.scalar_tensor_tensor(
            out=yh[:, :, 32:64], in0=th(t3), scalar=-1.0, in1=th(t2),
            op0=ALU.mult, op1=ALU.subtract))

    def _l1_attn(self):
        t, c = self.t, self.c
        XT, KF, VT, QSTG, XSTG, WRES, WORES, TMPB, ROPT = (t[k] for k in
            ("XT", "KF", "VT", "QSTG", "XSTG", "WRES", "WORES", "TMPB", "ROPT"))
        ROPEC, ROPES, BROWS, CONST, ONESB, TMP2 = (t[k] for k in
            ("ROPEC", "ROPES", "BROWS", "CONST", "ONESB", "TMP2"))
        psA, psB, psX, psY = (self.p[k] for k in ("psA", "psB", "psX", "psY"))
        L1_WT, L1_brow = t["L1_WT"], t["L1_brow"]
        xpark = self.dram["xpark"].ap()
        qpark = self.dram["qpark"].ap()

        self.wait("tensor", "s", c["s"].n)
        self.wait("tensor", "v", c["v"].n)
        self.wait("gpsimd", "dS", self.x2_parked)
        for i in range(NT):
            self.dma("gpsimd", ROPEC[:, i * 512:(i + 1) * 512],
                     t["ropeC_in"][i * 128:(i + 1) * 128, :], "dG")
            self.dma("gpsimd", ROPES[:, i * 512:(i + 1) * 512],
                     t["ropeS_in"][i * 128:(i + 1) * 128, :], "dG")
        for w_i, knm in enumerate("qkvo"):
            self.dma("gpsimd", BROWS[0:1, w_i * D:(w_i + 1) * D], L1_brow[knm], "dG")
        g_tabs = c["dG"].n

        xs_free = {0: 0, 1: 0}
        ev_marks = {}
        for i in range(NT):
            sl = i % 2
            if xs_free[sl]:
                self.wait("sync", "t", xs_free[sl])
            dm = self.dma("sync", XSTG[:, sl * D:(sl + 1) * D],
                          xpark[i * 128:(i + 1) * 128, :], "dS")
            self.wait("tensor", "dS", dm)
            for cp in range(NC):
                kk = i * NC + cp
                pst = psX if kk % 2 == 0 else psY
                if ev_marks.get(kk - 2):
                    self.wait("tensor", "s", ev_marks[kk - 2])
                tm = self.op("tensor", lambda e, sl=sl, cp=cp, pst=pst: e.transpose(
                    pst[:, 0:128], XSTG[:, sl * D + cp * 128:sl * D + cp * 128 + 128],
                    CONST[:]), "t")
                self.wait("scalar", "t", tm)
                ev_marks[kk] = self.op("scalar", lambda e, i=i, cp=cp, pst=pst: e.activation(
                    out=XT[:, cp * T + i * 128:cp * T + i * 128 + 128],
                    in_=pst[:, 0:128], func=ACT.Identity), "s")
            xs_free[sl] = tm
        v_xt = c["s"].n

        self.wait("tensor", "s", v_xt)
        ps_ev = {0: 0, 1: 0}
        self._l1_last_dv = 0
        self._l1_last_park = 0
        for w_i, knm in enumerate("qkv"):
            self.wait("gpsimd", "dG", g_tabs)
            if w_i > 0:
                self.wait("gpsimd", "t", self._wres_used)
            for cp in range(NC):
                self.dma("gpsimd", WRES[:, cp * D:(cp + 1) * D],
                         L1_WT[knm][cp * 128:(cp + 1) * 128, :], "dG")
            gw = c["dG"].n
            self.wait("tensor", "dG", gw)
            for tk in range(NT):
                pst, pi = (psA, 0) if tk % 2 == 0 else (psB, 1)
                if ps_ev[pi]:
                    self.wait("tensor", "v", ps_ev[pi])
                for cp in range(NC):
                    self.op("tensor", lambda e, cp=cp, tk=tk, pst=pst: e.matmul(
                        pst[:, 0:512], XT[:, cp * T + tk * 128:cp * T + tk * 128 + 128],
                        WRES[:, cp * D:cp * D + 512], start=(cp == 0), stop=False), None)
                    self.op("tensor", lambda e, cp=cp, tk=tk, pst=pst: e.matmul(
                        pst[:, 512:1024], XT[:, cp * T + tk * 128:cp * T + tk * 128 + 128],
                        WRES[:, cp * D + 512:(cp + 1) * D], start=(cp == 0), stop=False), None)
                self.op("tensor", lambda e, pst=pst, w_i=w_i: e.matmul(
                    pst[:, 0:512], ONESB[0:1, 0:128], BROWS[0:1, w_i * D:w_i * D + 512],
                    start=False, stop=True), None)
                tm = self.op("tensor", lambda e, pst=pst, w_i=w_i: e.matmul(
                    pst[:, 512:1024], ONESB[0:1, 0:128],
                    BROWS[0:1, w_i * D + 512:(w_i + 1) * D],
                    start=False, stop=True), "t")
                self._wres_used = tm
                self.wait("vector", "t", tm)
                pv_h = pst[:, 0:1024].rearrange("p (h d) -> p h d", h=NH)
                cv = ROPEC[:, tk * 512:(tk + 1) * 512].rearrange("p (h d) -> p h d", d=32)
                sv = ROPES[:, tk * 512:(tk + 1) * 512].rearrange("p (h d) -> p h d", d=32)
                rt = (ROPT[:, 0:512].rearrange("p (h d) -> p h d", d=32),
                      ROPT[:, 512:1024].rearrange("p (h d) -> p h d", d=32),
                      TMPB[:, 0:512].rearrange("p (h d) -> p h d", d=32),
                      TMPB[:, 512:1024].rearrange("p (h d) -> p h d", d=32))
                if knm == "v":
                    vt_tile = VT[:, tk * NH * 65:(tk + 1) * NH * 65].rearrange(
                        "p (h d) -> p h d", h=NH)
                    self.vop(lambda e, vt=vt_tile: e.memset(vt[:, :, 64:65], 1.0))
                    if getattr(self, "old_rope", False):
                        self.vop(lambda e, pst=pst, vt=vt_tile: e.tensor_copy(
                            out=vt[:, :, 0:64], in_=pst[:, 0:1024].rearrange(
                                "p (h d) -> p h d", h=NH)))
                        ps_ev[pi] = self._rope(
                            lambda half, vt=vt_tile: vt[:, :, half * 32:(half + 1) * 32],
                            cv, sv, *rt)
                    else:
                        ps_ev[pi], _ = self._rope_io(
                            lambda half, pv=pv_h: pv[:, :, half * 32:(half + 1) * 32],
                            lambda half, vt=vt_tile: vt[:, :, half * 32:(half + 1) * 32],
                            cv, sv, *rt)
                else:
                    if self._l1_last_dv:
                        self.wait("vector", "dC", self._l1_last_dv)
                        if self.q2[0] != "vector":
                            self.wait("vector", self.q2[1], c[self.q2[1]].n)
                    qv = QSTG[:, 0:D].rearrange("p (h d) -> p h d", h=NH)
                    if getattr(self, "old_rope", False):
                        ev = self.vop(lambda e, pst=pst: e.tensor_copy(
                            out=QSTG[:, 0:D], in_=pst[:, 0:1024]))
                        rm = self._rope(
                            lambda half, qv=qv: qv[:, :, half * 32:(half + 1) * 32],
                            cv, sv, *rt)
                        ps_ev[pi] = ev
                    else:
                        ps_ev[pi], rm = self._rope_io(
                            lambda half, pv=pv_h: pv[:, :, half * 32:(half + 1) * 32],
                            lambda half, qv=qv: qv[:, :, half * 32:(half + 1) * 32],
                            cv, sv, *rt)
                    if knm == "k":
                        self.wait("scalar", "v", rm)
                        kfv = KF.rearrange("p (c t) -> p c t", c=NC)[
                            :, :, tk * 128:(tk + 1) * 128]
                        self.dmat("scalar", kfv, QSTG[:, 0:D], "dC")
                        self._l1_last_dv = c["dC"].n
                    else:
                        self.wait("scalar", "v", rm)
                        if self._l1_last_park:
                            self.wait("scalar", "dS", self._l1_last_park)
                        self.dmat("scalar",
                                  QSTG[:, D:2 * D].rearrange("p (c f) -> p c f", c=NC),
                                  QSTG[:, 0:D], "dC")
                        self._l1_last_dv = c["dC"].n
                        self.wait("sync", "dC", c["dC"].n)
                        for cp in range(NC):
                            self._l1_last_park = self.dma(
                                "sync",
                                qpark[cp * 128:(cp + 1) * 128, tk * 128:(tk + 1) * 128],
                                QSTG[:, D + cp * 128:D + (cp + 1) * 128], "dS")
        d_qpark = c["dS"].n
        t_proj = c["t"].n
        v_proj = c["v"].n

        self.wait("gpsimd", "t", t_proj)
        for cp in range(NC):
            self.dma("gpsimd", WORES[:, cp * D:(cp + 1) * D],
                     L1_WT["o"][cp * 128:(cp + 1) * 128, :], "dG")
        self._g_wo = c["dG"].n

        self._attention(layer=1, start_after={
            "sync": [("dS", d_qpark), ("dC", c["dC"].n), ("t", t_proj)],
            "tensor": [("v", v_proj), ("s", c["s"].n), ("dC", c["dC"].n),
                       ("dS", d_qpark), (self.q2[1], c[self.q2[1]].n)]})
        self.x3_parked = self._x3_last_park

    def _l0_post_tile(self, b, j):
        t, c = self.t, self.c
        X, TMP2, SM, MUX = t["X"], t["TMP2"], t["SM"], t["MUX"]
        ti = 4 * b + j
        xpark = self.dram["xpark"].ap()
        self.wait("vector", "v", c["v"].n)
        self._layernorm_X(lambda i: X[:, ti * D:(ti + 1) * D], [0], TMP2,
                          mu_of=lambda i, j=j: MUX[:, 4 * j:4 * j + 4])
        ln_m = self._ln_done[0]
        self.wait("vector", "v", ln_m)
        self.vop(lambda e, ti=ti: e.reduce_max(
            out=SM[:, 0:1], in_=X[:, ti * D:(ti + 1) * D], axis=AX,
            apply_absolute_value=True))
        if ti == 0:
            self.vop(lambda e: e.tensor_copy(out=SM[:, 1:2], in_=SM[:, 0:1]))
        else:
            self.vop(lambda e: e.tensor_max(
                out=SM[:, 1:2], in0=SM[:, 1:2], in1=SM[:, 0:1]))
        self.wait("sync", "v", ln_m)
        self._l0_park = self.dma(
            "sync", xpark[ti * 128:(ti + 1) * 128, :], X[:, ti * D:(ti + 1) * D],
            "dS")

    def _l1_post_stage1(self, b, j):
        t, c = self.t, self.c
        YTM, XPOSE, XPO4, ROPT, TMPB = (t[k] for k in
            ("YTM", "XPOSE", "XPO4", "ROPT", "TMPB"))
        ROPEC, ROPES = t["ROPEC"], t["ROPES"]
        tk = 4 * b + j
        self.wait("vector", "v", self.c["v"].n)
        rope_done = self._rope_out(
            YTM[:, j * D:(j + 1) * D],
            ROPEC[:, tk * 512:(tk + 1) * 512],
            ROPES[:, tk * 512:(tk + 1) * 512],
            ROPT[:, 0:512], ROPT[:, 512:1024],
            TMPB[:, 0:512], TMPB[:, 512:1024])
        xpo = XPO4[:, j * D:(j + 1) * D] if j < 3 else XPOSE[:, 0:D]
        self.wait("scalar", "v", rope_done)
        if b > 0:
            self.wait("scalar", "t", self._post_wo[j])
        self.dmat("scalar", xpo.rearrange("p (c f) -> p c f", c=NC),
                  YTM[:, j * D:(j + 1) * D], "dC")
        self._post_dm[j] = (c["dC"].n, c[self.q2[1]].n)

    def _l1_block_post(self, b):
        t, c = self.t, self.c
        XPOSE, XPO4, WORES, XSTG, YS2, TMP2 = (t[k] for k in
            ("XPOSE", "XPO4", "WORES", "XSTG", "YS2", "TMP2"))
        ONESB, BROWS = t["ONESB"], t["BROWS"]
        psA, psB = self.p["psA"], self.p["psB"]
        xpark = self.dram["xpark"].ap()
        bufs = [XSTG[:, 0:D], XSTG[:, D:2 * D], YS2[:, 0:D], YS2[:, D:2 * D]]

        if b == 0:
            self.wait("tensor", "dG", self._g_wo)
        dmc, dms = self._post_dm[3]
        self.wait("tensor", "dC", dmc)
        self.wait("tensor", self.q2[1], dms)
        self.wait("sync", "dC", dmc)
        self.wait("sync", self.q2[1], dms)
        if self._post_park[3]:
            self.wait("sync", "dP", c["dP"].n)
        rld = []
        for j in range(4):
            tk = 4 * b + j
            rld.append(self.dma("sync", bufs[j],
                                xpark[tk * 128:(tk + 1) * 128, :], "dS"))
        for j in range(4):
            xpo = XPO4[:, j * D:(j + 1) * D] if j < 3 else XPOSE[:, 0:D]
            pst, pi = (psA, 0) if j % 2 == 0 else (psB, 1)
            if self._post_ps_free[pi]:
                self.wait("tensor", "v", self._post_ps_free[pi])
            for cp in range(NC):
                self.op("tensor", lambda e, cp=cp, pst=pst, xpo=xpo: e.matmul(
                    pst[:, 0:512], xpo[:, cp * 128:(cp + 1) * 128],
                    WORES[:, cp * D:cp * D + 512], start=(cp == 0), stop=False), None)
                self.op("tensor", lambda e, cp=cp, pst=pst, xpo=xpo: e.matmul(
                    pst[:, 512:1024], xpo[:, cp * 128:(cp + 1) * 128],
                    WORES[:, cp * D + 512:(cp + 1) * D], start=(cp == 0), stop=False), None)
            self.op("tensor", lambda e, pst=pst: e.matmul(
                pst[:, 0:512], ONESB[0:1, 0:128], BROWS[0:1, 3 * D:3 * D + 512],
                start=False, stop=True), None)
            tm = self.op("tensor", lambda e, pst=pst: e.matmul(
                pst[:, 512:1024], ONESB[0:1, 0:128], BROWS[0:1, 3 * D + 512:4 * D],
                start=False, stop=True), "t")
            self._post_wo[j] = tm
            self.wait("vector", "t", tm)
            if j == 0:
                self.wait("vector", "dS", rld[3])
            self._post_ps_free[pi] = self.vop(lambda e, pst=pst, j=j: e.tensor_add(
                out=bufs[j], in0=pst[:, 0:D], in1=bufs[j]))
        MUX = t["MUX"]
        self._layernorm_X(lambda j: bufs[j], range(4), TMP2,
                          mu_of=lambda j: MUX[:, 4 * j:4 * j + 4])
        for j in range(4):
            tk = 4 * b + j
            self.wait("sync", "v", self._ln_done[j])
            self._post_park[j] = self.dma(
                "sync", xpark[tk * 128:(tk + 1) * 128, :], bufs[j], "dP")
        self._x3_last_park = self._post_park[3]
        self.wait("vector", "dC", self.c["dC"].n)
        if self.q2[0] != "vector":
            self.wait("vector", self.q2[1], self.c[self.q2[1]].n)
        self.wait("vector", "dP", self._post_park[3])

    def _l1_ffn(self):
        t, c = self.t, self.c
        W2QT, XQ1T, H1S, ABSA, HTOK, RTOK, W1C = (t[k] for k in
            ("W2QT", "XQ1T", "H1S", "ABSA", "HTOK", "RTOK", "W1C"))
        CWCOL, CBCOL, TMP2, TMPB, SM, ROW, SC, CONST, ONESF, MU = (t[k] for k in
            ("CWCOL", "CBCOL", "TMP2", "TMPB", "SM", "ROW", "SC", "CONST", "ONESF", "MU"))
        psA, psB, psX, psY = (self.p[k] for k in ("psA", "psB", "psX", "psY"))
        w1T_in, w2T_in = t["L1_w1T_in"], t["L1_w2T_in"]
        xpark = self.dram["xpark"].ap()
        w1q_park = self.dram["w1q_park"].ap()
        w2q_park = self.dram["w2q_park"].ap()
        y_out = t["y_out"]
        XS2 = self.sb("XS2", [128, 2 * D], F32, 186 * 1024)

        self.wait("tensor", "s", c["s"].n)
        self.wait("tensor", "v", c["v"].n)
        self.wait("sync", "v", c["v"].n)
        self.wait("sync", "t", c["t"].n)
        self.wait("sync", "s", c["s"].n)
        dcw = self.dma("sync", CWCOL[:], t["L1_cw_in"], "dS")
        dcb = self.dma("sync", CBCOL[:], t["L1_cb_in"], "dS")
        WSC = t["WSC"]
        self.wait("sync", "dP", c["dP"].n)

        self.wait("gpsimd", "v", c["v"].n)
        self.wait("gpsimd", "s", c["s"].n)
        self.wait("gpsimd", "t", c["t"].n)
        self.wait("gpsimd", "dS", c["dS"].n)
        self.wait("gpsimd", "dP", c["dP"].n)
        for ch in range(NH1):
            self.dma("gpsimd", W2QT[:, ch * D:(ch + 1) * D],
                     w2q_park[ch * 128:(ch + 1) * 128, :], "dG")
        g_w2 = c["dG"].n

        self.wait("sync", "v", c["v"].n)
        tq_free = {0: 0, 1: 0}
        tq_t_last = 0
        xs_rd = {0: 0, 1: 0}
        for i in range(NT):
            sl = i % 2
            if xs_rd[sl]:
                self.wait("sync", "v", xs_rd[sl])
            dm = self.dma("sync", XS2[:, sl * D:(sl + 1) * D],
                          xpark[i * 128:(i + 1) * 128, :], "dS")
            self.wait("vector", "dS", dm)
            self.vop(lambda e, sl=sl: e.reduce_max(
                out=SM[:, 4:5], in_=XS2[:, sl * D:(sl + 1) * D], axis=AX,
                apply_absolute_value=True))
            self.vop(lambda e: e.tensor_scalar_max(
                out=SM[:, 4:5], in0=SM[:, 4:5], scalar1=1e-5))
            self.recip(SC[:, 2:3], SM[:, 4:5], SC[:, 7:8])
            self.vop(lambda e: e.tensor_scalar_mul(
                out=SC[:, 2:3], in0=SC[:, 2:3], scalar1=127.0))
            self.vop(lambda e, i=i: e.tensor_scalar_mul(
                out=SM[:, 16 + i:17 + i], in0=SM[:, 4:5], scalar1=1.0 / 127.0))
            self.vop(lambda e, i=i: e.tensor_mul(
                out=SM[:, 16 + i:17 + i], in0=SM[:, 16 + i:17 + i], in1=WSC[:, 2:3]))
            if tq_t_last:
                self.wait("vector", "t", tq_t_last)
            self.vop(lambda e, sl=sl: e.tensor_scalar(
                out=TMP2[:], in0=XS2[:, sl * D:(sl + 1) * D],
                scalar1=SC[:, 2:3], scalar2=MAGIC, op0=ALU.mult, op1=ALU.add))
            vq = self.vop(lambda e: e.tensor_scalar(
                out=TMP2[:], in0=TMP2[:], scalar1=MAGIC, scalar2=None,
                op0=ALU.subtract))
            xs_rd[sl] = vq
            self.wait("tensor", "v", vq)
            for cp in range(NC):
                pst = psX if cp % 2 == 0 else psY
                self.wait("tensor", "s", tq_free[cp % 2])
                tm = self.op("tensor", lambda e, cp=cp, pst=pst: e.transpose(
                    pst[:, 0:128], TMP2[:, cp * 128:(cp + 1) * 128], CONST[:]), "t")
                tq_t_last = tm
                self.wait("scalar", "t", tm)
                tq_free[cp % 2] = self.op("scalar", lambda e, i=i, cp=cp, pst=pst: e.activation(
                    out=XQ1T[:, cp * T + i * 128:cp * T + i * 128 + 128],
                    in_=pst[:, 0:128], func=ACT.Identity), "s")
        rt_marks = 0
        self.wait("tensor", "s", c["s"].n)
        for i in range(NT):
            self.wait("tensor", "v", c["v"].n)
            tm = self.op("tensor", lambda e, i=i: e.transpose(
                psX[0:1, 0:128], SM[:, 16 + i:17 + i], CONST[:]), "t")
            self.wait("vector", "t", tm)
            vv = self.vop(lambda e: e.tensor_copy(out=TMP2[0:1, 0:128], in_=psX[0:1, 0:128]))
            self.wait("tensor", "v", vv)
            tm2 = self.op("tensor", lambda e: e.matmul(
                psY[:, 0:128], ONESF[0:1, 0:128], TMP2[0:1, 0:128],
                start=True, stop=True), "t")
            self.wait("vector", "t", tm2)
            rt_marks = self.vop(lambda e, i=i: e.tensor_copy(
                out=RTOK[:, i * 128:(i + 1) * 128], in_=psY[:, 0:128]))

        self.wait("vector", "dS", c["dS"].n)
        self.wait("scalar", "dS", c["dS"].n)

        self.y_stored = 0
        w1c_free = {0: 0, 1: 0}
        ps_done = 0
        for tau in range(2):
            if tau == 1:
                self.wait("vector", "out", self._y_park_last)
                self.wait("vector", "dS", c["dS"].n)
            habs_first = True
            for j in range(NH1):
                sl = j % 2
                if w1c_free[sl]:
                    self.wait("sync", "t", w1c_free[sl])
                for cp in range(NC):
                    self.dma("sync", W1C[:, sl * D + cp * 128:sl * D + (cp + 1) * 128],
                             w1q_park[cp * 128:(cp + 1) * 128, j * 128:(j + 1) * 128], "dS")
                dmw = c["dS"].n
                self.wait("tensor", "dS", dmw)
                if j == 0:
                    self.wait("tensor", "v", c["v"].n)
                    self.wait("tensor", "s", c["s"].n)
                pst, pi = (psA, 0) if j % 2 == 0 else (psB, 1)
                if ps_done:
                    self.wait("tensor", "v", ps_done)
                tmm = None
                for cp in range(NC):
                    for half in range(2):
                        tmm = self.op("tensor", lambda e, cp=cp, half=half, tau=tau, sl=sl, pst=pst: e.matmul(
                            pst[:, half * 512:(half + 1) * 512],
                            W1C[:, sl * D + cp * 128:sl * D + (cp + 1) * 128],
                            XQ1T[:, cp * T + tau * 1024 + half * 512:cp * T + tau * 1024 + (half + 1) * 512],
                            start=(cp == 0), stop=(cp == NC - 1)),
                            tick=("t" if (cp == NC - 1 and half == 1) else None))
                halo_tok = tau * 1024 - 1 if tau == 1 else 1024
                tmh = None
                for cp in range(NC):
                    tmh = self.op("tensor", lambda e, cp=cp, sl=sl, ht=halo_tok: e.matmul(
                        psX[:, 0:1],
                        W1C[:, sl * D + cp * 128:sl * D + (cp + 1) * 128],
                        XQ1T[:, cp * T + ht:cp * T + ht + 1],
                        start=(cp == 0), stop=(cp == NC - 1)),
                        tick=("t" if cp == NC - 1 else None))
                w1c_free[sl] = tmh
                self.wait("vector", "t", tmh)
                hrow = j * 1026
                self.vop(lambda e, hrow=hrow, tau=tau, pst=pst: e.tensor_mul(
                    out=H1S[:, hrow + 1:hrow + 1025], in0=pst[:, 0:1024],
                    in1=RTOK[:, tau * 1024:(tau + 1) * 1024]))
                hcol = hrow + (0 if tau == 1 else 1025)
                self.vop(lambda e, hcol=hcol, ht=halo_tok: e.tensor_mul(
                    out=H1S[:, hcol:hcol + 1], in0=psX[:, 0:1],
                    in1=RTOK[:, ht:ht + 1]))
                zcol = hrow + (1025 if tau == 1 else 0)
                ps_done = self.vop(lambda e, zcol=zcol: e.memset(
                    H1S[:, zcol:zcol + 1], 0.0))
                self.vop(lambda e, hrow=hrow, j=j: e.tensor_scalar_mul(
                    out=TMPB[:], in0=H1S[:, hrow:hrow + 1024],
                    scalar1=CWCOL[:, j * 3:j * 3 + 1]))
                self.vop(lambda e, hrow=hrow, j=j: e.scalar_tensor_tensor(
                    out=TMPB[:], in0=H1S[:, hrow + 1:hrow + 1025],
                    scalar=CWCOL[:, j * 3 + 1:j * 3 + 2], in1=TMPB[:],
                    op0=ALU.mult, op1=ALU.add))
                vcv = self.vop(lambda e, hrow=hrow, j=j: e.scalar_tensor_tensor(
                    out=TMPB[:], in0=H1S[:, hrow + 2:hrow + 1026],
                    scalar=CWCOL[:, j * 3 + 2:j * 3 + 3], in1=TMPB[:],
                    op0=ALU.mult, op1=ALU.add))
                self.wait("scalar", "v", vcv)
                smk = self.op("scalar", lambda e, hrow=hrow, j=j: e.activation(
                    out=H1S[:, hrow + 1:hrow + 1025], in_=TMPB[:], func=ACT.Silu,
                    bias=CBCOL[:, j:j + 1], scale=1.0), "s")
                self.wait("vector", "s", smk)
                if habs_first:
                    self.vop(lambda e, hrow=hrow: e.tensor_copy(
                        out=ABSA[:], in_=H1S[:, hrow + 1:hrow + 1025]))
                    self.vop(lambda e, hrow=hrow: e.tensor_copy(
                        out=HTOK[:], in_=H1S[:, hrow + 1:hrow + 1025]))
                    habs_first = False
                else:
                    self.vop(lambda e, hrow=hrow: e.tensor_max(
                        out=ABSA[:], in0=ABSA[:], in1=H1S[:, hrow + 1:hrow + 1025]))
                    self.vop(lambda e, hrow=hrow: e.tensor_tensor(
                        out=HTOK[:], in0=HTOK[:], in1=H1S[:, hrow + 1:hrow + 1025],
                        op=ALU.min))
            self.vop(lambda e: e.tensor_scalar_mul(out=HTOK[:], in0=HTOK[:], scalar1=-1.0))
            self.vop(lambda e: e.tensor_max(out=ABSA[:], in0=ABSA[:], in1=HTOK[:]))
            for cc8 in range(NC):
                self.wait("tensor", "v", c["v"].n)
                tm = self.op("tensor", lambda e, cc8=cc8: e.transpose(
                    psX[:, 0:128], ABSA[:, cc8 * 128:(cc8 + 1) * 128], CONST[:]), "t")
                self.wait("vector", "t", tm)
                self.vop(lambda e: e.reduce_max(
                    out=SM[:, 5:6], in_=psX[:, 0:128], axis=AX))
                self.vop(lambda e: e.tensor_scalar_max(
                    out=SM[:, 5:6], in0=SM[:, 5:6], scalar1=1e-5))
                self.recip(SC[:, 3:4], SM[:, 5:6], SC[:, 7:8])
                self.vop(lambda e: e.tensor_scalar_mul(
                    out=SC[:, 3:4], in0=SC[:, 3:4], scalar1=127.0))
                self.vop(lambda e, cc8=cc8: e.tensor_scalar_mul(
                    out=SM[:, 24 + cc8:25 + cc8], in0=SM[:, 5:6], scalar1=1.0 / 127.0))
                self.vop(lambda e, cc8=cc8: e.tensor_mul(
                    out=SM[:, 24 + cc8:25 + cc8], in0=SM[:, 24 + cc8:25 + cc8],
                    in1=WSC[:, 3:4]))
                vv = self.vop(lambda e: e.tensor_copy(out=SM[:, 6:7], in_=SC[:, 3:4]))
                self.wait("tensor", "v", vv)
                tm2 = self.op("tensor", lambda e: e.transpose(
                    psX[0:1, 0:128], SM[:, 6:7], CONST[:]), "t")
                self.wait("vector", "t", tm2)
                vv = self.vop(lambda e: e.tensor_copy(out=TMP2[0:1, 0:128], in_=psX[0:1, 0:128]))
                self.wait("tensor", "v", vv)
                tm3 = self.op("tensor", lambda e: e.matmul(
                    psY[:, 0:128], ONESF[0:1, 0:128], TMP2[0:1, 0:128],
                    start=True, stop=True), "t")
                self.wait("vector", "t", tm3)
                self.vop(lambda e, cc8=cc8: e.tensor_copy(
                    out=HTOK[:, cc8 * 128:(cc8 + 1) * 128], in_=psY[:, 0:128]))
            hq_marks = {}
            for j in range(NH1):
                hrow = j * 1026
                self.vop(lambda e, hrow=hrow: e.tensor_mul(
                    out=TMP2[:], in0=H1S[:, hrow + 1:hrow + 1025], in1=HTOK[:]))
                hq_marks[j] = self.vop(lambda e, hrow=hrow: e.tensor_scalar(
                    out=H1S[:, hrow + 1:hrow + 1025], in0=TMP2[:],
                    scalar1=MAGIC, scalar2=MAGIC, op0=ALU.add, op1=ALU.subtract))
            v_hq = c["v"].n
            if tau == 0:
                self.wait("tensor", "dG", g_w2)
            xr_m = 0
            for lc in range(8):
                tk = tau * 8 + lc
                pst, pi = (psA, 0) if lc % 2 == 0 else (psB, 1)
                if ps_done:
                    self.wait("tensor", "v", ps_done)
                    ps_done = 0
                if getattr(self, "_w2_ps_free", {}).get(pi):
                    self.wait("tensor", "v", self._w2_ps_free[pi])
                tmm = None
                for j in range(NH1):
                    if lc == 0:
                        self.wait("tensor", "v", hq_marks[j])
                    for half in range(2):
                        tmm = self.op("tensor", lambda e, j=j, lc=lc, half=half, pst=pst: e.matmul(
                            pst[:, half * 512:(half + 1) * 512],
                            H1S[:, j * 1026 + 1 + lc * 128:j * 1026 + 1 + (lc + 1) * 128],
                            W2QT[:, j * D + half * 512:j * D + (half + 1) * 512],
                            start=(j == 0), stop=(j == NH1 - 1)),
                            tick=("t" if (j == NH1 - 1 and half == 1) else None))
                if xr_m:
                    self.wait("vector", "out", self._y_park_last)
                if xr_m:
                    self.wait("sync", "v", xr_m)
                if lc == 0:
                    self.wait("sync", "v", c["v"].n)
                dm = self.dma("sync", ABSA[:], xpark[tk * 128:(tk + 1) * 128, :], "dS")
                self.wait("vector", "t", tmm)
                self.wait("vector", "dS", dm)
                if not hasattr(self, "_w2_ps_free"):
                    self._w2_ps_free = {0: 0, 1: 0}
                xr_m = self.vop(lambda e, pst=pst, cc8=lc: e.scalar_tensor_tensor(
                    out=HTOK[:], in0=pst[:, 0:D],
                    scalar=SM[:, 24 + cc8:25 + cc8], in1=ABSA[:],
                    op0=ALU.mult, op1=ALU.add))
                self._w2_ps_free[pi] = xr_m
                self._layernorm_X(lambda i: HTOK[:], [0], TMP2)
                vln = c["v"].n
                self.wait("sync", "v", vln)
                self._y_park_last = self.dma(
                    "sync", y_out[tk * 128:(tk + 1) * 128, :], HTOK[:], "out")



def _host_consts():
    cst = {}
    cst["identity"] = np.eye(128, dtype=np.float32)
    cst["onesf"] = np.ones((1, 128), dtype=np.float32)
    cst["onesb"] = np.ones((1, 512), dtype=BF)
    cst["triu"] = np.triu(np.ones((128, 128))).astype(BF)
    pos = np.arange(T, dtype=np.float32)[:, None]
    dim_pair = np.arange(0, HD, 2, dtype=np.float32)[None, :]
    theta = np.float32(10000.0) ** (-dim_pair / np.float32(HD))
    freqs = (pos * theta).astype(np.float32)
    rope = np.concatenate([np.cos(freqs), np.sin(freqs)], axis=-1).astype(np.float32)
    cst["ropeC"] = np.ascontiguousarray(np.tile(rope[:, 0::2], (1, NH))).astype(np.float32)
    cst["ropeS"] = np.ascontiguousarray(np.tile(rope[:, 1::2], (1, NH))).astype(np.float32)
    return cst


def _prep_weights(inp):
    w = {}
    w["L0_WT"] = np.ascontiguousarray(np.asarray(inp["L0_attn_W"], np.float32).T)
    b = np.asarray(inp["L0_attn_b"], np.float32)
    w["L0_bcol"] = np.ascontiguousarray(b[:2048].reshape(16, 128).T)
    w["L0_bvrow"] = np.ascontiguousarray(b[2048:].reshape(1, D))
    w["L0_wg"] = np.asarray(inp["L0_ffn_w_gate"], np.float32)
    w["L0_wu"] = np.asarray(inp["L0_ffn_w_up"], np.float32)
    w["L0_wd"] = np.asarray(inp["L0_ffn_w_down"], np.float32)
    w["L0_wdT"] = np.ascontiguousarray(w["L0_wd"].T)
    w["L0_clipx"] = np.asarray(inp["L0_ffn_clip_x"], np.float32).reshape(1, 1)
    w["L0_clipd"] = np.asarray(inp["L0_ffn_clip_down"], np.float32).reshape(1, 1)
    pi = np.concatenate([np.arange(0, HD, 2), np.arange(1, HD, 2)])
    rowperm = np.concatenate([h * HD + pi for h in range(NH)])
    for nm, wn, bn in (("q", "L1_attn_Wq", "L1_attn_bq"), ("k", "L1_attn_Wk", "L1_attn_bk"),
                       ("v", "L1_attn_Wv", "L1_attn_bv")):
        W = np.asarray(inp[wn], np.float32)[rowperm, :]
        w[f"L1_W{nm}T"] = np.ascontiguousarray(W.T)
        w[f"L1_b{nm}row"] = np.asarray(inp[bn], np.float32)[rowperm].reshape(1, D)
    Wo = np.asarray(inp["L1_attn_Wo"], np.float32)
    w["L1_WoT"] = np.ascontiguousarray(Wo.T)
    w["L1_borow"] = np.asarray(inp["L1_attn_bo"], np.float32).reshape(1, D)
    w["L1_w1T"] = np.ascontiguousarray(np.asarray(inp["L1_ffn_w1"], np.float32).T)
    w["L1_w2T"] = np.ascontiguousarray(np.asarray(inp["L1_ffn_w2"], np.float32).T)
    cw = np.asarray(inp["L1_ffn_conv_w"], np.float32).reshape(H1, 3)
    w["L1_cw"] = np.ascontiguousarray(cw.reshape(NH1, 128, 3).transpose(1, 0, 2).reshape(128, NH1 * 3))
    w["L1_cb"] = np.ascontiguousarray(np.asarray(inp["L1_ffn_conv_b"], np.float32).reshape(NH1, 128).T)
    return w


def _build_in_maps(inputs):
    x = np.asarray(inputs["L0_attn_x"], np.float32)
    shared = {**_host_consts(), **_prep_weights(inputs)}
    return [dict(shared, x=np.ascontiguousarray(x[core])) for core in range(8)]


def kernel(**inputs):
    in_maps = _build_in_maps(inputs)
    nc = MK(tap=None).build()
    res = run_bass_kernel_spmd(nc, in_maps, core_ids=list(range(8)))
    return np.stack([np.asarray(r["y"], dtype=np.float32) for r in res.results], axis=0)


def run_debug(tap, inputs):
    in_maps = _build_in_maps(inputs)
    nc = MK(tap=tap).build()
    res = run_bass_kernel_spmd(nc, in_maps, core_ids=list(range(8)))
    return res.results


# version 19
# speedup vs baseline: 1.1773x; 1.0183x over previous
import numpy as np
import ml_dtypes
from contextlib import ExitStack
import concourse.bass as bass
import concourse.mybir as mybir
from concourse.bass_utils import run_bass_kernel_spmd

F32 = mybir.dt.float32
BF16 = mybir.dt.bfloat16
F16 = mybir.dt.float16
ALU = mybir.AluOpType
ACT = mybir.ActivationFunctionType
AX = mybir.AxisListType.X
MAGIC = 12582912.0
BF = ml_dtypes.bfloat16

T, D = 2048, 1024
NT, NC = 16, 8
NH, HD, NHP = 16, 64, 8
TB, NB = 512, 4
I0, NI0 = 2816, 22
H1, NH1 = 4096, 32
EPS = 1e-5
RG8 = [[0, 1, 2, 3, 4, 5, 6, 7]]


class Ctr:
    def __init__(self, stack, nc, name, step=1):
        self.sem = stack.enter_context(nc.semaphore(name))
        self.n = 0
        self.step = step
        self.maxwait = 0
        self.selfwait = 0
        self.sealed = 0


class MK:
    def __init__(self, tap=None):
        self.tap = tap
        self.q2 = ("sync", "dX")
        self.nc = bass.Bass("TRN2", target_bir_lowering=False)
        self.stack = ExitStack()
        self.q = {e: [] for e in ("sync", "vector", "scalar", "tensor", "gpsimd")}
        st, nc = self.stack, self.nc
        self.c = {k: Ctr(st, nc, k, 16 if (k.startswith("d") or k == "out") else 1)
                  for k in ("v", "s", "t", "g", "dS", "dG", "dV", "dT", "dC",
                            "dP", "dX", "out")}
        self.dram = {}

    def op(self, eng, fn, tick=None, k=1):
        if tick is None:
            self.q[eng].append(fn)
            return None
        ctr = self.c[tick]
        ctr.n += k * ctr.step
        kk = k * ctr.step
        self.q[eng].append(lambda e, f=fn, s=ctr.sem, kk=kk: f(e).then_inc(s, kk))
        return ctr.n

    def wait(self, eng, tick, val):
        if val and val > 0:
            ctr = self.c[tick]
            if ctr.step == 16 and val < ctr.n and val > ctr.sealed:
                raise AssertionError(
                    f"late dma wait: {tick}>={val} registered after {ctr.n} issued "
                    f"(sealed {ctr.sealed}) - wait on the newest mark or "
                    f"reorder so the wait precedes subsequent dmas")
            ctr.maxwait = max(ctr.maxwait, val)
            self.q[eng].append(lambda e, s=ctr.sem, v=val: e.wait_ge(s, v))

    def _dma_gate(self, eng, tick):
        ctr = self.c[tick]
        if ctr.maxwait > ctr.selfwait:
            v = ctr.maxwait
            self.q[eng].append(lambda e, s=ctr.sem, v=v: e.wait_ge(s, v))
            ctr.selfwait = v
        ctr.sealed = ctr.selfwait

    def dma(self, eng, out, in_, tick):
        self._dma_gate(eng, tick)
        return self.op(eng, lambda e, o=out, i=in_: e.dma_start(out=o, in_=i), tick=tick)

    def dmat(self, eng, out, in_, tick):
        self._dma_gate(eng, tick)
        return self.op(eng, lambda e, o=out, i=in_: e.dma_start_transpose(out=o, in_=i), tick=tick)

    def din(self, name, shape, dtype=F32):
        h = self.nc.dram_tensor(name, list(shape), dtype, kind="ExternalInput")
        self.dram[name] = h
        return h.ap()

    def dout(self, name, shape, dtype=F32):
        h = self.nc.dram_tensor(name, list(shape), dtype, kind="ExternalOutput")
        self.dram[name] = h
        return h.ap()

    def dscr(self, name, shape, dtype=F32):
        h = self.nc.dram_tensor(name, list(shape), dtype, kind="Internal")
        self.dram[name] = h
        return h.ap()

    def sb(self, name, shape, dtype, off):
        return self.nc.alloc_sbuf_tensor_at(name, list(shape), dtype, offset=off)

    def _finish(self):
        self.wait("sync", "out", self.c["out"].n)
        for k in ("dS", "dG", "dV", "dT", "dC", "dP", "dX"):
            self.wait("sync", k, self.c[k].n)
        block = self.stack.enter_context(self.nc.Block())
        for name in ("sync", "vector", "scalar", "tensor", "gpsimd"):
            ops = self.q[name]

            def make(ops):
                def _(e):
                    for f in ops:
                        f(e)
                return _
            getattr(block, name)(make(ops))
        self.stack.close()
        return self.nc

    def _tap_sbuf(self, name, ap_list, shape):
        dbg = self.dout("dbg", shape, F32)
        col = 0
        for ap, width in ap_list:
            self.dma("gpsimd", dbg[:, col:col + width], ap, "out")
            col += width
        return self._finish()

    def build(self):
        nc, tap = self.nc, self.tap
        K = 1024

        x_in = self.din("x", [T, D])
        y_out = self.dout("y", [T, D])
        identity = self.din("identity", [128, 128])
        onesf_in = self.din("onesf", [1, 128])
        onesb_in = self.din("onesb", [1, 512], BF16)
        triu_in = self.din("triu", [128, 128], BF16)
        ropeC_in = self.din("ropeC", [T, 512])
        ropeS_in = self.din("ropeS", [T, 512])
        L0_WT = self.din("L0_WT", [D, 3 * D])
        L0_bcol_in = self.din("L0_bcol", [128, 16])
        L0_bvrow_in = self.din("L0_bvrow", [1, D])
        L0_wg_in = self.din("L0_wg", [I0, D])
        L0_wu_in = self.din("L0_wu", [I0, D])
        L0_wd_in = self.din("L0_wd", [D, I0])
        L0_wdT_in = self.din("L0_wdT", [I0, D])
        L0_clipx = self.din("L0_clipx", [1, 1])
        L0_clipd = self.din("L0_clipd", [1, 1])
        L1_WT = {k: self.din(f"L1_W{k}T", [D, D]) for k in "qkvo"}
        L1_brow = {k: self.din(f"L1_b{k}row", [1, D]) for k in "qkvo"}
        L1_w1T_in = self.din("L1_w1T", [D, H1])
        L1_w2T_in = self.din("L1_w2T", [H1, D])
        L1_cw_in = self.din("L1_cw", [128, NH1 * 3])
        L1_cb_in = self.din("L1_cb", [128, NH1])

        qpark = self.dscr("qpark", [D, T], BF16)
        xpark = self.dscr("xpark", [T, D], F32)
        w1q_park = self.dscr("w1q_park", [D, H1], BF16)
        w2q_park = self.dscr("w2q_park", [H1, D], BF16)
        cc_in = self.dscr("cc_in", [1, 1])
        cc_out = self.dscr("cc_out", [1, 1])
        cc_in2 = self.dscr("cc_in2", [1, 1])
        cc_out2 = self.dscr("cc_out2", [1, 1])
        self.ccs = (cc_in, cc_out, cc_in2, cc_out2)

        BASE = 17 * K
        CONST = self.sb("CONST", [128, 128], F32, 205 * K)
        TRIU = self.sb("TRIU", [128, 128], BF16, int(205.5 * K))
        ONESF = self.sb("ONESF", [1, 128], F32, int(205.75 * K))
        ONESB = self.sb("ONESB", [1, 512], BF16, int(206.25 * K))
        BCOL = self.sb("BCOL", [128, 16], F32, int(207.25 * K))
        BQ8 = self.sb("BQ8", [128, 8], F32, int(207.5 * K))
        SM = self.sb("SM", [128, 32], F32, int(207.625 * K))
        ROW = self.sb("ROW", [1, 32], F32, int(207.8125 * K))
        NK = self.sb("NK", [128, 2], F32, 208 * K)
        RC = self.sb("RC", [128, 4], F32, int(208.0625 * K))
        MU = self.sb("MU", [128, 8], F32, int(208.25 * K))
        SC = self.sb("SC", [128, 8], F32, int(208.5 * K))
        BVROW = self.sb("BVROW", [1, D], BF16, 209 * K)
        TMPB = self.sb("TMPB", [128, D], BF16, 211 * K)
        TMP2 = self.sb("TMP2", [128, D], F32, int(219.5 * K))

        X = self.sb("X", [128, NT * D], F32, BASE)
        ROPEC = self.sb("ROPEC", [128, NT * 512], BF16, BASE)
        ROPES = self.sb("ROPES", [128, NT * 512], BF16, BASE + 16 * K)
        BROWS = self.sb("BROWS", [1, 4 * D], BF16, BASE + 32 * K)
        XSTG = self.sb("XSTG", [128, 2 * D], F32, BASE + 40 * K)
        WRES = self.sb("WRES", [128, NC * D], BF16, BASE + 48 * K)
        W2QT = self.sb("W2QT", [128, NH1 * D], BF16, BASE)
        XT = self.sb("XT", [128, NC * T], BF16, 81 * K)
        WS2 = self.sb("WS2", [128, 7 * D], F32, 81 * K)
        WACC = self.sb("WACC", [128, D], F32, 109 * K)
        WQB = self.sb("WQB", [128, D], BF16, 213 * K)
        WQF = self.sb("WQF", [128, D], F32, 215 * K)
        WSM = self.sb("WSM", [128, 4], F32, 219 * K)
        WROW = self.sb("WROW", [1, 8], F32, int(219.125 * K))
        WSC = self.sb("WSC", [128, 4], F32, int(219.25 * K))
        NK8 = self.sb("NK8", [128, NHP], F32, int(219.3125 * K))
        WDSB = self.sb("WDSB", [128, 1024], F32, 213 * K)
        WDSA = self.sb("WDSA", [128, 512], F32, 217 * K)
        MUX = self.sb("MUX", [128, 16], F32, int(219.375 * K))
        YTM = self.sb("YTM", [128, 4 * D], BF16, 81 * K)
        XPOSE = self.sb("XPOSE", [128, D], BF16, 89 * K)
        XPO4 = self.sb("XPO4", [128, 3 * D], BF16, 107 * K)
        YS2 = self.sb("YS2", [128, 2 * D], F32, 81 * K)
        WORES = self.sb("WORES", [128, NC * D], BF16, 91 * K)
        XQ1T = self.sb("XQ1T", [128, NC * T], BF16, 81 * K)
        KF = self.sb("KF", [128, NC * T], BF16, 113 * K)
        H1S = self.sb("H1S", [128, NH1 * 1026], BF16, 113 * K)
        VT = self.sb("VT", [128, NT * NH * 65], BF16, 145 * K)
        QSTG = self.sb("QSTG", [128, T], BF16, 178 * K)
        YF = self.sb("YF", [128, 2 * TB], F32, 182 * K)
        PT = self.sb("PT", [128, 2 * 1024], BF16, 186 * K)
        QB = self.sb("QB", [128, NC * TB], BF16, 190 * K)
        VW = self.sb("VW", [128, NC * D], BF16, 182 * K)
        WTILE = self.sb("WTILE", [128, 2 * NC * 128], BF16, 198 * K)
        HBUF = self.sb("HBUF", [128, NI0 * T], F16, BASE)
        XQT = self.sb("XQT", [128, NC * T], BF16, 105 * K)
        WDQT = self.sb("WDQT", [128, NI0 * D], F16, 139 * K)
        WQT = self.sb("WQT", [128, NC * 128], BF16, 137 * K)
        WSTR = self.sb("WSTR", [128, I0], F32, 183 * K)
        TMPF = self.sb("TMPF", [128, I0], F32, 194 * K)
        ABSA = self.sb("ABSA", [128, D], F32, 178 * K)
        HTOK = self.sb("HTOK", [128, D], F32, 182 * K)
        RTOK = self.sb("RTOK", [128, T], F32, 186 * K)
        W1C = self.sb("W1C", [128, 2 * NC * 128], BF16, 194 * K)
        ROPT = self.sb("ROPT", [128, D], BF16, 202 * K)
        CWCOL = self.sb("CWCOL", [128, NH1 * 3], F32, 198 * K)
        CBCOL = self.sb("CBCOL", [128, NH1], F32, int(198.5 * K))

        self.t = dict(locals())

        psA = self.stack.enter_context(nc.psum_tensor("psA", [128, 1024], F32))
        psB = self.stack.enter_context(nc.psum_tensor("psB", [128, 1024], F32))
        psPA = self.stack.enter_context(nc.psum_tensor("psPA", [128, 512], F32))
        psPB = self.stack.enter_context(nc.psum_tensor("psPB", [128, 512], F32))
        psX = self.stack.enter_context(nc.psum_tensor("psX", [128, 512], F32))
        psY = self.stack.enter_context(nc.psum_tensor("psY", [128, 512], F32))
        self.p = dict(psA=psA, psB=psB, psPA=psPA, psPB=psPB, psX=psX, psY=psY)

        c = self.c

        self.dma("gpsimd", CONST[:], identity, "dG")
        self.dma("gpsimd", TRIU[:], triu_in, "dG")
        self.dma("gpsimd", ONESF[:], onesf_in, "dG")
        self.dma("gpsimd", ONESB[:], onesb_in, "dG")
        self.dma("gpsimd", BCOL[:], L0_bcol_in, "dG")
        self.dma("gpsimd", BVROW[:], L0_bvrow_in, "dG")
        g_consts = c["dG"].n
        for i in range(NT):
            self.dma("sync", X[:, i * D:(i + 1) * D], x_in[i * 128:(i + 1) * 128, :], "dS")
        s_x = c["dS"].n
        self.wait("vector", "dG", g_consts)
        v_consts = self.op("vector", lambda e: e.tensor_scalar_mul(
            out=BQ8[:], in0=BCOL[:, 0:8], scalar1=0.125), "v")

        self.wait("tensor", "dS", s_x)
        self.wait("tensor", "dG", g_consts)
        v_xt = self._transpose_f32_to_bf16(
            src=lambda i, cp: X[:, i * D + cp * 128:i * D + cp * 128 + 128],
            dst=lambda i, cp: XT[:, cp * T + i * 128:cp * T + i * 128 + 128],
            n_i=NT, n_c=NC, psX=psX, psY=psY, CONST=CONST)
        if tap == "xt":
            self.wait("gpsimd", "s", v_xt)
            return self._tap_sbuf("xt", [(XT[:, cp * T:(cp + 1) * T], T) for cp in range(NC)],
                                  [128, NC * T])

        self.wait("tensor", "s", v_xt)
        self.wait("scalar", "v", v_consts)
        ps_free = {0: 0, 1: 0}
        wt_free = {0: 0, 1: 0}
        qstg_parked = 0
        for o in range(16):
            buf = o % 2
            self.wait("gpsimd", "t", wt_free[buf])
            for cp in range(NC):
                self.dma("gpsimd", WTILE[:, buf * 1024 + cp * 128:buf * 1024 + cp * 128 + 128],
                         L0_WT[cp * 128:(cp + 1) * 128, o * 128:(o + 1) * 128], "dG")
            gm = c["dG"].n
            self.wait("tensor", "dG", gm)
            for tb4 in range(4):
                pst, pi = (psA, 0) if tb4 % 2 == 0 else (psB, 1)
                self.wait("tensor", "s", ps_free[pi])
                tm = None
                for cp in range(NC):
                    tm = self.op("tensor", lambda e, b=buf, cp=cp, tb4=tb4, pst=pst: e.matmul(
                        pst[:, 0:512], WTILE[:, b * 1024 + cp * 128:b * 1024 + cp * 128 + 128],
                        XT[:, cp * T + tb4 * 512:cp * T + (tb4 + 1) * 512],
                        start=(cp == 0), stop=(cp == NC - 1)),
                        tick=("t" if cp == NC - 1 else None))
                if tb4 == 3:
                    wt_free[buf] = tm
                self.wait("scalar", "t", tm)
                if o < 8:
                    if tb4 == 0 and qstg_parked:
                        self.wait("scalar", "dS", qstg_parked)
                    sm = self.op("scalar", lambda e, o=o, tb4=tb4, pst=pst: e.activation(
                        out=QSTG[:, tb4 * 512:(tb4 + 1) * 512], in_=pst[:, 0:512],
                        func=ACT.Identity, bias=BQ8[:, o:o + 1], scale=0.125), "s")
                else:
                    sm = self.op("scalar", lambda e, o=o, tb4=tb4, pst=pst: e.activation(
                        out=KF[:, (o - 8) * T + tb4 * 512:(o - 8) * T + (tb4 + 1) * 512],
                        in_=pst[:, 0:512], func=ACT.Identity,
                        bias=BCOL[:, o:o + 1], scale=1.0), "s")
                ps_free[pi] = sm
            if o < 8:
                self.wait("sync", "s", sm)
                qstg_parked = self.dma("sync", qpark[o * 128:(o + 1) * 128, :], QSTG[:], "dS")
        s_qk = c["s"].n
        d_qpark = c["dS"].n

        for cp in range(NC):
            self.dma("gpsimd", VW[:, cp * D:(cp + 1) * D],
                     L0_WT[cp * 128:(cp + 1) * 128, 2 * D:3 * D], "dG")
        gm_vw = c["dG"].n
        self.wait("tensor", "dG", gm_vw)
        ps_free_v = {0: 0, 1: 0}
        for tk in range(NT):
            pst, pi = (psA, 0) if tk % 2 == 0 else (psB, 1)
            if ps_free[pi]:
                self.wait("tensor", "s", ps_free[pi])
                ps_free[pi] = 0
            if ps_free_v[pi]:
                self.wait("tensor", "v", ps_free_v[pi])
            for cp in range(NC):
                self.op("tensor", lambda e, cp=cp, tk=tk, pst=pst: e.matmul(
                    pst[:, 0:512], XT[:, cp * T + tk * 128:cp * T + tk * 128 + 128],
                    VW[:, cp * D:cp * D + 512], start=(cp == 0), stop=False), None)
                self.op("tensor", lambda e, cp=cp, tk=tk, pst=pst: e.matmul(
                    pst[:, 512:1024], XT[:, cp * T + tk * 128:cp * T + tk * 128 + 128],
                    VW[:, cp * D + 512:(cp + 1) * D], start=(cp == 0), stop=False), None)
            self.op("tensor", lambda e, pst=pst: e.matmul(
                pst[:, 0:512], ONESB[0:1, 0:128], BVROW[0:1, 0:512],
                start=False, stop=True), None)
            tm = self.op("tensor", lambda e, pst=pst: e.matmul(
                pst[:, 512:1024], ONESB[0:1, 0:128], BVROW[0:1, 512:1024],
                start=False, stop=True), "t")
            self.wait("vector", "t", tm)
            vt_tile = VT[:, tk * NH * 65:(tk + 1) * NH * 65].rearrange("p (h d) -> p h d", h=NH)
            self.op("vector", lambda e, pst=pst, vt=vt_tile: e.tensor_copy(
                out=vt[:, :, 0:64], in_=pst[:, 0:1024].rearrange("p (h d) -> p h d", h=NH)), None)
            ps_free_v[pi] = self.op("vector", lambda e, vt=vt_tile: e.memset(
                vt[:, :, 64:65], 1.0), "v")
        v_vt = c["v"].n
        t_vproj = c["t"].n

        if tap == "qkv":
            self.wait("gpsimd", "s", s_qk)
            self.wait("gpsimd", "v", v_vt)
            aps = [(KF[:, cp * T:(cp + 1) * T], T) for cp in range(NC)]
            aps += [(VT[:, i * NH * 65:(i + 1) * NH * 65], NH * 65) for i in range(NT)]
            return self._tap_sbuf("qkv", aps, [128, NC * T + NT * NH * 65])

        wp_hook, wp_drain = self._wprep_build(xt_free_t=t_vproj)
        self._attention(layer=0, start_after={
            "sync": [("dS", d_qpark), ("t", t_vproj)],
            "tensor": [("v", v_vt), ("s", s_qk)]}, hook=wp_hook)
        wp_drain()
        v_ln1 = c["v"].n

        if tap == "x1":
            self.wait("gpsimd", "v", v_ln1)
            return self._tap_sbuf("x1", [(X[:, i * D:(i + 1) * D], D) for i in range(NT)],
                                  [128, NT * D])

        if tap is not None and tap.startswith("stop_x1"):
            return self._finish()

        self._l0_ffn()
        if tap == "x2":
            self.wait("gpsimd", "dS", self.x2_parked)
            dbg = self.dout("dbg", [T, D], F32)
            for i in range(NT):
                self.dma("gpsimd", dbg[i * 128:(i + 1) * 128, :],
                         xpark[i * 128:(i + 1) * 128, :], "out")
            return self._finish()

        self._l1_attn()
        if tap == "x3":
            self.wait("gpsimd", "dP", self.x3_parked)
            dbg = self.dout("dbg", [T, D], F32)
            for i in range(NT):
                self.dma("gpsimd", dbg[i * 128:(i + 1) * 128, :],
                         xpark[i * 128:(i + 1) * 128, :], "out")
            return self._finish()

        self._l1_ffn()
        return self._finish()

    def _transpose_f32_to_bf16(self, src, dst, n_i, n_c, psX, psY, CONST):
        evacs = []
        for kk in range(n_i * n_c):
            i, cp = divmod(kk, n_c)
            pst = psX if kk % 2 == 0 else psY
            if kk >= 2:
                self.wait("tensor", "s", evacs[kk - 2])
            tm = self.op("tensor", lambda e, i=i, cp=cp, pst=pst: e.transpose(
                pst[:, 0:128], src(i, cp), CONST[:]), "t")
            self.wait("scalar", "t", tm)
            evacs.append(self.op("scalar", lambda e, i=i, cp=cp, pst=pst: e.activation(
                out=dst(i, cp), in_=pst[:, 0:128], func=ACT.Identity), "s"))
        return self.c["s"].n

    def vop(self, fn):
        m = self.op("vector", fn, "v")
        self.wait("vector", "v", m)
        return m

    def gop(self, fn):
        m = self.op("gpsimd", fn, "g")
        self.wait("gpsimd", "g", m)
        return m

    def recip(self, dst, src, tmp):
        self.vop(lambda e: e.reciprocal(out=dst, in_=src))
        self.vop(lambda e: e.tensor_mul(out=tmp, in0=dst, in1=src))
        self.vop(lambda e: e.tensor_scalar(out=tmp, in0=tmp, scalar1=-1.0,
                                           scalar2=2.0, op0=ALU.mult, op1=ALU.add))
        return self.vop(lambda e: e.tensor_mul(out=dst, in0=dst, in1=tmp))

    def _layernorm_X(self, xt_of, tiles, SQS, mu_of=None):
        self._ln_done = {}
        for i in tiles:
            MU = mu_of(i) if mu_of is not None else self.t["MU"]
            xt = xt_of(i)
            vm = self.vop(lambda e, xt=xt: e.reduce_sum(
                out=MU[:, 0:1], in_=xt, axis=AX))
            self.wait("scalar", "v", vm)
            sm = self.op("scalar", lambda e, xt=xt: e.activation(
                out=SQS[:], in_=xt, func=ACT.Square, accum_out=MU[:, 1:2]), "s")
            self.wait("vector", "s", sm)
            self.vop(lambda e: e.tensor_scalar_mul(
                out=MU[:, 0:1], in0=MU[:, 0:1], scalar1=1.0 / D))
            self.vop(lambda e: e.tensor_mul(
                out=MU[:, 2:3], in0=MU[:, 0:1], in1=MU[:, 0:1]))
            self.vop(lambda e: e.scalar_tensor_tensor(
                out=MU[:, 1:2], in0=MU[:, 1:2], scalar=1.0 / D, in1=MU[:, 2:3],
                op0=ALU.mult, op1=ALU.subtract))
            vm = self.vop(lambda e: e.tensor_scalar_add(
                out=MU[:, 1:2], in0=MU[:, 1:2], scalar1=EPS))
            self.wait("scalar", "v", vm)
            sm0 = self.op("scalar", lambda e: e.activation(
                out=MU[:, 2:3], in_=MU[:, 1:2], func=ACT.Ln), "s")
            self.wait("scalar", "s", sm0)
            sm = self.op("scalar", lambda e: e.activation(
                out=MU[:, 2:3], in_=MU[:, 2:3], func=ACT.Exp, scale=-0.5), "s")
            self.wait("vector", "s", sm)
            self.vop(lambda e: e.scalar_tensor_tensor(
                out=MU[:, 3:4], in0=MU[:, 0:1], scalar=-1.0, in1=MU[:, 2:3],
                op0=ALU.mult, op1=ALU.mult))
            self._ln_done[i] = self.op("vector", lambda e, xt=xt: e.tensor_scalar(
                out=xt, in0=xt, scalar1=MU[:, 2:3], scalar2=MU[:, 3:4],
                op0=ALU.mult, op1=ALU.add), "v")

    def _attention(self, layer, start_after=None, hook=None):
        t = self.t
        c = self.c
        KF, VT, QB, PT, YF = t["KF"], t["VT"], t["QB"], t["PT"], t["YF"]
        NK8, RC, ROW, CONST, ONESF, TRIU = t["NK8"], t["RC"], t["ROW"], t["CONST"], t["ONESF"], t["TRIU"]
        X, YTM = t["X"], t["YTM"]
        psA, psB, psPA, psPB, psX, psY = (self.p[k] for k in
                                          ("psA", "psB", "psPA", "psPB", "psX", "psY"))
        qpark = self.dram["qpark"].ap()
        causal = (layer == 0)

        exp_done = {0: 0, 1: 0}
        km_done = {0: 0, 1: 0}
        pt_free = {0: 0, 1: 0}
        pv_free = 0
        ps = {"x": 0, "y": 0}
        qb_used = 0

        self._post_ps_free = {0: 0, 1: 0}
        self._post_wo = {}
        self._post_dm = {}
        self._post_park = {0: 0, 1: 0, 2: 0, 3: 0}
        for b in range(NB):
            self.wait("sync", "t", qb_used)
            if start_after and b == 0:
                for tick, val in start_after.get("sync", []):
                    self.wait("sync", tick, val)
            qb_marks = {}
            for cp in range(NC):
                qb_marks[cp] = self.dma(
                    "sync", QB[:, cp * TB:(cp + 1) * TB],
                    qpark[cp * 128:(cp + 1) * 128, b * TB:(b + 1) * TB], "dS")
                self.wait("sync", "dS", qb_marks[cp])
            if start_after and b == 0:
                for tick, val in start_after.get("tensor", []):
                    self.wait("tensor", tick, val)
            if self._post_ps_free[0] or self._post_ps_free[1]:
                self.wait("tensor", "v", max(self._post_ps_free[0], self._post_ps_free[1]))
                self._post_ps_free = {0: 0, 1: 0}

            valid = list(range(4 * (b + 1))) if causal else list(range(NT))
            last_i = len(valid) - 1
            for hp in range(NHP):
                if hook is not None:
                    hook(2 * (b * NHP + hp), ps)
                self.wait("tensor", "dS", qb_marks[hp])
                hA, hB = 2 * hp, 2 * hp + 1

                def chunk_lo(tc):
                    jj = tc - 4 * b
                    return 128 * jj if (causal and jj > 0) else 0

                def emit_sc(idx):
                    tc = valid[idx]
                    buf = idx % 2
                    pst = psA if buf == 0 else psB
                    lo = chunk_lo(tc)
                    self.wait("tensor", "s", exp_done[buf])
                    self.wait("tensor", "v", km_done[buf])
                    self.op("tensor", lambda e, hp=hp, tc=tc, pst=pst, lo=lo: e.matmul(
                        pst[:, lo:512], KF[0:64, hp * T + tc * 128:hp * T + tc * 128 + 128],
                        QB[0:64, hp * TB + lo:hp * TB + 512],
                        start=True, stop=True, tile_position=(0, 0)), None)
                    return self.op("tensor", lambda e, hp=hp, tc=tc, pst=pst, lo=lo: e.matmul(
                        pst[:, 512 + lo:1024], KF[64:128, hp * T + tc * 128:hp * T + tc * 128 + 128],
                        QB[64:128, hp * TB + lo:hp * TB + 512],
                        start=True, stop=True, tile_position=(64, 0)), "t")

                sc_marks = {0: emit_sc(0)}
                nk_ready = 0
                if b == 0 or getattr(self, "fresh_k", False):
                    self.wait("vector", "t", sc_marks[0])
                    km = self.op("vector", lambda e: e.reduce_max(
                        out=RC[:, 2:3], in_=psA[:, 0:1024], axis=AX, negate=True), "v")
                    km_done[0] = km
                    sc_marks[1] = emit_sc(1)
                    self.wait("tensor", "v", km)
                    self.wait("tensor", "v", ps["x"])
                    tmk = self.op("tensor", lambda e: e.transpose(
                        psX[0:1, 0:128], RC[:, 2:3], CONST[:]), "t")
                    self.wait("vector", "t", tmk)
                    vk = self.op("vector", lambda e: e.tensor_reduce(
                        out=ROW[0:1, 0:1], in_=psX[0:1, 0:128], axis=AX, op=ALU.min), "v")
                    if layer == 1:
                        self.wait("vector", "v", vk)
                        vk = self.vop(lambda e: e.tensor_scalar_mul(
                            out=ROW[0:1, 0:1], in0=ROW[0:1, 0:1], scalar1=0.125))
                    ps["x"] = vk
                    self.wait("tensor", "v", vk)
                    self.wait("tensor", "v", ps["y"])
                    tmk2 = self.op("tensor", lambda e: e.matmul(
                        psY[:, 0:1], ONESF[0:1, 0:128], ROW[0:1, 0:1],
                        start=True, stop=True), "t")
                    self.wait("vector", "t", tmk2)
                    vk2 = self.op("vector", lambda e, hp=hp: e.tensor_copy(
                        out=NK8[:, hp:hp + 1], in_=psY[:, 0:1]), "v")
                    ps["y"] = vk2
                    nk_ready = vk2
                else:
                    sc_marks[1] = emit_sc(1)

                for idx, tc in enumerate(valid):
                    buf = idx % 2
                    pst = psA if buf == 0 else psB
                    lo = chunk_lo(tc)
                    self.wait("scalar", "t", sc_marks.pop(idx))
                    if idx == 0 and nk_ready:
                        self.wait("scalar", "v", nk_ready)
                    self.wait("scalar", "t", pt_free[buf])
                    pview_in = pst[:, 0:1024].rearrange("p (g f) -> p g f", g=2)
                    pview_out = PT[:, buf * 1024:(buf + 1) * 1024].rearrange("p (g f) -> p g f", g=2)
                    esc = 0.125 if layer == 1 else 1.0
                    sm_exp = self.op("scalar", lambda e, pi=pview_in, po=pview_out, lo=lo, hp=hp, esc=esc: e.activation(
                        out=po[:, :, lo:512], in_=pi[:, :, lo:512], func=ACT.Exp,
                        bias=NK8[:, hp:hp + 1], scale=esc), "s")
                    exp_done[buf] = sm_exp
                    pv_wait_v = 0
                    if causal and tc >= 4 * b:
                        self.wait("vector", "s", sm_exp)
                        self.op("vector", lambda e, buf=buf, lo=lo: e.tensor_mul(
                            out=PT[:, buf * 1024 + lo:buf * 1024 + lo + 128],
                            in0=PT[:, buf * 1024 + lo:buf * 1024 + lo + 128], in1=TRIU[:]), None)
                        pv_wait_v = self.op("vector", lambda e, buf=buf, lo=lo: e.tensor_mul(
                            out=PT[:, buf * 1024 + 512 + lo:buf * 1024 + 512 + lo + 128],
                            in0=PT[:, buf * 1024 + 512 + lo:buf * 1024 + 512 + lo + 128],
                            in1=TRIU[:]), "v")
                    if idx + 2 <= last_i:
                        sc_marks[idx + 2] = emit_sc(idx + 2)
                    self.wait("tensor", "s", sm_exp)
                    if pv_wait_v:
                        self.wait("tensor", "v", pv_wait_v)
                    if idx == 0:
                        self.wait("tensor", "s", pv_free)
                    self.op("tensor", lambda e, tk=tc, hA=hA, buf=buf, lo=lo, st=(idx == 0), sp=(idx == last_i): e.matmul(
                        psPA[0:65, lo:512],
                        VT[:, tk * NH * 65 + hA * 65:tk * NH * 65 + hA * 65 + 65],
                        PT[:, buf * 1024 + lo:buf * 1024 + 512],
                        start=st, stop=sp), None)
                    pt_free[buf] = self.op("tensor", lambda e, tk=tc, hB=hB, buf=buf, lo=lo, st=(idx == 0), sp=(idx == last_i): e.matmul(
                        psPB[0:65, lo:512],
                        VT[:, tk * NH * 65 + hB * 65:tk * NH * 65 + hB * 65 + 65],
                        PT[:, buf * 1024 + 512 + lo:buf * 1024 + 1024],
                        start=st, stop=sp), "t")
                tm_pv = pt_free[last_i % 2]
                if hook is not None:
                    hook(2 * (b * NHP + hp) + 1, ps)
                self.wait("scalar", "t", tm_pv)
                self.op("scalar", lambda e: e.activation(
                    out=YF[0:65, 0:512], in_=psPA[0:65, 0:512], func=ACT.Identity), None)
                pv_free = self.op("scalar", lambda e: e.activation(
                    out=YF[0:65, 512:1024], in_=psPB[0:65, 0:512], func=ACT.Identity), "s")
                for j in range(4):
                    ti = 4 * b + j
                    self.wait("tensor", "v", ps["x"])
                    self.wait("tensor", "s", pv_free)
                    tmtA = self.op("tensor", lambda e, j=j: e.transpose(
                        psX[:, 0:65], YF[0:65, j * 128:j * 128 + 128],
                        CONST[0:65, 0:65]), "t")
                    self.wait("tensor", "v", ps["y"])
                    tmtB = self.op("tensor", lambda e, j=j: e.transpose(
                        psY[:, 0:65], YF[0:65, 512 + j * 128:512 + j * 128 + 128],
                        CONST[0:65, 0:65]), "t")
                    self.wait("vector", "t", tmtA)
                    self.op("vector", lambda e: e.reciprocal(
                        out=RC[:, 0:1], in_=psX[:, 64:65]), None)
                    self.wait("vector", "t", tmtB)
                    vrc = self.op("vector", lambda e: e.reciprocal(
                        out=RC[:, 1:2], in_=psY[:, 64:65]), "v")
                    self.wait("vector", "v", vrc)
                    for half, h, pst2 in ((0, hA, psX), (1, hB, psY)):
                        if layer == 0:
                            mark = self.op("vector", lambda e, pst2=pst2, half=half, ti=ti, h=h: e.scalar_tensor_tensor(
                                out=X[:, ti * D + h * 64:ti * D + h * 64 + 64],
                                in0=pst2[:, 0:64], scalar=RC[:, half:half + 1],
                                in1=X[:, ti * D + h * 64:ti * D + h * 64 + 64],
                                op0=ALU.mult, op1=ALU.add), "v")
                        else:
                            mark = self.op("vector", lambda e, pst2=pst2, half=half, j=j, h=h: e.tensor_scalar_mul(
                                out=YTM[:, j * D + h * 64:j * D + h * 64 + 64],
                                in0=pst2[:, 0:64], scalar1=RC[:, half:half + 1]), "v")
                        if half == 0:
                            ps["x"] = mark
                        else:
                            ps["y"] = mark
                    if hp == NHP - 1:
                        if layer == 1:
                            self._l1_post_stage1(b, j)
                        else:
                            self._l0_post_tile(b, j)
            qb_used = c["t"].n
            if layer == 1:
                self._l1_block_post(b)
        self.attn_v_done = c["v"].n

    def _scalar_bcast(self, src_row_ap, dst_col_ap, psY, ONESF, width=1):
        self.wait("tensor", "v", self.c["v"].n)
        tm = self.op("tensor", lambda e: e.matmul(
            psY[:, 0:width], ONESF[0:1, 0:128], src_row_ap,
            start=True, stop=True), "t")
        self.wait("vector", "t", tm)
        return self.vop(lambda e: e.tensor_copy(out=dst_col_ap, in_=psY[:, 0:width]))

    def _part_reduce_max(self, col_ap, row_cell, psX, CONST):
        self.wait("tensor", "v", self.c["v"].n)
        tm = self.op("tensor", lambda e: e.transpose(
            psX[0:1, 0:128], col_ap, CONST[:]), "t")
        self.wait("vector", "t", tm)
        return self.vop(lambda e: e.tensor_reduce(
            out=row_cell, in_=psX[0:1, 0:128], axis=AX, op=ALU.max))

    def _wprep_build(self, xt_free_t):
        t, c = self.t, self.c
        WS2, WQB, WQF, WSM, WROW, WSC, CONST, ONESF = (t[k] for k in
            ("WS2", "WQB", "WQF", "WSM", "WROW", "WSC", "CONST", "ONESF"))
        psX, psY = self.p["psX"], self.p["psY"]
        w1q_park = self.dram["w1q_park"].ap()
        w2q_park = self.dram["w2q_park"].ap()
        pieces = [("w1", t["L1_w1T_in"], w1q_park, cp, qq)
                  for cp in range(NC) for qq in range(4)] + \
                 [("w2", t["L1_w2T_in"], w2q_park, ch, 0) for ch in range(NH1)]
        NPC = len(pieces)
        st = {"xs": {}, "first": {"w1": True, "w2": True}, "gated": False,
              "need_wsc": True, "wsc_v": 0}
        self._wprep_parked = 0

        def load(n_):
            which, src, dst, r0, q0 = pieces[n_ % NPC]
            sl = n_ % 7
            eng, tick = ("scalar", "dC") if n_ % 2 == 0 else ("gpsimd", "dG")
            if not st["gated"]:
                self.wait("scalar", "t", xt_free_t)
                self.wait("gpsimd", "t", xt_free_t)
                st["gated"] = True
            if st["xs"].get(sl):
                kind, m = st["xs"][sl]
                self.wait(eng, kind, m)
            self.dma(eng, WS2[:, sl * D:(sl + 1) * D],
                     src[r0 * 128:(r0 + 1) * 128, q0 * D:(q0 + 1) * D], tick)

        def gate_loads(scal=False):
            self.wait("vector", "dC", c["dC"].n)
            self.wait("vector", "dG", c["dG"].n)
            if scal:
                self.wait("scalar", "dC", c["dC"].n)
                self.wait("scalar", "dG", c["dG"].n)

        def comp_a(n_):
            which = pieces[n_][0]
            sl = n_ % 7
            self.vop(lambda e, sl=sl: e.reduce_sum(
                out=WSM[:, 0:1], in_=WS2[:, sl * D:(sl + 1) * D], axis=AX,
                apply_absolute_value=True))
            col = 1 if which == "w1" else 2
            if st["first"][which]:
                st["first"][which] = False
                m = self.vop(lambda e, col=col: e.tensor_copy(
                    out=WSM[:, col:col + 1], in_=WSM[:, 0:1]))
            else:
                m = self.vop(lambda e, col=col: e.tensor_add(
                    out=WSM[:, col:col + 1], in0=WSM[:, col:col + 1],
                    in1=WSM[:, 0:1]))
            st["xs"][sl] = ("v", m)

        def finale(ps):
            for which, col, rc in (("w1", 1, 0), ("w2", 2, 2)):
                self.wait("tensor", "v", c["v"].n)
                tm = self.op("tensor", lambda e, col=col: e.transpose(
                    psX[0:1, 0:128], WSM[:, col:col + 1], CONST[:]), "t")
                self.wait("vector", "t", tm)
                ps["x"] = self.vop(lambda e, rc=rc: e.tensor_reduce(
                    out=WROW[0:1, rc:rc + 1], in_=psX[0:1, 0:128], axis=AX,
                    op=ALU.add))
                self.vop(lambda e, rc=rc: e.tensor_scalar_mul(
                    out=WROW[0:1, rc:rc + 1], in0=WROW[0:1, rc:rc + 1],
                    scalar1=1.0 / (D * H1)))
                self.vop(lambda e, rc=rc: e.tensor_scalar_max(
                    out=WROW[0:1, rc:rc + 1], in0=WROW[0:1, rc:rc + 1],
                    scalar1=1e-5))
                self.recip(WROW[0:1, rc + 1:rc + 2], WROW[0:1, rc:rc + 1],
                           WROW[0:1, 6:7])
            self._scalar_bcast(WROW[0:1, 1:2], WSC[:, 0:1], psY, ONESF, width=1)
            self._scalar_bcast(WROW[0:1, 3:4], WSC[:, 1:2], psY, ONESF, width=1)
            self._scalar_bcast(WROW[0:1, 0:1], WSC[:, 2:3], psY, ONESF, width=1)
            ps["y"] = self._scalar_bcast(WROW[0:1, 2:3], WSC[:, 3:4], psY, ONESF,
                                         width=1)
            st["wsc_v"] = self.vop(lambda e: e.memset(WSM[:, 3:4], MAGIC))

        def comp_b(n_):
            which, src, dst, r0, q0 = pieces[n_ - NPC]
            sl = n_ % 7
            if st["need_wsc"]:
                st["need_wsc"] = False
                self.wait("scalar", "v", st["wsc_v"])
            scol = WSC[:, 0:1] if which == "w1" else WSC[:, 1:2]
            if st.get("wqf_free"):
                self.wait("scalar", "v", st["wqf_free"])
            m = self.op("scalar", lambda e, sl=sl, scol=scol: e.activation(
                out=WQF[:], in_=WS2[:, sl * D:(sl + 1) * D], func=ACT.Identity,
                bias=WSM[:, 3:4], scale=scol), "s")
            st["xs"][sl] = ("s", m)
            self.wait("vector", "s", m)
            self.vop(lambda e: e.tensor_scalar(
                out=WQF[:], in0=WQF[:], scalar1=MAGIC, scalar2=-1.0,
                op0=ALU.subtract, op1=ALU.max))
            if self._wprep_parked:
                self.wait("vector", "dP", self._wprep_parked)
            v3 = self.vop(lambda e: e.tensor_scalar_min(
                out=WQB[:], in0=WQF[:], scalar1=1.0))
            st["wqf_free"] = v3
            self.wait("sync", "v", v3)
            self._wprep_parked = self.dma(
                "sync", dst[r0 * 128:(r0 + 1) * 128, q0 * D:(q0 + 1) * D],
                WQB[:], "dP")

        def hook(i, ps):
            if i == 0:
                pass
            elif i <= 32:
                gate_loads()
                for n_ in range(2 * i - 2, 2 * i):
                    comp_a(n_)
                if i == 32:
                    finale(ps)
            else:
                gate_loads(scal=True)
                for n_ in range(2 * i - 2, 2 * i):
                    comp_b(n_)
            for n_ in range(2 * i, min(2 * i + 2, 2 * NPC)):
                load(n_)

        def drain():
            self.wait("vector", "dC", c["dC"].n)
            self.wait("vector", "dG", c["dG"].n)
            self.wait("scalar", "dC", c["dC"].n)
            self.wait("scalar", "dG", c["dG"].n)
            for n_ in range(2 * NPC - 2, 2 * NPC):
                comp_b(n_)

        return hook, drain

    def _l0_ffn(self):
        t, c = self.t, self.c
        X, XQT, HBUF, WDQT, WSTR, TMPF, WQT = (t[k] for k in
            ("X", "XQT", "HBUF", "WDQT", "WSTR", "TMPF", "WQT"))
        TMPB, TMP2, SM, ROW, SC, MU, CONST, ONESF = (t[k] for k in
            ("TMPB", "TMP2", "SM", "ROW", "SC", "MU", "CONST", "ONESF"))
        psA, psB, psX, psY = (self.p[k] for k in ("psA", "psB", "psX", "psY"))
        L0_wg_in, L0_wu_in, L0_wd_in = t["L0_wg_in"], t["L0_wu_in"], t["L0_wd_in"]
        L0_wdT_in, WDSA, WDSB = t["L0_wdT_in"], t["WDSA"], t["WDSB"]
        psPA, psPB = self.p["psPA"], self.p["psPB"]
        L0_clipx, L0_clipd = t["L0_clipx"], t["L0_clipd"]
        xpark = self.dram["xpark"].ap()
        cc_in, cc_out, cc_in2, cc_out2 = self.ccs
        rg = getattr(self, "rg", RG8)
        SWDB = self.sb("SWDB", [128, D], F32, 105 * 1024)
        self.wait("tensor", "s", c["s"].n)
        self.wait("tensor", "v", c["v"].n)

        vloc = self._part_reduce_max(SM[:, 1:2], ROW[0:1, 0:1], psX, CONST)
        self.wait("sync", "v", vloc)
        dm = self.dma("sync", cc_in, ROW[0:1, 0:1], "dS")
        self.wait("gpsimd", "dS", dm)
        gm = self.op("gpsimd", lambda e: e.collective_compute(
            "AllReduce", ALU.max, replica_groups=getattr(self, "rg1", rg),
            ins=[cc_in], outs=[cc_out]), "g")
        self.wait("sync", "g", gm)
        dm = self.dma("sync", ROW[0:1, 1:2], cc_out, "dS")
        dmc = self.dma("sync", ROW[0:1, 2:3], L0_clipx, "dS")
        self.wait("scalar", "dS", dmc)
        sm = self.op("scalar", lambda e: e.activation(
            out=ROW[0:1, 3:4], in_=ROW[0:1, 2:3], func=ACT.Exp, scale=-1.0), "s")
        self.wait("vector", "s", sm)
        self.wait("vector", "dS", dmc)
        self.vop(lambda e: e.tensor_scalar_add(out=ROW[0:1, 3:4], in0=ROW[0:1, 3:4], scalar1=1.0))
        self.vop(lambda e: e.reciprocal(out=ROW[0:1, 4:5], in_=ROW[0:1, 3:4]))
        self.vop(lambda e: e.tensor_mul(out=ROW[0:1, 5:6], in0=ROW[0:1, 4:5], in1=ROW[0:1, 1:2]))
        self.vop(lambda e: e.tensor_scalar_mul(out=ROW[0:1, 5:6], in0=ROW[0:1, 5:6], scalar1=1.0 / 127.0))
        self.recip(ROW[0:1, 6:7], ROW[0:1, 5:6], ROW[0:1, 30:31])
        self.vop(lambda e: e.tensor_scalar_mul(out=ROW[0:1, 7:8], in0=ROW[0:1, 5:6], scalar1=1.0 / 127.0))
        vbc = self._scalar_bcast(ROW[0:1, 5:8], SC[:, 0:3], psY, ONESF, width=3)

        tq_free = {0: 0, 1: 0}
        tq_t_last = 0
        for i in range(NT):
            if tq_t_last:
                self.wait("vector", "t", tq_t_last)
            self.vop(lambda e, i=i: e.tensor_scalar(
                out=TMPF[:, 0:D], in0=X[:, i * D:(i + 1) * D],
                scalar1=SC[:, 1:2], scalar2=MAGIC, op0=ALU.mult, op1=ALU.add))
            self.vop(lambda e: e.tensor_scalar(
                out=TMPF[:, D:2 * D], in0=TMPF[:, 0:D],
                scalar1=MAGIC, scalar2=-128.0, op0=ALU.subtract, op1=ALU.max))
            vq = self.vop(lambda e: e.tensor_scalar_min(
                out=TMPF[:, 0:D], in0=TMPF[:, D:2 * D], scalar1=127.0))
            self.wait("tensor", "v", vq)
            for cp in range(NC):
                pst = psX if cp % 2 == 0 else psY
                self.wait("tensor", "s", tq_free[cp % 2])
                tm = self.op("tensor", lambda e, cp=cp, pst=pst: e.transpose(
                    pst[:, 0:128], TMPF[:, cp * 128:(cp + 1) * 128], CONST[:]), "t")
                tq_t_last = tm
                self.wait("scalar", "t", tm)
                tq_free[cp % 2] = self.op("scalar", lambda e, i=i, cp=cp, pst=pst: e.activation(
                    out=XQT[:, cp * T + i * 128:cp * T + i * 128 + 128],
                    in_=pst[:, 0:128], func=ACT.Identity), "s")
        v_xq = c["v"].n
        s_xq = c["s"].n
        self.x1_parked = self._l0_park
        self.wait("scalar", "v", v_xq)
        self.wait("vector", "v", v_xq)
        self.wait("scalar", "dS", c["dS"].n)
        self.wait("vector", "dS", c["dS"].n)

        wd_slots = [WDSB[:, 0:512], WDSB[:, 512:1024], WDSA[:, 0:512]]
        wd = {"xs": {0: 0, 1: 0, 2: 0}, "done": 0, "sinv": 0}
        self.wait("sync", "dP", self._wprep_parked)
        self.wait("sync", "v", c["v"].n)

        def wd_piece_a(n_):
            dc, q = divmod(n_, 6)
            width = 512 if q < 5 else 256
            return (L0_wd_in[dc * 128:(dc + 1) * 128,
                             q * 512:q * 512 + width], dc, q, width)

        def wd_load(n_):
            sl = n_ % 3
            if wd["xs"].get(sl):
                self.wait("sync", "v", wd["xs"][sl])
            if n_ < 48:
                srcap, dc, q, width = wd_piece_a(n_)
            else:
                m = n_ - 48
                ch, half = divmod(m, 2)
                srcap = L0_wdT_in[ch * 128:(ch + 1) * 128,
                                  half * 512:(half + 1) * 512]
                width = 512
            self.dma("sync", wd_slots[sl][:, 0:width], srcap, "dS")

        def wd_comp(n_):
            sl = n_ % 3
            if n_ < 48:
                dc, q, width = wd_piece_a(n_)[1:]
                self.vop(lambda e, sl=sl, width=width: e.reduce_max(
                    out=SM[:, 4:5], in_=wd_slots[sl][:, 0:width], axis=AX,
                    apply_absolute_value=True))
                if q == 0:
                    wd["xs"][sl] = self.vop(lambda e, dc=dc: e.tensor_copy(
                        out=SM[:, 8 + dc:9 + dc], in_=SM[:, 4:5]))
                else:
                    wd["xs"][sl] = self.vop(lambda e, dc=dc: e.tensor_max(
                        out=SM[:, 8 + dc:9 + dc], in0=SM[:, 8 + dc:9 + dc],
                        in1=SM[:, 4:5]))
            else:
                m = n_ - 48
                ch, half = divmod(m, 2)
                if wd["sinv"]:
                    self.wait("vector", "v", wd["sinv"])
                    wd["sinv"] = 0
                self.vop(lambda e, sl=sl, half=half: e.tensor_mul(
                    out=wd_slots[sl], in0=wd_slots[sl],
                    in1=TMP2[:, half * 512:(half + 1) * 512]))
                wd["done"] = self.vop(lambda e, ch=ch, half=half, sl=sl: e.tensor_scalar(
                    out=WDQT[:, ch * D + half * 512:ch * D + (half + 1) * 512],
                    in0=wd_slots[sl], scalar1=MAGIC, scalar2=MAGIC,
                    op0=ALU.add, op1=ALU.subtract))
                wd["xs"][sl] = wd["done"]

        def wd_chain():
            for dc in range(NC):
                self.wait("tensor", "v", c["v"].n)
                tm = self.op("tensor", lambda e, dc=dc: e.transpose(
                    psPA[0:1, 0:128], SM[:, 8 + dc:9 + dc], CONST[:]), "t")
                self.wait("vector", "t", tm)
                self.vop(lambda e, dc=dc: e.tensor_copy(
                    out=TMP2[0:1, dc * 128:(dc + 1) * 128], in_=psPA[0:1, 0:128]))
            self.vop(lambda e: e.reciprocal(out=TMP2[0:1, 0:D], in_=TMP2[0:1, 0:D]))
            vr = self.vop(lambda e: e.tensor_scalar_mul(
                out=TMP2[0:1, 0:D], in0=TMP2[0:1, 0:D], scalar1=127.0))
            for half in range(2):
                self.wait("tensor", "v", vr if half == 0 else c["v"].n)
                tm = self.op("tensor", lambda e, half=half: e.matmul(
                    psPB[:, 0:512], ONESF[0:1, 0:128],
                    TMP2[0:1, half * 512:(half + 1) * 512],
                    start=True, stop=True), "t")
                self.wait("vector", "t", tm)
                wd["sinv"] = self.vop(lambda e, half=half: e.tensor_copy(
                    out=TMP2[:, half * 512:(half + 1) * 512], in_=psPB[:, 0:512]))

        def wd_hook(hk):
            if hk == 17:
                wd_chain()
            if 1 <= hk <= 31:
                self.wait("vector", "dS", c["dS"].n)
                for n_ in range(3 * hk - 3, min(3 * hk, 92)):
                    wd_comp(n_)
            for n_ in range(3 * hk, min(3 * hk + 3, 92)):
                wd_load(n_)

        self.wait("tensor", "s", s_xq)
        wq_slots = [WQT, TMPB]
        wq_free = [0, 0]
        wstr_free = [0, 0]
        ps_free = {0: 0, 1: 0}
        habs_started = False
        tp_last = tq_t_last
        sc5_safe = 0
        for ic in range(NI0):
            self.wait("sync", "v", wstr_free[0])
            mg = self.dma("sync", WSTR[:, 0:D], L0_wg_in[ic * 128:(ic + 1) * 128, :], "dS")
            self.wait("sync", "v", wstr_free[1])
            mu_ = self.dma("sync", WSTR[:, D:2 * D], L0_wu_in[ic * 128:(ic + 1) * 128, :], "dS")
            self.wait("vector", "dS", mu_)
            for which, wofs in ((0, 0), (1, D)):
                slot = wq_slots[which]
                self.wait("vector", "v", vbc if ic == 0 and which == 0 else 0)
                self.vop(lambda e, wofs=wofs: e.reduce_max(
                    out=SC[:, 3:4], in_=WSTR[:, wofs:wofs + D], axis=AX,
                    apply_absolute_value=True))
                self.recip(SC[:, 4:5], SC[:, 3:4], SC[:, 7:8])
                self.vop(lambda e: e.tensor_scalar_mul(out=SC[:, 4:5], in0=SC[:, 4:5], scalar1=127.0))
                if which == 0 and sc5_safe:
                    self.wait("vector", "s", sc5_safe)
                dqm = self.vop(lambda e, which=which: e.tensor_mul(
                    out=SC[:, 5 + which:6 + which], in0=SC[:, 3:4], in1=SC[:, 2:3]))
                if tp_last:
                    self.wait("vector", "t", tp_last)
                self.vop(lambda e, wofs=wofs: e.tensor_scalar(
                    out=TMPF[:, 0:D], in0=WSTR[:, wofs:wofs + D],
                    scalar1=SC[:, 4:5], scalar2=MAGIC, op0=ALU.mult, op1=ALU.add))
                wsf = self.vop(lambda e: e.tensor_scalar(
                    out=TMPF[:, D:2 * D], in0=TMPF[:, 0:D],
                    scalar1=MAGIC, scalar2=None, op0=ALU.subtract))
                wstr_free[which] = wsf
                self.wait("tensor", "v", wsf)
                self.wait("vector", "t", wq_free[which])
                ev_marks = {}
                for cp in range(NC):
                    pst = psX if cp % 2 == 0 else psY
                    if cp >= 2:
                        self.wait("tensor", "s", ev_marks[cp - 2])
                    tm = self.op("tensor", lambda e, cp=cp, pst=pst: e.transpose(
                        pst[:, 0:128], TMPF[:, D + cp * 128:D + (cp + 1) * 128], CONST[:]), "t")
                    tp_last = tm
                    self.wait("scalar", "t", tm)
                    wvm = self.op("scalar", lambda e, cp=cp, pst=pst, slot=slot: e.activation(
                        out=slot[:, cp * 128:(cp + 1) * 128], in_=pst[:, 0:128],
                        func=ACT.Identity), "s")
                    ev_marks[cp] = wvm
                self.wait("tensor", "s", wvm)
                for tb4 in range(4):
                    pst, pi = (psA, 0) if tb4 % 2 == 0 else (psB, 1)
                    po = (tb4 // 2) * 512
                    if ps_free[pi]:
                        eng, val = ps_free[pi]
                        self.wait("tensor", eng, val)
                    tmm = None
                    for cp in range(NC):
                        tmm = self.op("tensor", lambda e, cp=cp, tb4=tb4, pst=pst, po=po, slot=slot: e.matmul(
                            pst[:, po:po + 512], slot[:, cp * 128:(cp + 1) * 128],
                            XQT[:, cp * T + tb4 * 512:cp * T + (tb4 + 1) * 512],
                            start=(cp == 0), stop=(cp == NC - 1)),
                            tick=("t" if cp == NC - 1 else None))
                    wq_free[which] = tmm
                    col = ic * T + tb4 * 512
                    if which == 0:
                        self.wait("scalar", "t", tmm)
                        self.wait("scalar", "v", dqm)
                        smk = self.op("scalar", lambda e, col=col, pst=pst, po=po: e.activation(
                            out=HBUF[:, col:col + 512], in_=pst[:, po:po + 512],
                            func=ACT.Silu, scale=SC[:, 5:6]), "s")
                        ps_free[pi] = ("s", smk)
                        sc5_safe = smk
                    else:
                        self.wait("vector", "t", tmm)
                        vmk = self.vop(lambda e, col=col, pst=pst, po=po: e.scalar_tensor_tensor(
                            out=HBUF[:, col:col + 512], in0=pst[:, po:po + 512],
                            scalar=SC[:, 6:7], in1=HBUF[:, col:col + 512],
                            op0=ALU.mult, op1=ALU.mult))
                        ps_free[pi] = ("v", vmk)
                if which == 1:
                    self.vop(lambda e, ic=ic: e.reduce_max(
                        out=SM[:, 2:3], in_=HBUF[:, ic * T:(ic + 1) * T], axis=AX,
                        apply_absolute_value=True))
                    if not habs_started:
                        self.vop(lambda e: e.tensor_copy(out=SM[:, 3:4], in_=SM[:, 2:3]))
                        habs_started = True
                    else:
                        self.vop(lambda e: e.tensor_max(out=SM[:, 3:4], in0=SM[:, 3:4], in1=SM[:, 2:3]))
                wd_hook(2 * ic + which)

        vloc = self._part_reduce_max(SM[:, 3:4], ROW[0:1, 8:9], psX, CONST)
        self.wait("sync", "v", vloc)
        dm = self.dma("sync", cc_in2, ROW[0:1, 8:9], "dS")
        self.wait("gpsimd", "dS", dm)
        gm = self.op("gpsimd", lambda e: e.collective_compute(
            "AllReduce", ALU.max, replica_groups=getattr(self, "rg2", rg),
            ins=[cc_in2], outs=[cc_out2]), "g")
        self.wait("sync", "g", gm)
        dm = self.dma("sync", ROW[0:1, 9:10], cc_out2, "dS")
        dmc = self.dma("sync", ROW[0:1, 10:11], L0_clipd, "dS")
        self.wait("scalar", "dS", dmc)
        sm = self.op("scalar", lambda e: e.activation(
            out=ROW[0:1, 11:12], in_=ROW[0:1, 10:11], func=ACT.Exp, scale=-1.0), "s")
        self.wait("vector", "s", sm)
        self.wait("vector", "dS", dmc)
        self.vop(lambda e: e.tensor_scalar_add(out=ROW[0:1, 11:12], in0=ROW[0:1, 11:12], scalar1=1.0))
        self.vop(lambda e: e.reciprocal(out=ROW[0:1, 12:13], in_=ROW[0:1, 11:12]))
        self.vop(lambda e: e.tensor_mul(out=ROW[0:1, 13:14], in0=ROW[0:1, 12:13], in1=ROW[0:1, 9:10]))
        self.vop(lambda e: e.tensor_scalar_mul(out=ROW[0:1, 13:14], in0=ROW[0:1, 13:14], scalar1=1.0 / 127.0))
        self.recip(ROW[0:1, 14:15], ROW[0:1, 13:14], ROW[0:1, 30:31])
        self.vop(lambda e: e.tensor_scalar_mul(out=ROW[0:1, 15:16], in0=ROW[0:1, 13:14], scalar1=1.0 / 127.0))
        vbc2 = self._scalar_bcast(ROW[0:1, 13:16], SC[:, 3:6], psY, ONESF, width=3)

        vmg = self.vop(lambda e: e.memset(SM[:, 7:8], MAGIC))
        self.wait("scalar", "v", vmg)
        slot_free = {0: 0, 1: 0}
        for n_ in range(2 * NI0):
            ic, half = divmod(n_, 2)
            col = ic * T + half * D
            sl = n_ % 2
            if slot_free[sl]:
                self.wait("scalar", "v", slot_free[sl])
            sm_ = self.op("scalar", lambda e, col=col, sl=sl: e.activation(
                out=TMPF[:, sl * D:(sl + 1) * D], in_=HBUF[:, col:col + D],
                func=ACT.Identity, bias=SM[:, 7:8], scale=SC[:, 4:5]), "s")
            self.wait("vector", "s", sm_)
            self.vop(lambda e, sl=sl: e.tensor_scalar(
                out=TMP2[:], in0=TMPF[:, sl * D:(sl + 1) * D],
                scalar1=MAGIC, scalar2=-128.0, op0=ALU.subtract, op1=ALU.max))
            slot_free[sl] = self.vop(lambda e, col=col: e.tensor_scalar_min(
                out=HBUF[:, col:col + D], in0=TMP2[:], scalar1=127.0))
        v_hq = c["v"].n

        wstr_m = wd["done"]
        self.wait("tensor", "v", wstr_m)
        vv = 0
        for dc in range(NC):
            tm = self.op("tensor", lambda e, dc=dc: e.transpose(
                psX[0:1, 0:128], SM[:, 8 + dc:9 + dc], CONST[:]), "t")
            self.wait("vector", "t", tm)
            vv = self.vop(lambda e, dc=dc: e.tensor_copy(
                out=TMP2[0:1, dc * 128:(dc + 1) * 128], in_=psX[0:1, 0:128]))
            self.wait("tensor", "v", vv)
        for half in range(2):
            tm = self.op("tensor", lambda e, half=half: e.matmul(
                psY[:, 0:512], ONESF[0:1, 0:128], TMP2[0:1, half * 512:(half + 1) * 512],
                start=True, stop=True), "t")
            self.wait("vector", "t", tm)
            vv = self.vop(lambda e, half=half: e.tensor_copy(
                out=SWDB[:, half * 512:(half + 1) * 512], in_=psY[:, 0:512]))
            self.wait("tensor", "v", vv)
        vsw = self.vop(lambda e: e.tensor_scalar_mul(
            out=SWDB[:], in0=SWDB[:], scalar1=SC[:, 5:6]))

        xr_m = 0
        ps_free2 = {0: 0, 1: 0}
        self.x2_tiles = []
        for tk in range(NT):
            pst, pi = (psA, 0) if tk % 2 == 0 else (psB, 1)
            if ps_free2[pi]:
                self.wait("tensor", "v", ps_free2[pi])
            if tk == 0:
                self.wait("tensor", "v", max(v_hq, vsw, wd["done"]))
            tmm = None
            for ic in range(NI0):
                for half in range(2):
                    tmm = self.op("tensor", lambda e, ic=ic, tk=tk, half=half, pst=pst: e.matmul(
                        pst[:, half * 512:(half + 1) * 512],
                        HBUF[:, ic * T + tk * 128:ic * T + tk * 128 + 128],
                        WDQT[:, ic * D + half * 512:ic * D + (half + 1) * 512],
                        start=(ic == 0), stop=(ic == NI0 - 1)),
                        tick=("t" if (ic == NI0 - 1 and half == 1) else None))
            if tk > 0:
                self.wait("vector", "dS", self._x2_park_last)
            if tk == 0:
                self.wait("sync", "v", vsw)
            if xr_m:
                self.wait("sync", "v", xr_m)
            dm = self.dma("sync", TMPF[:, 0:D], xpark[tk * 128:(tk + 1) * 128, :], "dS")
            self.wait("vector", "t", tmm)
            self.vop(lambda e, pst=pst: e.tensor_mul(
                out=TMP2[:], in0=pst[:, 0:D], in1=SWDB[:]))
            self.wait("vector", "dS", dm)
            xr_m = self.vop(lambda e: e.tensor_add(
                out=TMPF[:, D:2 * D], in0=TMP2[:], in1=TMPF[:, 0:D]))
            ps_free2[pi] = xr_m
            self._layernorm_X(lambda i: TMPF[:, D:2 * D], [0], WSTR[:, 0:D])
            vln = c["v"].n
            self.wait("sync", "v", vln)
            self._x2_park_last = self.dma("sync", xpark[tk * 128:(tk + 1) * 128, :], TMPF[:, D:2 * D], "dS")
        self.x2_parked = c["dS"].n

    def _rope(self, view_of, cview, sview, t1, t2, t3, t4, inverse=False):
        e_, o_ = view_of(0), view_of(1)
        self.vop(lambda e: e.tensor_mul(out=t1, in0=o_, in1=sview))
        self.vop(lambda e: e.tensor_mul(out=t2, in0=o_, in1=cview))
        self.vop(lambda e: e.tensor_mul(out=t3, in0=e_, in1=sview))
        self.vop(lambda e: e.tensor_mul(out=t4, in0=e_, in1=cview))
        if not inverse:
            self.vop(lambda e: e.tensor_add(out=o_, in0=t3, in1=t2))
            return self.vop(lambda e: e.tensor_sub(out=e_, in0=t4, in1=t1))
        else:
            self.vop(lambda e: e.tensor_sub(out=e_, in0=t1, in1=t4))
            return self.vop(lambda e: e.scalar_tensor_tensor(
                out=o_, in0=t3, scalar=-1.0, in1=t2, op0=ALU.mult, op1=ALU.subtract))

    def _rope_io(self, in_of, out_of, cview, sview, t1, t2, t3, t4):
        ei, oi = in_of(0), in_of(1)
        eo, oo = out_of(0), out_of(1)
        self.vop(lambda e: e.tensor_mul(out=t1, in0=oi, in1=sview))
        self.vop(lambda e: e.tensor_mul(out=t2, in0=oi, in1=cview))
        self.vop(lambda e: e.tensor_mul(out=t3, in0=ei, in1=sview))
        m4 = self.vop(lambda e: e.tensor_mul(out=t4, in0=ei, in1=cview))
        self.vop(lambda e: e.tensor_add(out=oo, in0=t3, in1=t2))
        done = self.vop(lambda e: e.tensor_sub(out=eo, in0=t4, in1=t1))
        return m4, done

    def _rope_out(self, ybase, cbase, sbase, t1, t2, t3, t4):
        yp = ybase.rearrange("p (x two) -> p x two", two=2)
        av, bv = yp[:, :, 0:1], yp[:, :, 1:2]
        r3 = lambda t: t.rearrange("p (x one) -> p x one", one=1)
        c3, s3 = r3(cbase), r3(sbase)
        self.vop(lambda e: e.tensor_mul(out=r3(t1), in0=bv, in1=s3))
        self.vop(lambda e: e.tensor_mul(out=r3(t2), in0=bv, in1=c3))
        self.vop(lambda e: e.tensor_mul(out=r3(t3), in0=av, in1=s3))
        self.vop(lambda e: e.tensor_mul(out=r3(t4), in0=av, in1=c3))
        yh = ybase.rearrange("p (h d) -> p h d", d=HD)
        th = lambda t: t.rearrange("p (h d) -> p h d", d=32)
        self.vop(lambda e: e.tensor_sub(out=yh[:, :, 0:32], in0=th(t1), in1=th(t4)))
        return self.vop(lambda e: e.scalar_tensor_tensor(
            out=yh[:, :, 32:64], in0=th(t3), scalar=-1.0, in1=th(t2),
            op0=ALU.mult, op1=ALU.subtract))

    def _l1_attn(self):
        t, c = self.t, self.c
        XT, KF, VT, QSTG, XSTG, WRES, WORES, TMPB, ROPT = (t[k] for k in
            ("XT", "KF", "VT", "QSTG", "XSTG", "WRES", "WORES", "TMPB", "ROPT"))
        ROPEC, ROPES, BROWS, CONST, ONESB, TMP2 = (t[k] for k in
            ("ROPEC", "ROPES", "BROWS", "CONST", "ONESB", "TMP2"))
        psA, psB, psX, psY = (self.p[k] for k in ("psA", "psB", "psX", "psY"))
        L1_WT, L1_brow = t["L1_WT"], t["L1_brow"]
        xpark = self.dram["xpark"].ap()
        qpark = self.dram["qpark"].ap()

        self.wait("tensor", "s", c["s"].n)
        self.wait("tensor", "v", c["v"].n)
        self.wait("gpsimd", "dS", self.x2_parked)
        for i in range(NT):
            self.dma("gpsimd", ROPEC[:, i * 512:(i + 1) * 512],
                     t["ropeC_in"][i * 128:(i + 1) * 128, :], "dG")
            self.dma("gpsimd", ROPES[:, i * 512:(i + 1) * 512],
                     t["ropeS_in"][i * 128:(i + 1) * 128, :], "dG")
        for w_i, knm in enumerate("qkvo"):
            self.dma("gpsimd", BROWS[0:1, w_i * D:(w_i + 1) * D], L1_brow[knm], "dG")
        g_tabs = c["dG"].n

        xs_free = {0: 0, 1: 0}
        ev_marks = {}
        for i in range(NT):
            sl = i % 2
            if xs_free[sl]:
                self.wait("sync", "t", xs_free[sl])
            dm = self.dma("sync", XSTG[:, sl * D:(sl + 1) * D],
                          xpark[i * 128:(i + 1) * 128, :], "dS")
            self.wait("tensor", "dS", dm)
            for cp in range(NC):
                kk = i * NC + cp
                pst = psX if kk % 2 == 0 else psY
                if ev_marks.get(kk - 2):
                    self.wait("tensor", "s", ev_marks[kk - 2])
                tm = self.op("tensor", lambda e, sl=sl, cp=cp, pst=pst: e.transpose(
                    pst[:, 0:128], XSTG[:, sl * D + cp * 128:sl * D + cp * 128 + 128],
                    CONST[:]), "t")
                self.wait("scalar", "t", tm)
                ev_marks[kk] = self.op("scalar", lambda e, i=i, cp=cp, pst=pst: e.activation(
                    out=XT[:, cp * T + i * 128:cp * T + i * 128 + 128],
                    in_=pst[:, 0:128], func=ACT.Identity), "s")
            xs_free[sl] = tm
        v_xt = c["s"].n

        self.wait("tensor", "s", v_xt)
        ps_ev = {0: 0, 1: 0}
        self._l1_last_dv = 0
        self._l1_last_park = 0
        for w_i, knm in enumerate("qkv"):
            self.wait("gpsimd", "dG", g_tabs)
            if w_i > 0:
                self.wait("gpsimd", "t", self._wres_used)
            for cp in range(NC):
                self.dma("gpsimd", WRES[:, cp * D:(cp + 1) * D],
                         L1_WT[knm][cp * 128:(cp + 1) * 128, :], "dG")
            gw = c["dG"].n
            self.wait("tensor", "dG", gw)
            for tk in range(NT):
                pst, pi = (psA, 0) if tk % 2 == 0 else (psB, 1)
                if ps_ev[pi]:
                    self.wait("tensor", "v", ps_ev[pi])
                for cp in range(NC):
                    self.op("tensor", lambda e, cp=cp, tk=tk, pst=pst: e.matmul(
                        pst[:, 0:512], XT[:, cp * T + tk * 128:cp * T + tk * 128 + 128],
                        WRES[:, cp * D:cp * D + 512], start=(cp == 0), stop=False), None)
                    self.op("tensor", lambda e, cp=cp, tk=tk, pst=pst: e.matmul(
                        pst[:, 512:1024], XT[:, cp * T + tk * 128:cp * T + tk * 128 + 128],
                        WRES[:, cp * D + 512:(cp + 1) * D], start=(cp == 0), stop=False), None)
                self.op("tensor", lambda e, pst=pst, w_i=w_i: e.matmul(
                    pst[:, 0:512], ONESB[0:1, 0:128], BROWS[0:1, w_i * D:w_i * D + 512],
                    start=False, stop=True), None)
                tm = self.op("tensor", lambda e, pst=pst, w_i=w_i: e.matmul(
                    pst[:, 512:1024], ONESB[0:1, 0:128],
                    BROWS[0:1, w_i * D + 512:(w_i + 1) * D],
                    start=False, stop=True), "t")
                self._wres_used = tm
                self.wait("vector", "t", tm)
                pv_h = pst[:, 0:1024].rearrange("p (h d) -> p h d", h=NH)
                cv = ROPEC[:, tk * 512:(tk + 1) * 512].rearrange("p (h d) -> p h d", d=32)
                sv = ROPES[:, tk * 512:(tk + 1) * 512].rearrange("p (h d) -> p h d", d=32)
                rt = (ROPT[:, 0:512].rearrange("p (h d) -> p h d", d=32),
                      ROPT[:, 512:1024].rearrange("p (h d) -> p h d", d=32),
                      TMPB[:, 0:512].rearrange("p (h d) -> p h d", d=32),
                      TMPB[:, 512:1024].rearrange("p (h d) -> p h d", d=32))
                if knm == "v":
                    vt_tile = VT[:, tk * NH * 65:(tk + 1) * NH * 65].rearrange(
                        "p (h d) -> p h d", h=NH)
                    self.vop(lambda e, vt=vt_tile: e.memset(vt[:, :, 64:65], 1.0))
                    if getattr(self, "old_rope", False):
                        self.vop(lambda e, pst=pst, vt=vt_tile: e.tensor_copy(
                            out=vt[:, :, 0:64], in_=pst[:, 0:1024].rearrange(
                                "p (h d) -> p h d", h=NH)))
                        ps_ev[pi] = self._rope(
                            lambda half, vt=vt_tile: vt[:, :, half * 32:(half + 1) * 32],
                            cv, sv, *rt)
                    else:
                        ps_ev[pi], _ = self._rope_io(
                            lambda half, pv=pv_h: pv[:, :, half * 32:(half + 1) * 32],
                            lambda half, vt=vt_tile: vt[:, :, half * 32:(half + 1) * 32],
                            cv, sv, *rt)
                else:
                    if self._l1_last_dv:
                        self.wait("vector", "dC", self._l1_last_dv)
                        if self.q2[0] != "vector":
                            self.wait("vector", self.q2[1], c[self.q2[1]].n)
                    qv = QSTG[:, 0:D].rearrange("p (h d) -> p h d", h=NH)
                    if getattr(self, "old_rope", False):
                        ev = self.vop(lambda e, pst=pst: e.tensor_copy(
                            out=QSTG[:, 0:D], in_=pst[:, 0:1024]))
                        rm = self._rope(
                            lambda half, qv=qv: qv[:, :, half * 32:(half + 1) * 32],
                            cv, sv, *rt)
                        ps_ev[pi] = ev
                    else:
                        ps_ev[pi], rm = self._rope_io(
                            lambda half, pv=pv_h: pv[:, :, half * 32:(half + 1) * 32],
                            lambda half, qv=qv: qv[:, :, half * 32:(half + 1) * 32],
                            cv, sv, *rt)
                    if knm == "k":
                        self.wait("scalar", "v", rm)
                        kfv = KF.rearrange("p (c t) -> p c t", c=NC)[
                            :, :, tk * 128:(tk + 1) * 128]
                        self.dmat("scalar", kfv, QSTG[:, 0:D], "dC")
                        self._l1_last_dv = c["dC"].n
                    else:
                        self.wait("scalar", "v", rm)
                        if self._l1_last_park:
                            self.wait("scalar", "dS", self._l1_last_park)
                        self.dmat("scalar",
                                  QSTG[:, D:2 * D].rearrange("p (c f) -> p c f", c=NC),
                                  QSTG[:, 0:D], "dC")
                        self._l1_last_dv = c["dC"].n
                        self.wait("sync", "dC", c["dC"].n)
                        for cp in range(NC):
                            self._l1_last_park = self.dma(
                                "sync",
                                qpark[cp * 128:(cp + 1) * 128, tk * 128:(tk + 1) * 128],
                                QSTG[:, D + cp * 128:D + (cp + 1) * 128], "dS")
        d_qpark = c["dS"].n
        t_proj = c["t"].n
        v_proj = c["v"].n

        self.wait("gpsimd", "t", t_proj)
        for cp in range(NC):
            self.dma("gpsimd", WORES[:, cp * D:(cp + 1) * D],
                     L1_WT["o"][cp * 128:(cp + 1) * 128, :], "dG")
        self._g_wo = c["dG"].n

        self._attention(layer=1, start_after={
            "sync": [("dS", d_qpark), ("dC", c["dC"].n), ("t", t_proj)],
            "tensor": [("v", v_proj), ("s", c["s"].n), ("dC", c["dC"].n),
                       ("dS", d_qpark), (self.q2[1], c[self.q2[1]].n)]})
        self.x3_parked = self._x3_last_park

    def _l0_post_tile(self, b, j):
        t, c = self.t, self.c
        X, TMP2, SM, MUX = t["X"], t["TMP2"], t["SM"], t["MUX"]
        ti = 4 * b + j
        xpark = self.dram["xpark"].ap()
        self.wait("vector", "v", c["v"].n)
        self._layernorm_X(lambda i: X[:, ti * D:(ti + 1) * D], [0], TMP2,
                          mu_of=lambda i, j=j: MUX[:, 4 * j:4 * j + 4])
        ln_m = self._ln_done[0]
        self.wait("vector", "v", ln_m)
        self.vop(lambda e, ti=ti: e.reduce_max(
            out=SM[:, 0:1], in_=X[:, ti * D:(ti + 1) * D], axis=AX,
            apply_absolute_value=True))
        if ti == 0:
            self.vop(lambda e: e.tensor_copy(out=SM[:, 1:2], in_=SM[:, 0:1]))
        else:
            self.vop(lambda e: e.tensor_max(
                out=SM[:, 1:2], in0=SM[:, 1:2], in1=SM[:, 0:1]))
        self.wait("sync", "v", ln_m)
        self._l0_park = self.dma(
            "sync", xpark[ti * 128:(ti + 1) * 128, :], X[:, ti * D:(ti + 1) * D],
            "dS")

    def _l1_post_stage1(self, b, j):
        t, c = self.t, self.c
        YTM, XPOSE, XPO4, ROPT, TMPB = (t[k] for k in
            ("YTM", "XPOSE", "XPO4", "ROPT", "TMPB"))
        ROPEC, ROPES = t["ROPEC"], t["ROPES"]
        tk = 4 * b + j
        self.wait("vector", "v", self.c["v"].n)
        rope_done = self._rope_out(
            YTM[:, j * D:(j + 1) * D],
            ROPEC[:, tk * 512:(tk + 1) * 512],
            ROPES[:, tk * 512:(tk + 1) * 512],
            ROPT[:, 0:512], ROPT[:, 512:1024],
            TMPB[:, 0:512], TMPB[:, 512:1024])
        xpo = XPO4[:, j * D:(j + 1) * D] if j < 3 else XPOSE[:, 0:D]
        self.wait("scalar", "v", rope_done)
        if b > 0:
            self.wait("scalar", "t", self._post_wo[j])
        self.dmat("scalar", xpo.rearrange("p (c f) -> p c f", c=NC),
                  YTM[:, j * D:(j + 1) * D], "dC")
        self._post_dm[j] = (c["dC"].n, c[self.q2[1]].n)

    def _l1_block_post(self, b):
        t, c = self.t, self.c
        XPOSE, XPO4, WORES, XSTG, YS2, TMP2 = (t[k] for k in
            ("XPOSE", "XPO4", "WORES", "XSTG", "YS2", "TMP2"))
        ONESB, BROWS = t["ONESB"], t["BROWS"]
        psA, psB = self.p["psA"], self.p["psB"]
        xpark = self.dram["xpark"].ap()
        bufs = [XSTG[:, 0:D], XSTG[:, D:2 * D], YS2[:, 0:D], YS2[:, D:2 * D]]

        if b == 0:
            self.wait("tensor", "dG", self._g_wo)
        dmc, dms = self._post_dm[3]
        self.wait("tensor", "dC", dmc)
        self.wait("tensor", self.q2[1], dms)
        self.wait("sync", "dC", dmc)
        self.wait("sync", self.q2[1], dms)
        if self._post_park[3]:
            self.wait("sync", "dP", c["dP"].n)
        rld = []
        for j in range(4):
            tk = 4 * b + j
            rld.append(self.dma("sync", bufs[j],
                                xpark[tk * 128:(tk + 1) * 128, :], "dS"))
        for j in range(4):
            xpo = XPO4[:, j * D:(j + 1) * D] if j < 3 else XPOSE[:, 0:D]
            pst, pi = (psA, 0) if j % 2 == 0 else (psB, 1)
            if self._post_ps_free[pi]:
                self.wait("tensor", "v", self._post_ps_free[pi])
            for cp in range(NC):
                self.op("tensor", lambda e, cp=cp, pst=pst, xpo=xpo: e.matmul(
                    pst[:, 0:512], xpo[:, cp * 128:(cp + 1) * 128],
                    WORES[:, cp * D:cp * D + 512], start=(cp == 0), stop=False), None)
                self.op("tensor", lambda e, cp=cp, pst=pst, xpo=xpo: e.matmul(
                    pst[:, 512:1024], xpo[:, cp * 128:(cp + 1) * 128],
                    WORES[:, cp * D + 512:(cp + 1) * D], start=(cp == 0), stop=False), None)
            self.op("tensor", lambda e, pst=pst: e.matmul(
                pst[:, 0:512], ONESB[0:1, 0:128], BROWS[0:1, 3 * D:3 * D + 512],
                start=False, stop=True), None)
            tm = self.op("tensor", lambda e, pst=pst: e.matmul(
                pst[:, 512:1024], ONESB[0:1, 0:128], BROWS[0:1, 3 * D + 512:4 * D],
                start=False, stop=True), "t")
            self._post_wo[j] = tm
            self.wait("vector", "t", tm)
            if j == 0:
                self.wait("vector", "dS", rld[3])
            self._post_ps_free[pi] = self.vop(lambda e, pst=pst, j=j: e.tensor_add(
                out=bufs[j], in0=pst[:, 0:D], in1=bufs[j]))
        MUX = t["MUX"]
        self._layernorm_X(lambda j: bufs[j], range(4), TMP2,
                          mu_of=lambda j: MUX[:, 4 * j:4 * j + 4])
        for j in range(4):
            tk = 4 * b + j
            self.wait("sync", "v", self._ln_done[j])
            self._post_park[j] = self.dma(
                "sync", xpark[tk * 128:(tk + 1) * 128, :], bufs[j], "dP")
        self._x3_last_park = self._post_park[3]
        self.wait("vector", "dC", self.c["dC"].n)
        if self.q2[0] != "vector":
            self.wait("vector", self.q2[1], self.c[self.q2[1]].n)
        self.wait("vector", "dP", self._post_park[3])

    def _l1_ffn(self):
        t, c = self.t, self.c
        W2QT, XQ1T, H1S, ABSA, HTOK, RTOK, W1C = (t[k] for k in
            ("W2QT", "XQ1T", "H1S", "ABSA", "HTOK", "RTOK", "W1C"))
        CWCOL, CBCOL, TMP2, TMPB, SM, ROW, SC, CONST, ONESF, MU = (t[k] for k in
            ("CWCOL", "CBCOL", "TMP2", "TMPB", "SM", "ROW", "SC", "CONST", "ONESF", "MU"))
        psA, psB, psX, psY = (self.p[k] for k in ("psA", "psB", "psX", "psY"))
        w1T_in, w2T_in = t["L1_w1T_in"], t["L1_w2T_in"]
        xpark = self.dram["xpark"].ap()
        w1q_park = self.dram["w1q_park"].ap()
        w2q_park = self.dram["w2q_park"].ap()
        y_out = t["y_out"]
        XS2 = self.sb("XS2", [128, 2 * D], F32, 186 * 1024)

        self.wait("tensor", "s", c["s"].n)
        self.wait("tensor", "v", c["v"].n)
        self.wait("sync", "v", c["v"].n)
        self.wait("sync", "t", c["t"].n)
        self.wait("sync", "s", c["s"].n)
        dcw = self.dma("sync", CWCOL[:], t["L1_cw_in"], "dS")
        dcb = self.dma("sync", CBCOL[:], t["L1_cb_in"], "dS")
        WSC = t["WSC"]
        self.wait("sync", "dP", c["dP"].n)

        self.wait("gpsimd", "v", c["v"].n)
        self.wait("gpsimd", "s", c["s"].n)
        self.wait("gpsimd", "t", c["t"].n)
        self.wait("gpsimd", "dS", c["dS"].n)
        self.wait("gpsimd", "dP", c["dP"].n)
        for ch in range(NH1):
            self.dma("gpsimd", W2QT[:, ch * D:(ch + 1) * D],
                     w2q_park[ch * 128:(ch + 1) * 128, :], "dG")
        g_w2 = c["dG"].n

        self.wait("sync", "v", c["v"].n)
        tq_free = {0: 0, 1: 0}
        tq_t_last = 0
        xs_rd = {0: 0, 1: 0}
        for i in range(NT):
            sl = i % 2
            if xs_rd[sl]:
                self.wait("sync", "v", xs_rd[sl])
            dm = self.dma("sync", XS2[:, sl * D:(sl + 1) * D],
                          xpark[i * 128:(i + 1) * 128, :], "dS")
            self.wait("vector", "dS", dm)
            self.vop(lambda e, sl=sl: e.reduce_max(
                out=SM[:, 4:5], in_=XS2[:, sl * D:(sl + 1) * D], axis=AX,
                apply_absolute_value=True))
            self.vop(lambda e: e.tensor_scalar_max(
                out=SM[:, 4:5], in0=SM[:, 4:5], scalar1=1e-5))
            self.recip(SC[:, 2:3], SM[:, 4:5], SC[:, 7:8])
            self.vop(lambda e: e.tensor_scalar_mul(
                out=SC[:, 2:3], in0=SC[:, 2:3], scalar1=127.0))
            self.vop(lambda e, i=i: e.tensor_scalar_mul(
                out=SM[:, 16 + i:17 + i], in0=SM[:, 4:5], scalar1=1.0 / 127.0))
            self.vop(lambda e, i=i: e.tensor_mul(
                out=SM[:, 16 + i:17 + i], in0=SM[:, 16 + i:17 + i], in1=WSC[:, 2:3]))
            if tq_t_last:
                self.wait("vector", "t", tq_t_last)
            self.vop(lambda e, sl=sl: e.tensor_scalar(
                out=TMP2[:], in0=XS2[:, sl * D:(sl + 1) * D],
                scalar1=SC[:, 2:3], scalar2=MAGIC, op0=ALU.mult, op1=ALU.add))
            vq = self.vop(lambda e: e.tensor_scalar(
                out=TMP2[:], in0=TMP2[:], scalar1=MAGIC, scalar2=None,
                op0=ALU.subtract))
            xs_rd[sl] = vq
            self.wait("tensor", "v", vq)
            for cp in range(NC):
                pst = psX if cp % 2 == 0 else psY
                self.wait("tensor", "s", tq_free[cp % 2])
                tm = self.op("tensor", lambda e, cp=cp, pst=pst: e.transpose(
                    pst[:, 0:128], TMP2[:, cp * 128:(cp + 1) * 128], CONST[:]), "t")
                tq_t_last = tm
                self.wait("scalar", "t", tm)
                tq_free[cp % 2] = self.op("scalar", lambda e, i=i, cp=cp, pst=pst: e.activation(
                    out=XQ1T[:, cp * T + i * 128:cp * T + i * 128 + 128],
                    in_=pst[:, 0:128], func=ACT.Identity), "s")
        rt_marks = 0
        self.wait("tensor", "s", c["s"].n)
        for i in range(NT):
            self.wait("tensor", "v", c["v"].n)
            tm = self.op("tensor", lambda e, i=i: e.transpose(
                psX[0:1, 0:128], SM[:, 16 + i:17 + i], CONST[:]), "t")
            self.wait("vector", "t", tm)
            vv = self.vop(lambda e: e.tensor_copy(out=TMP2[0:1, 0:128], in_=psX[0:1, 0:128]))
            self.wait("tensor", "v", vv)
            tm2 = self.op("tensor", lambda e: e.matmul(
                psY[:, 0:128], ONESF[0:1, 0:128], TMP2[0:1, 0:128],
                start=True, stop=True), "t")
            self.wait("vector", "t", tm2)
            rt_marks = self.vop(lambda e, i=i: e.tensor_copy(
                out=RTOK[:, i * 128:(i + 1) * 128], in_=psY[:, 0:128]))

        self.wait("vector", "dS", c["dS"].n)
        self.wait("scalar", "dS", c["dS"].n)

        self.y_stored = 0
        w1c_free = {0: 0, 1: 0}
        ps_done = 0
        ps_done_p = {0: 0, 1: 0}
        halo_done = 0
        for tau in range(2):
            if tau == 1:
                self.wait("vector", "out", self._y_park_last)
                self.wait("vector", "dS", c["dS"].n)
            habs_first = True
            for j in range(NH1):
                sl = j % 2
                if w1c_free[sl]:
                    self.wait("sync", "t", w1c_free[sl])
                for cp in range(NC):
                    self.dma("sync", W1C[:, sl * D + cp * 128:sl * D + (cp + 1) * 128],
                             w1q_park[cp * 128:(cp + 1) * 128, j * 128:(j + 1) * 128], "dS")
                dmw = c["dS"].n
                self.wait("tensor", "dS", dmw)
                if j == 0:
                    self.wait("tensor", "v", c["v"].n)
                    self.wait("tensor", "s", c["s"].n)
                pst, pi = (psA, 0) if j % 2 == 0 else (psB, 1)
                if ps_done_p[pi]:
                    self.wait("tensor", "v", ps_done_p[pi])
                tmm = None
                for cp in range(NC):
                    for half in range(2):
                        tmm = self.op("tensor", lambda e, cp=cp, half=half, tau=tau, sl=sl, pst=pst: e.matmul(
                            pst[:, half * 512:(half + 1) * 512],
                            W1C[:, sl * D + cp * 128:sl * D + (cp + 1) * 128],
                            XQ1T[:, cp * T + tau * 1024 + half * 512:cp * T + tau * 1024 + (half + 1) * 512],
                            start=(cp == 0), stop=(cp == NC - 1)),
                            tick=("t" if (cp == NC - 1 and half == 1) else None))
                if halo_done:
                    self.wait("tensor", "v", halo_done)
                halo_tok = tau * 1024 - 1 if tau == 1 else 1024
                tmh = None
                for cp in range(NC):
                    tmh = self.op("tensor", lambda e, cp=cp, sl=sl, ht=halo_tok: e.matmul(
                        psX[:, 0:1],
                        W1C[:, sl * D + cp * 128:sl * D + (cp + 1) * 128],
                        XQ1T[:, cp * T + ht:cp * T + ht + 1],
                        start=(cp == 0), stop=(cp == NC - 1)),
                        tick=("t" if cp == NC - 1 else None))
                w1c_free[sl] = tmh
                self.wait("vector", "t", tmh)
                hrow = j * 1026
                ps_done_p[pi] = self.vop(lambda e, hrow=hrow, tau=tau, pst=pst: e.tensor_mul(
                    out=H1S[:, hrow + 1:hrow + 1025], in0=pst[:, 0:1024],
                    in1=RTOK[:, tau * 1024:(tau + 1) * 1024]))
                hcol = hrow + (0 if tau == 1 else 1025)
                halo_done = self.vop(lambda e, hcol=hcol, ht=halo_tok: e.tensor_mul(
                    out=H1S[:, hcol:hcol + 1], in0=psX[:, 0:1],
                    in1=RTOK[:, ht:ht + 1]))
                zcol = hrow + (1025 if tau == 1 else 0)
                ps_done = self.vop(lambda e, zcol=zcol: e.memset(
                    H1S[:, zcol:zcol + 1], 0.0))
                self.vop(lambda e, hrow=hrow, j=j: e.tensor_scalar_mul(
                    out=TMPB[:], in0=H1S[:, hrow:hrow + 1024],
                    scalar1=CWCOL[:, j * 3:j * 3 + 1]))
                self.vop(lambda e, hrow=hrow, j=j: e.scalar_tensor_tensor(
                    out=TMPB[:], in0=H1S[:, hrow + 1:hrow + 1025],
                    scalar=CWCOL[:, j * 3 + 1:j * 3 + 2], in1=TMPB[:],
                    op0=ALU.mult, op1=ALU.add))
                vcv = self.vop(lambda e, hrow=hrow, j=j: e.scalar_tensor_tensor(
                    out=TMPB[:], in0=H1S[:, hrow + 2:hrow + 1026],
                    scalar=CWCOL[:, j * 3 + 2:j * 3 + 3], in1=TMPB[:],
                    op0=ALU.mult, op1=ALU.add))
                self.wait("scalar", "v", vcv)
                smk = self.op("scalar", lambda e, hrow=hrow, j=j: e.activation(
                    out=H1S[:, hrow + 1:hrow + 1025], in_=TMPB[:], func=ACT.Silu,
                    bias=CBCOL[:, j:j + 1], scale=1.0), "s")
                self.wait("vector", "s", smk)
                if habs_first:
                    self.vop(lambda e, hrow=hrow: e.tensor_copy(
                        out=ABSA[:], in_=H1S[:, hrow + 1:hrow + 1025]))
                    self.vop(lambda e, hrow=hrow: e.tensor_copy(
                        out=HTOK[:], in_=H1S[:, hrow + 1:hrow + 1025]))
                    habs_first = False
                else:
                    self.vop(lambda e, hrow=hrow: e.tensor_max(
                        out=ABSA[:], in0=ABSA[:], in1=H1S[:, hrow + 1:hrow + 1025]))
                    self.vop(lambda e, hrow=hrow: e.tensor_tensor(
                        out=HTOK[:], in0=HTOK[:], in1=H1S[:, hrow + 1:hrow + 1025],
                        op=ALU.min))
            self.vop(lambda e: e.tensor_scalar_mul(out=HTOK[:], in0=HTOK[:], scalar1=-1.0))
            self.vop(lambda e: e.tensor_max(out=ABSA[:], in0=ABSA[:], in1=HTOK[:]))
            for cc8 in range(NC):
                self.wait("tensor", "v", c["v"].n)
                tm = self.op("tensor", lambda e, cc8=cc8: e.transpose(
                    psX[:, 0:128], ABSA[:, cc8 * 128:(cc8 + 1) * 128], CONST[:]), "t")
                self.wait("vector", "t", tm)
                self.vop(lambda e: e.reduce_max(
                    out=SM[:, 5:6], in_=psX[:, 0:128], axis=AX))
                self.vop(lambda e: e.tensor_scalar_max(
                    out=SM[:, 5:6], in0=SM[:, 5:6], scalar1=1e-5))
                self.recip(SC[:, 3:4], SM[:, 5:6], SC[:, 7:8])
                self.vop(lambda e: e.tensor_scalar_mul(
                    out=SC[:, 3:4], in0=SC[:, 3:4], scalar1=127.0))
                self.vop(lambda e, cc8=cc8: e.tensor_scalar_mul(
                    out=SM[:, 24 + cc8:25 + cc8], in0=SM[:, 5:6], scalar1=1.0 / 127.0))
                self.vop(lambda e, cc8=cc8: e.tensor_mul(
                    out=SM[:, 24 + cc8:25 + cc8], in0=SM[:, 24 + cc8:25 + cc8],
                    in1=WSC[:, 3:4]))
                vv = self.vop(lambda e: e.tensor_copy(out=SM[:, 6:7], in_=SC[:, 3:4]))
                self.wait("tensor", "v", vv)
                tm2 = self.op("tensor", lambda e: e.transpose(
                    psX[0:1, 0:128], SM[:, 6:7], CONST[:]), "t")
                self.wait("vector", "t", tm2)
                vv = self.vop(lambda e: e.tensor_copy(out=TMP2[0:1, 0:128], in_=psX[0:1, 0:128]))
                self.wait("tensor", "v", vv)
                tm3 = self.op("tensor", lambda e: e.matmul(
                    psY[:, 0:128], ONESF[0:1, 0:128], TMP2[0:1, 0:128],
                    start=True, stop=True), "t")
                self.wait("vector", "t", tm3)
                self.vop(lambda e, cc8=cc8: e.tensor_copy(
                    out=HTOK[:, cc8 * 128:(cc8 + 1) * 128], in_=psY[:, 0:128]))
            hq_marks = {}
            for j in range(NH1):
                hrow = j * 1026
                self.vop(lambda e, hrow=hrow: e.tensor_mul(
                    out=TMP2[:], in0=H1S[:, hrow + 1:hrow + 1025], in1=HTOK[:]))
                hq_marks[j] = self.vop(lambda e, hrow=hrow: e.tensor_scalar(
                    out=H1S[:, hrow + 1:hrow + 1025], in0=TMP2[:],
                    scalar1=MAGIC, scalar2=MAGIC, op0=ALU.add, op1=ALU.subtract))
            v_hq = c["v"].n
            ps_done = max(ps_done_p[0], ps_done_p[1])
            if tau == 0:
                self.wait("tensor", "dG", g_w2)
            xr_m = 0
            for lc in range(8):
                tk = tau * 8 + lc
                pst, pi = (psA, 0) if lc % 2 == 0 else (psB, 1)
                if ps_done:
                    self.wait("tensor", "v", ps_done)
                    ps_done = 0
                if getattr(self, "_w2_ps_free", {}).get(pi):
                    self.wait("tensor", "v", self._w2_ps_free[pi])
                tmm = None
                for j in range(NH1):
                    if lc == 0:
                        self.wait("tensor", "v", hq_marks[j])
                    for half in range(2):
                        tmm = self.op("tensor", lambda e, j=j, lc=lc, half=half, pst=pst: e.matmul(
                            pst[:, half * 512:(half + 1) * 512],
                            H1S[:, j * 1026 + 1 + lc * 128:j * 1026 + 1 + (lc + 1) * 128],
                            W2QT[:, j * D + half * 512:j * D + (half + 1) * 512],
                            start=(j == 0), stop=(j == NH1 - 1)),
                            tick=("t" if (j == NH1 - 1 and half == 1) else None))
                if xr_m:
                    self.wait("vector", "out", self._y_park_last)
                if xr_m:
                    self.wait("sync", "v", xr_m)
                if lc == 0:
                    self.wait("sync", "v", c["v"].n)
                dm = self.dma("sync", ABSA[:], xpark[tk * 128:(tk + 1) * 128, :], "dS")
                self.wait("vector", "t", tmm)
                self.wait("vector", "dS", dm)
                if not hasattr(self, "_w2_ps_free"):
                    self._w2_ps_free = {0: 0, 1: 0}
                xr_m = self.vop(lambda e, pst=pst, cc8=lc: e.scalar_tensor_tensor(
                    out=HTOK[:], in0=pst[:, 0:D],
                    scalar=SM[:, 24 + cc8:25 + cc8], in1=ABSA[:],
                    op0=ALU.mult, op1=ALU.add))
                self._w2_ps_free[pi] = xr_m
                self._layernorm_X(lambda i: HTOK[:], [0], TMP2)
                vln = c["v"].n
                self.wait("sync", "v", vln)
                self._y_park_last = self.dma(
                    "sync", y_out[tk * 128:(tk + 1) * 128, :], HTOK[:], "out")



def _host_consts():
    cst = {}
    cst["identity"] = np.eye(128, dtype=np.float32)
    cst["onesf"] = np.ones((1, 128), dtype=np.float32)
    cst["onesb"] = np.ones((1, 512), dtype=BF)
    cst["triu"] = np.triu(np.ones((128, 128))).astype(BF)
    pos = np.arange(T, dtype=np.float32)[:, None]
    dim_pair = np.arange(0, HD, 2, dtype=np.float32)[None, :]
    theta = np.float32(10000.0) ** (-dim_pair / np.float32(HD))
    freqs = (pos * theta).astype(np.float32)
    rope = np.concatenate([np.cos(freqs), np.sin(freqs)], axis=-1).astype(np.float32)
    cst["ropeC"] = np.ascontiguousarray(np.tile(rope[:, 0::2], (1, NH))).astype(np.float32)
    cst["ropeS"] = np.ascontiguousarray(np.tile(rope[:, 1::2], (1, NH))).astype(np.float32)
    return cst


def _prep_weights(inp):
    w = {}
    w["L0_WT"] = np.ascontiguousarray(np.asarray(inp["L0_attn_W"], np.float32).T)
    b = np.asarray(inp["L0_attn_b"], np.float32)
    w["L0_bcol"] = np.ascontiguousarray(b[:2048].reshape(16, 128).T)
    w["L0_bvrow"] = np.ascontiguousarray(b[2048:].reshape(1, D))
    w["L0_wg"] = np.asarray(inp["L0_ffn_w_gate"], np.float32)
    w["L0_wu"] = np.asarray(inp["L0_ffn_w_up"], np.float32)
    w["L0_wd"] = np.asarray(inp["L0_ffn_w_down"], np.float32)
    w["L0_wdT"] = np.ascontiguousarray(w["L0_wd"].T)
    w["L0_clipx"] = np.asarray(inp["L0_ffn_clip_x"], np.float32).reshape(1, 1)
    w["L0_clipd"] = np.asarray(inp["L0_ffn_clip_down"], np.float32).reshape(1, 1)
    pi = np.concatenate([np.arange(0, HD, 2), np.arange(1, HD, 2)])
    rowperm = np.concatenate([h * HD + pi for h in range(NH)])
    for nm, wn, bn in (("q", "L1_attn_Wq", "L1_attn_bq"), ("k", "L1_attn_Wk", "L1_attn_bk"),
                       ("v", "L1_attn_Wv", "L1_attn_bv")):
        W = np.asarray(inp[wn], np.float32)[rowperm, :]
        w[f"L1_W{nm}T"] = np.ascontiguousarray(W.T)
        w[f"L1_b{nm}row"] = np.asarray(inp[bn], np.float32)[rowperm].reshape(1, D)
    Wo = np.asarray(inp["L1_attn_Wo"], np.float32)
    w["L1_WoT"] = np.ascontiguousarray(Wo.T)
    w["L1_borow"] = np.asarray(inp["L1_attn_bo"], np.float32).reshape(1, D)
    w["L1_w1T"] = np.ascontiguousarray(np.asarray(inp["L1_ffn_w1"], np.float32).T)
    w["L1_w2T"] = np.ascontiguousarray(np.asarray(inp["L1_ffn_w2"], np.float32).T)
    cw = np.asarray(inp["L1_ffn_conv_w"], np.float32).reshape(H1, 3)
    w["L1_cw"] = np.ascontiguousarray(cw.reshape(NH1, 128, 3).transpose(1, 0, 2).reshape(128, NH1 * 3))
    w["L1_cb"] = np.ascontiguousarray(np.asarray(inp["L1_ffn_conv_b"], np.float32).reshape(NH1, 128).T)
    return w


def _build_in_maps(inputs):
    x = np.asarray(inputs["L0_attn_x"], np.float32)
    shared = {**_host_consts(), **_prep_weights(inputs)}
    return [dict(shared, x=np.ascontiguousarray(x[core])) for core in range(8)]


def kernel(**inputs):
    in_maps = _build_in_maps(inputs)
    nc = MK(tap=None).build()
    res = run_bass_kernel_spmd(nc, in_maps, core_ids=list(range(8)))
    return np.stack([np.asarray(r["y"], dtype=np.float32) for r in res.results], axis=0)


def run_debug(tap, inputs):
    in_maps = _build_in_maps(inputs)
    nc = MK(tap=tap).build()
    res = run_bass_kernel_spmd(nc, in_maps, core_ids=list(range(8)))
    return res.results
